# version 18
# baseline (speedup 1.0000x reference)
from contextlib import ExitStack

import numpy as np
import ml_dtypes

import concourse.bass as bass
import concourse.mybir as mybir
from concourse.bass_utils import run_bass_kernel_spmd

F32 = mybir.dt.float32
BF16 = mybir.dt.bfloat16
F16 = mybir.dt.float16
I32 = mybir.dt.int32
AF = mybir.ActivationFunctionType
ALU = mybir.AluOpType

D = 2048
NH = 16
HD = 128
SEQ = 2048
BATCH = 16
NCORES = 8
SPC = 2
TOK = SPC * SEQ
QSCALE = float(HD) ** -0.5
SHIFT = 8.0

FI = 5632
GU = 2 * FI
GW = 256
DWRD = 704
NIT = 44
NOB = 22
NGU = 2 * NOB
TTOK = 512
NTT = TOK // TTOK

LN_EPS = 1e-5
RMS_EPS = 1e-6

DN_TILES = []
for _t in range(NIT):
    _a, _j = (128 * _t) % DWRD, (128 * _t) // DWRD
    DN_TILES.append((_a, _j, (128 * _t + 127) // DWRD != _j))


def _qg_plan(g):
    if g <= 1:
        nblk = 16
        keys = [128 * c for c in range(nblk)]
    else:
        kfg = 4 * (g - 1)
        nblk = 16 - kfg + 1
        keys = [0] + [128 * (kfg + c - 1) for c in range(1, nblk)]
    qts = []
    for l in range(4):
        masks, zeros = [], []
        if g == 0:
            accum = (0, nblk)
            runs = [(0, nblk)]
        elif g == 1:
            if l == 0:
                masks = [(0, "trisink")]
                accum = (1, nblk)
                runs = [(0, nblk)]
            else:
                masks = [(0, "sink"), (l, "tri")]
                zeros = list(range(1, l))
                accum = (l + 1, nblk)
                runs = [(0, 1), (l, nblk)]
        else:
            masks = [(0, "sink"), (l + 1, "tri")]
            zeros = list(range(1, l + 1))
            accum = (l + 2, nblk)
            runs = [(0, 1), (l + 1, nblk)]
        mms = []
        for (c0, c1) in runs:
            c = c0
            while c < c1:
                ce = min(c1, (c // 4 + 1) * 4)
                mms.append((c, ce))
                c = ce
        qts.append(dict(masks=masks, zeros=zeros, accum=accum, mms=mms))
    return nblk, keys, qts


QG_PLANS = [_qg_plan(g) for g in range(4)]
NGRP = [(QG_PLANS[g][0] + 3) // 4 for g in range(4)]


def _chunk_maps(g):
    nblk, keys, qts = QG_PLANS[g]
    zmap = [[] for _ in range(nblk)]
    mmap = [[] for _ in range(nblk)]
    for l in range(4):
        for c in qts[l]["zeros"]:
            zmap[c].append(l)
        for (c, w) in qts[l]["masks"]:
            mmap[c].append((l, w))
    return zmap, mmap


CHUNK_MAPS = [_chunk_maps(g) for g in range(4)]


def _lastgrp(qi):
    g = (qi // 64) % 4
    return NGRP[g] - 1


def build_program(M, dbg=(), upto=3):
    M2 = {}

    def at(tag):
        return M.get(tag, 0)

    nc = bass.Bass("TRN2", target_bir_lowering=False, debug=False)

    def din(name, shape, dtype):
        return nc.dram_tensor(name, shape, dtype, kind="ExternalInput")

    def scratch(name, shape, dtype):
        kind = "ExternalOutput" if name in dbg else "Internal"
        return nc.dram_tensor(name, shape, dtype, kind=kind)

    x_in = din("x", [TOK, D], F32)
    w_t = {w: din(f"w{w}t", [D, D], F32) for w in "qkvo"}
    b_in = {w: din(f"b{w}", [128, 16], F32) for w in "qkvo"}
    gut = din("gut", [GW, GU], I32)
    sgu = din("sgu", [2, NOB, 128, 512], F16)
    dnt = din("dnt", [DWRD, D], I32)
    dnsb = din("dnsb", [NIT, 128, 2048], F16)
    idm = din("idm", [128, 128], F32)
    m_tri = din("mT_tri", [128, 128], F32)
    m_trisink = din("mT_trisink", [128, 128], F32)
    m_sink = din("mT_sink", [128, 128], F32)
    ones1 = din("ones1", [1, 128], F32)
    onesd = din("onesd", [128, 1], F32)
    negsh = din("negsh", [128, 1], F32)
    eps1 = din("eps1", [128, 1], F32)
    eps2 = din("eps2", [128, 1], F32)

    y_out = nc.dram_tensor("y", [TOK, D], F32, kind="ExternalOutput")

    s_wbf = {w: scratch(f"s_w{w}", [D, D], BF16) for w in "qkvo"}
    s_xfm = scratch("s_xfm", [SPC, D, SEQ], BF16)
    s_q = scratch("s_q", [SPC, D, SEQ], BF16)
    s_k = scratch("s_k", [SPC, D, SEQ], BF16)
    s_v = scratch("s_v", [SPC, NH, 128, SEQ], BF16)
    s_wgu = scratch("s_wgu", [2, NOB, 8, 128, 512], BF16)
    s_wdn = scratch("s_wdn", [NIT, 128, 2048], BF16)
    s_x1t = scratch("s_x1t", [TOK, D], F32)
    s_x1f = scratch("s_x1f", [D, TOK], BF16)

    with ExitStack() as ctx:
        AR_BYTES = 204 * 1024
        arena_h = ctx.enter_context(
            nc.sbuf_tensor("arena", [128, AR_BYTES // 2], BF16))
        ARENA_ADDR = nc.lookup_mloc(arena_h).addr
        ESZ = {F32: 4, I32: 4, BF16: 2, F16: 2}

        class Cur:
            def __init__(self, base):
                self.o = base

            def take(self, nbytes, align=64):
                self.o = (self.o + align - 1) // align * align
                r = self.o
                self.o += nbytes
                return r

        _tn = [0]

        def tile(cur, rows, cols, dtype):
            esz = ESZ[dtype]
            off = cur.take(cols * esz)
            _tn[0] += 1
            h = nc.alloc_sbuf_tensor_at(f"tl{_tn[0]}", [rows, cols], dtype,
                                        offset=ARENA_ADDR + off)
            return h[:, :]

        def tile_at(byte_off, rows, cols, dtype):
            _tn[0] += 1
            h = nc.alloc_sbuf_tensor_at(f"tl{_tn[0]}", [rows, cols], dtype,
                                        offset=ARENA_ADDR + byte_off)
            return h[:, :]

        R0 = Cur(0)
        c_id32 = tile(R0, 128, 128, F32)
        c_id16 = tile(R0, 128, 128, BF16)
        c_tri = tile(R0, 128, 128, BF16)
        c_trisink = tile(R0, 128, 128, BF16)
        c_sink = tile(R0, 128, 128, BF16)
        c_ones1 = tile(R0, 1, 128, F32)
        c_onesd = tile(R0, 128, 1, BF16)
        c_neg8 = tile(R0, 128, 1, F32)
        c_eps1 = tile(R0, 128, 1, F32)
        c_eps2 = tile(R0, 128, 1, F32)
        c_bias = {w: tile(R0, 128, 16, F32) for w in "qkvo"}
        c_bqs = tile(R0, 128, 16, F32)
        st_bn = tile(R0, 128, 24, F32)
        st_ag = tile(R0, 128, 2, F32)
        st_var = tile(R0, 128, 1, F32)
        st_rstd = tile(R0, 128, 1, F32)
        st_nmr = tile(R0, 128, 1, F32)
        st_sm = tile(R0, 128, 1, F32)
        st_ssq = tile(R0, 128, 1, F32)
        st_m2 = tile(R0, 128, 1, F32)
        st_rstdn = tile(R0, 128, 1, F32)
        p2_dn = tile(R0, 128, 4, F32)
        p2_kp = tile(R0, 128, 2, F32)
        p2_rd = tile(R0, 128, 1, F32)
        p2_rdr = tile(R0, 1, 512, F32)
        p2_bc = tile(R0, 128, 512, F32)
        assert R0.o <= 15 * 1024, R0.o

        RD_BASE = 15 * 1024
        RD = Cur(RD_BASE)
        dq_pk = [tile(RD, 128, 2048, I32) for _ in range(2)]
        dq_sc = [tile(RD, 128, 512, F32) for _ in range(2)]
        dq_out = [tile(RD, 128, 4096, BF16) for _ in range(2)]
        dq_mi = tile(RD, 128, 512, I32)
        dq_si = tile(RD, 128, 512, I32)
        dq_m = tile(RD, 128, 512, F32)
        dq_s = tile(RD, 128, 512, F32)
        dq_r4 = tile(RD, 128, 512, F32)
        dq_r6 = tile(RD, 128, 512, F32)
        assert RD.o <= 64 * 1024, RD.o
        RDP3 = Cur(RD_BASE)
        p3_wd = [tile(RDP3, 128, 22 * 256, BF16) for _ in range(2)]
        assert RDP3.o <= 64 * 1024, RDP3.o

        RP_BASE = 64 * 1024
        RP = Cur(RP_BASE)
        p1_wog = [tile(RP, 128, 16 * 512, BF16) for _ in range(2)]
        WOG_END = RP.o
        p1_x32 = [tile(RP, 128, 2048, F32) for _ in range(2)]
        p1_xb = [tile(RP, 128, 2048, BF16) for _ in range(2)]
        STX_OFF = (RP.o + 63) // 64 * 64
        p1_stx = tile(RP, 128, 4 * 2048, BF16)
        XQ_OFF = (RP.o + 63) // 64 * 64
        p1_xq = [tile(RP, 128, 16 * 512, BF16) for _ in range(2)]
        STG_OFF = (RP.o + 63) // 64 * 64
        p1_stg = tile(RP, 128, 4 * 2048, BF16)
        VFM_OFF = (RP.o + 63) // 64 * 64
        p1_vfm = tile(RP, 128, 4 * 2048, BF16)
        STV_OFF = (RP.o + 63) // 64 * 64
        p1_stv = tile(RP, 128, 2048, BF16)
        assert RP.o <= AR_BYTES, RP.o
        wb16 = [tile_at(VFM_OFF, 128, 2048, BF16),
                tile_at(VFM_OFF + 4096, 128, 2048, BF16)]

        RP2 = Cur(WOG_END)
        p2_q = [tile(RP2, 128, 512, BF16) for _ in range(2)]
        p2_k = [tile(RP2, 128, 2048, BF16) for _ in range(2)]
        p2_v = [tile(RP2, 128, 2048, BF16) for _ in range(2)]
        p2_pr = [tile(RP2, 128, 2048, BF16) for _ in range(2)]
        p2_pt = tile(RP2, 128, 16 * 512, BF16)
        p2_afm = tile(RP2, 128, 16 * 512, BF16)
        p2_wof = tile(RP2, 128, 16 * 512, BF16)
        p2_x32 = [tile(RP2, 128, 2048, F32) for _ in range(2)]
        p2_r1 = tile(RP2, 128, 2048, F32)
        p2_x1b = tile(RP2, 128, 2048, BF16)
        p2_stx1 = tile(RP2, 128, 2048, BF16)
        assert RP2.o <= AR_BYTES, RP2.o

        RP3 = Cur(RP_BASE)
        p3_x1j = tile(RP3, 128, 16 * 512, BF16)
        p3_wg = [tile(RP3, 128, 16 * 256, BF16) for _ in range(2)]
        p3_wu = [tile(RP3, 128, 16 * 256, BF16) for _ in range(2)]
        p3_sg = tile(RP3, 128, 512, BF16)
        p3_h = tile(RP3, 128, NIT * 512, BF16)
        p3_yf = tile(RP3, 128, 16 * 512, BF16)
        p3_yt = tile(RP3, 128, 2048, F32)
        p3_x1r = tile(RP3, 128, 2048, F32)
        p3_out = tile(RP3, 128, 2048, F32)
        assert RP3.o <= AR_BYTES, RP3.o

        psa = lambda n, s: ctx.enter_context(nc.psum_tensor(n, s, F32))
        ps_big = psa("ps_big", [128, 2048])
        ps_c = psa("ps_c", [128, 512])
        ps_d = psa("ps_d", [128, 512])
        ps_e = psa("ps_e", [128, 512])
        ps_f = psa("ps_f", [128, 512])
        QUAR = lambda i: ps_big[:, 512 * i:512 * i + 512]
        SETB = [ps_c, ps_d, ps_e, ps_f]

        def mksem(name):
            class Ct:
                __slots__ = ("h", "n", "name")
            c = Ct()
            c.h = ctx.enter_context(nc.semaphore(name))
            c.n = 0
            c.name = name
            return c

        def mark(c, instr, tag=None, k=1):
            instr.then_inc(c.h, k)
            c.n += k
            if tag is not None:
                M2[tag] = c.n
            return c.n

        s_pret = mksem("s_pret")
        s_prec = mksem("s_prec")
        s_c1 = mksem("s_c1")
        s_t1 = mksem("s_t1")
        s_t2 = mksem("s_t2")
        s_v2 = mksem("s_v2")
        s_c2 = mksem("s_c2")
        s_t3 = mksem("s_t3")
        s_v3 = mksem("s_v3")
        s_c3 = mksem("s_c3")
        s_dq = mksem("s_dq")

        d_cst = mksem("d_cst")
        d_gusb = [mksem(f"d_gusb{i}") for i in range(2)]
        d_indb = [mksem(f"d_indb{i}") for i in range(2)]
        d_pres = [mksem(f"d_pres{i}") for i in range(4)]
        d_presd = [mksem(f"d_presd{i}") for i in range(2)]
        d_p0bld = [mksem(f"d_p0bld{i}") for i in range(2)]
        d_p0bst = [mksem(f"d_p0bst{i}") for i in range(2)]
        d_p1ax = [mksem(f"d_p1ax{i}") for i in range(2)]
        d_p1ast = mksem("d_p1ast")
        d_p1w = [mksem(f"d_p1w{i}") for i in range(2)]
        d_p1xq = [mksem(f"d_p1xq{i}") for i in range(2)]
        d_p1st = mksem("d_p1st")
        d_p1stv = mksem("d_p1stv")
        d_p2k = [mksem(f"d_p2k{i}") for i in range(2)]
        d_p2v = [mksem(f"d_p2v{i}") for i in range(2)]
        d_p2q = [mksem(f"d_p2q{i}") for i in range(2)]
        d_p2wo = [mksem(f"d_p2wo{i}") for i in range(2)]
        d_p2x = [mksem(f"d_p2x{i}") for i in range(2)]
        d_p2s1t = mksem("d_p2s1t")
        d_p2s1f = mksem("d_p2s1f")
        d_p3x1j = mksem("d_p3x1j")
        d_p3wg = [mksem(f"d_p3wg{i}") for i in range(2)]
        d_p3wu = [mksem(f"d_p3wu{i}") for i in range(2)]
        d_p3wd = [mksem(f"d_p3wd{i}") for i in range(2)]
        d_p3x1r = mksem("d_p3x1r")
        d_out = mksem("d_out")
        d_poolc = mksem("d_poolc")
        d_pk = [mksem(f"d_pk{i}") for i in range(2)]
        d_psc = [mksem(f"d_psc{i}") for i in range(2)]
        d_gust = [mksem(f"d_gust{i}") for i in range(2)]
        d_dnst = [mksem(f"d_dnst{i}") for i in range(2)]

        def droute(tag):
            p = tag.split(":")
            f = p[0]
            if f == "cst":
                return d_cst
            if f == "gusb":
                return d_gusb[int(p[1]) % 2]
            if f == "indb":
                return d_indb[int(p[1]) % 2]
            if f == "pres":
                k = p[1]
                if k.startswith("d"):
                    return d_presd[int(k[1:]) % 2]
                return d_pres[int(k) % 4]
            if f == "p0b_ld":
                return d_p0bld[int(p[1]) % 2]
            if f == "p0b_st":
                return d_p0bst[int(p[1]) % 2]
            if f == "p1a_x":
                return d_p1ax[int(p[1]) % 2]
            if f == "p1a_st":
                return d_p1ast
            if f == "p1_w":
                return d_p1w[int(p[3]) % 2]
            if f == "p1_xq":
                return d_p1xq[int(p[4]) % 2]
            if f == "p1_st":
                return d_p1st
            if f == "p1_stv":
                return d_p1stv
            if f == "p2_k":
                return d_p2k[int(p[1]) % 2]
            if f == "p2_v":
                return d_p2v[int(p[1]) % 2]
            if f == "p2_q":
                return d_p2q[int(p[1]) % 2]
            if f == "p2_wo":
                return d_p2wo[int(p[2]) % 2]
            if f == "p2_x":
                return d_p2x[int(p[2]) % 2]
            if f == "p2_stx1t":
                return d_p2s1t
            if f == "p2_stx1f":
                return d_p2s1f
            if f == "p3_x1j":
                return d_p3x1j
            if f == "p3_wg":
                return d_p3wg[int(p[2]) % 2]
            if f == "p3_wu":
                return d_p3wu[int(p[2]) % 2]
            if f == "p3_wd":
                return d_p3wd[int(p[2]) % 2]
            if f == "p3_x1r":
                return d_p3x1r
            if f == "out":
                return d_out
            if f == "poolcst":
                return d_poolc
            if f in ("p0_pk", "p0_pkd"):
                return d_pk[int(p[1]) % 2]
            if f == "p0_sc":
                return d_psc[int(p[1]) % 2]
            if f == "p0_scd":
                return d_psc[int(p[2]) % 2]
            if f == "p0_gu":
                return d_gust[int(p[1]) % 2]
            if f == "p0_dnst":
                return d_dnst[int(p[2]) % 2]
            raise KeyError(tag)

        SPECIALS = {
            "pres:all": [(d_pres[i], f"pres:f{i}") for i in range(4)],
            "p0b:done": [(d_p0bst[0], "p0bst:f0"), (d_p0bst[1], "p0bst:f1")],
            "p1:done": [(d_p1st, "p1st:f"), (d_p1stv, "p1stv:f"),
                        (d_p1ast, "p1ast:f"),
                        (d_p0bst[0], "p0bst:f0"), (d_p0bst[1], "p0bst:f1")],
            "p2:done": [(d_p2s1t, "p2s1t:f"), (d_p2s1f, "p2s1f:f")],
            "p0gu:done": [(d_gust[0], "gust:f0"), (d_gust[1], "gust:f1")],
            "p0:done": [(d_gust[0], "gust:f0"), (d_gust[1], "gust:f1"),
                        (d_dnst[0], "dnst:f0"), (d_dnst[1], "dnst:f1")],
        }

        def dmark(instr, tag):
            return mark(droute(tag), instr, tag, 16)

        def dwait(e, tag):
            if tag in SPECIALS:
                for c, t in SPECIALS[tag]:
                    e.wait_ge(c.h, at(t))
            else:
                e.wait_ge(droute(tag).h, at(tag))

        blk = ctx.enter_context(nc.Block())

        @blk.sync
        def _(e):
            for dst_t, src_t in ((c_id32, idm), (c_ones1, ones1),
                                 (c_neg8, negsh),
                                 (c_eps1, eps1), (c_eps2, eps2)):
                dmark(e.dma_start(out=dst_t, in_=src_t[:]), "cst")
            for w in "qkvo":
                dmark(e.dma_start(out=c_bias[w], in_=b_in[w][:]), "cst")
            M2["cst"] = d_cst.n



            for s in range(SPC):
                for tt in range(16):
                    n = s * 16 + tt
                    if n >= 2:
                        e.wait_ge(s_c1.h, at(f"p1a_cast:{n - 2}"))
                    dmark(e.dma_start(out=p1_x32[n % 2],
                                      in_=x_in[2048 * s + 128 * tt:
                                               2048 * s + 128 * tt + 128, :]),
                          f"p1a_x:{n}")
                    if tt % 4 == 3:
                        tq = tt // 4
                        e.wait_ge(s_c1.h, at(f"p1a_evac:{n}"))
                        for dc in range(16):
                            src = p1_stx.rearrange("p (k c t) -> p k c t",
                                                   k=4, c=16)[:, :, dc, :]
                            dmark(e.dma_start(
                                out=s_xfm[s, 128 * dc:128 * dc + 128,
                                          512 * tq:512 * tq + 512], in_=src),
                                f"p1a_st:{s}:{tq}:{dc}")
                for wi, w in enumerate("qkv"):
                    dwait(e, f"p0b_st:{16 * wi + 15}")
                    dwait(e, f"p0b_st:{16 * wi + 14}")
                    for og in range(4):
                        par = og % 2
                        u0 = (s * 3 + wi) * 4 + og
                        if u0 >= 2:
                            pu = u0 - 2
                            ps_, r = divmod(pu, 12)
                            pw, po = divmod(r, 4)
                            e.wait_ge(s_t1.h, at(f"p1_mm:{ps_}:{pw}:{po}:3"))
                        for dc in range(16):
                            dmark(e.dma_start(
                                out=p1_wog[par][:, 512 * dc:512 * dc + 512],
                                in_=s_wbf[w][128 * dc:128 * dc + 128,
                                             512 * og:512 * og + 512]),
                                f"p1_w:{s}:{wi}:{og}:{dc}")
            if upto < 2:
                dwait(e, "p1:done")
                dwait(e, "p0b:done")
                dwait(e, "p0:done")
                return

            dwait(e, "p1:done")
            for s in range(SPC):
                for g in range(4):
                    qgi = s * 4 + g
                    for h in range(NH):
                        par = h % 2
                        hi = qgi * NH + h
                        if hi >= 2:
                            e.wait_ge(s_t2.h, at(f"p2_pv:{hi - 2}"))
                        dmark(e.dma_start(out=p2_k[par],
                                          in_=s_k[s, 128 * h:128 * h + 128, :]),
                              f"p2_k:{hi}")
                        dmark(e.dma_start(out=p2_v[par], in_=s_v[s, h]),
                              f"p2_v:{hi}")
                        dmark(e.dma_start(out=p2_q[par],
                                          in_=s_q[s, 128 * h:128 * h + 128,
                                                  512 * g:512 * g + 512]),
                              f"p2_q:{hi}")
                    for og in range(4):
                        par = og % 2
                        if qgi == 0 and og < 2:
                            e.wait_ge(s_t1.h, at("p1_mm:1:2:3:3"))
                        if og >= 2:
                            e.wait_ge(s_t2.h, at(f"p2_womm:{qgi}:{og - 2}"))
                        elif qgi >= 1:
                            e.wait_ge(s_t2.h, at(f"p2_womm:{qgi - 1}:{og + 2}"))
                        for dc in range(16):
                            dmark(e.dma_start(
                                out=p1_wog[par][:, 512 * dc:512 * dc + 512],
                                in_=s_wbf["o"][128 * dc:128 * dc + 128,
                                               512 * og:512 * og + 512]),
                                f"p2_wo:{qgi}:{og}:{dc}")
                    def emit_x1store(pn):
                        pt0 = (2048 * (pn // 16) + 512 * ((pn // 4) % 4)
                               + 128 * (pn % 4))
                        e.wait_ge(s_c2.h, at(f"p2_x1tt:{pn}"))
                        dmark(e.dma_start(out=s_x1t[pt0:pt0 + 128, :],
                                          in_=p2_r1),
                              f"p2_stx1t:{pn}")
                        e.wait_ge(s_c2.h, at(f"p2_x1f:{pn}"))
                        dst = s_x1f.rearrange("(c p) t -> p c t", p=128)[
                            :, :, pt0:pt0 + 128]
                        dmark(e.dma_start(out=dst,
                                          in_=p2_stx1.rearrange(
                                              "p (c t) -> p c t", c=16)),
                              f"p2_stx1f:{pn}")

                    for tc in range(4):
                        n = qgi * 4 + tc
                        t0 = 2048 * s + 512 * g + 128 * tc
                        if n >= 1:
                            emit_x1store(n - 1)
                        if n >= 2:
                            e.wait_ge(s_v2.h, at(f"p2_r1:{n - 2}"))
                        dmark(e.dma_start(out=p2_x32[tc % 2],
                                          in_=x_in[t0:t0 + 128, :]),
                              f"p2_x:{qgi}:{tc}")
            emit_x1store(SPC * 16 - 1)
            M2["p2s1t:f"] = d_p2s1t.n
            M2["p2s1f:f"] = d_p2s1f.n
            if upto < 3:
                dwait(e, "p2:done")
                dwait(e, "p1:done")
                dwait(e, "p0:done")
                return

            dwait(e, "p2:done")
            for tt in range(NTT):
                if tt >= 1:
                    e.wait_ge(s_t3.h, at(f"p3_gudone:{tt - 1}"))
                for cc in range(2):
                    for j in range(8):
                        k = 8 * cc + j
                        src = s_x1f.rearrange("(w r) t -> w r t", r=8)[
                            128 * cc:128 * cc + 128, j,
                            TTOK * tt:TTOK * tt + TTOK]
                        dmark(e.dma_start(out=p3_x1j[:, 512 * k:512 * k + 512],
                                          in_=src), f"p3_x1j:{tt}:{k}")
                for usb in range(22):
                    gp, sbi = divmod(usb, 2)
                    par = usb % 2
                    if usb >= 2:
                        e.wait_ge(s_t3.h, at(f"p3_gumm:{tt}:{usb - 2}"))
                    elif tt >= 1:
                        e.wait_ge(s_t3.h, at(f"p3_gumm:{tt - 1}:{usb + 20}"))
                    dwait(e, "p0gu:done")
                    for cc in range(2):
                        srcg = s_wgu[cc, gp].rearrange("j p c -> p j c")[
                            :, :, 256 * sbi:256 * sbi + 256]
                        srcu = s_wgu[cc, 11 + gp].rearrange("j p c -> p j c")[
                            :, :, 256 * sbi:256 * sbi + 256]
                        dmark(e.dma_start(
                            out=p3_wg[par].rearrange("p (k c) -> p k c", c=256)[
                                :, 8 * cc:8 * cc + 8, :], in_=srcg),
                            f"p3_wg:{tt}:{usb}:{cc}")
                        dmark(e.dma_start(
                            out=p3_wu[par].rearrange("p (k c) -> p k c", c=256)[
                                :, 8 * cc:8 * cc + 8, :], in_=srcu),
                            f"p3_wu:{tt}:{usb}:{cc}")
                for ds in range(16):
                    ob, r = divmod(ds, 4)
                    ocp, half = divmod(r, 2)
                    par = ds % 2
                    if ds >= 2:
                        e.wait_ge(s_t3.h, at(f"p3_dmm:{tt}:{ds - 2}"))
                    elif tt >= 1:
                        e.wait_ge(s_t3.h, at(f"p3_dmm:{tt - 1}:{ds + 14}"))
                    dwait(e, "p0:done")
                    col0 = 512 * ob + 256 * ocp
                    for q in range(2):
                        src = s_wdn[22 * half + 11 * q:22 * half + 11 * q + 11,
                                    :, col0:col0 + 256].rearrange(
                                        "t p c -> p t c")
                        dmark(e.dma_start(
                            out=p3_wd[par].rearrange("p (i c) -> p i c", c=256)[
                                :, 11 * q:11 * q + 11, :], in_=src),
                            f"p3_wd:{tt}:{ds}:{q}")
                def emit_ostore(pn):
                    pt0 = TTOK * (pn // 4) + 128 * (pn % 4)
                    e.wait_ge(s_c3.h, at(f"p3_out:{pn}"))
                    dmark(e.dma_start(out=y_out[pt0:pt0 + 128, :], in_=p3_out),
                          f"out:{pn}")

                for tc in range(4):
                    n = tt * 4 + tc
                    t0 = TTOK * tt + 128 * tc
                    if n >= 1:
                        emit_ostore(n - 1)
                        e.wait_ge(s_v3.h, at(f"p3_r2:{n - 1}"))
                    dmark(e.dma_start(out=p3_x1r, in_=s_x1t[t0:t0 + 128, :]),
                          f"p3_x1r:{n}")
            emit_ostore(NTT * 4 - 1)
            e.wait_ge(d_out.h, d_out.n)

        @blk.gpsimd
        def _(e):
            e.dma_start(out=c_id16, in_=idm[:]).then_inc(d_poolc.h, 16)
            e.dma_start(out=c_tri, in_=m_tri[:]).then_inc(d_poolc.h, 16)
            e.dma_start(out=c_trisink, in_=m_trisink[:]).then_inc(d_poolc.h, 16)
            e.dma_start(out=c_onesd, in_=onesd[:]).then_inc(d_poolc.h, 16)
            d_poolc.n = 64
            dmark(e.dma_start(out=c_sink, in_=m_sink[:]), "poolcst")
            M2["poolcst"] = d_poolc.n

            def p0b_tile(n):
                wi_, dt = divmod(n, 16)
                w = "qkvo"[wi_]
                if n >= 2:
                    dwait(e, f"p0b_st:{n - 2}")
                dmark(e.dma_start(out=wb16[n % 2],
                                  in_=w_t[w][128 * dt:128 * dt + 128, :]),
                      f"p0b_ld:{n}")
                dwait(e, f"p0b_ld:{n}")
                dmark(e.dma_start(out=s_wbf[w][128 * dt:128 * dt + 128, :],
                                  in_=wb16[n % 2]), f"p0b_st:{n}")

            for n in range(48):
                p0b_tile(n)

            for n in range(NGU):
                cc, ob = divmod(n, NOB)
                par = n % 2
                for _w in range(4):
                    if 48 + 4 * n + _w < 64:
                        p0b_tile(48 + 4 * n + _w)
                if n >= 2:
                    e.wait_ge(s_dq.h, at(f"dqv:{n - 2}"))
                dmark(e.dma_start(out=dq_pk[par][:, 0:512],
                                  in_=gut[128 * cc:128 * cc + 128,
                                          512 * ob:512 * ob + 512]),
                      f"p0_pk:{n}")
                dmark(e.dma_start(out=dq_sc[par], in_=sgu[cc, ob]),
                      f"p0_sc:{n}")
                e.wait_ge(s_dq.h, at(f"dqv:{n}"))
                dmark(e.dma_start(
                    out=s_wgu[cc, ob].rearrange("j p c -> p j c"),
                    in_=dq_out[par].rearrange("p (j c) -> p j c", j=8)),
                    f"p0_gu:{n}")
            M2["p0bst:f0"] = d_p0bst[0].n
            M2["p0bst:f1"] = d_p0bst[1].n
            for t in range(NIT):
                a, j, mixed = DN_TILES[t]
                par = t % 2
                if t >= 2:
                    e.wait_ge(s_dq.h, at(f"dqv:d{t - 2}:3"))
                else:
                    e.wait_ge(s_dq.h, at(f"dqv:{NGU - 2 + t}"))
                if mixed:
                    dmark(e.dma_start(out=dq_pk[par][0:64, :],
                                      in_=dnt[a:a + 64, :]), f"p0_pkd:{t}")
                    dmark(e.dma_start(out=dq_pk[par][64:128, :],
                                      in_=dnt[0:64, :]), f"p0_pkd:{t}")
                else:
                    dmark(e.dma_start(out=dq_pk[par], in_=dnt[a:a + 128, :]),
                          f"p0_pkd:{t}")
                M2[f"p0_pkdone:{t}"] = d_pk[par].n
                for q in range(4):
                    qpar = q % 2
                    if 4 * t + q >= 2:
                        pt, pq = (t, q - 2) if q >= 2 else (t - 1, q + 2)
                        e.wait_ge(s_dq.h, at(f"dqv:d{pt}:{pq}"))
                    else:
                        e.wait_ge(s_dq.h, at(f"dqv:{NGU - 2 + qpar}"))
                    dmark(e.dma_start(
                        out=dq_sc[qpar],
                        in_=dnsb[t, :, 512 * q:512 * q + 512]),
                        f"p0_scd:{t}:{q}")
                    e.wait_ge(s_dq.h, at(f"dqv:d{t}:{q}"))
                    dmark(e.dma_start(
                        out=s_wdn[t, :, 512 * q:512 * q + 512],
                        in_=dq_out[par][:, 512 * qpar:512 * qpar + 512]),
                        f"p0_dnst:{t}:{q}")
            M2["gust:f0"] = d_gust[0].n
            M2["gust:f1"] = d_gust[1].n
            M2["dnst:f0"] = d_dnst[0].n
            M2["dnst:f1"] = d_dnst[1].n

        @blk.tensor
        def _(e):
            dwait(e, "cst")
            dwait(e, "poolcst")
            for s in range(SPC):
                for tt in range(16):
                    n = s * 16 + tt
                    e.wait_ge(s_c1.h, at(f"p1a_cast:{n}"))
                    if n >= 1:
                        e.wait_ge(s_c1.h, at(f"p1a_evac:{n - 1}"))
                    if s == 1 and tt == 0:
                        e.wait_ge(s_c1.h, at("p1_evacdone:0:2:3:3"))
                    for dc in range(16):
                        ins = e.matmul(ps_big[:, 128 * dc:128 * dc + 128],
                                       p1_xb[n % 2][:, 128 * dc:128 * dc + 128],
                                       c_id16, start=True, stop=True,
                                       skip_group_check=True)
                    mark(s_t1, ins, f"p1a_mm:{n}", 1)
                for wi, w in enumerate("qkv"):
                    for og in range(4):
                        par = og % 2
                        dwait(e, f"p1_w:{s}:{wi}:{og}:15")
                        for tq in range(4):
                            xpar = tq % 2
                            dwait(e, f"p1_xq:{s}:{wi}:{og}:{tq}:15")
                            u = ((s * 3 + wi) * 4 + og) * 4 + tq
                            if u >= 2:
                                pu = u - 2
                                pl = pu % 4
                                pog = (pu // 4) % 4
                                pw = (pu // 16) % 3
                                ps_ = pu // 48
                                e.wait_ge(s_c1.h,
                                          at(f"p1_evacdone:{ps_}:{pw}:{pog}:{pl}"))
                                if pw == 2 and pl == 3:
                                    e.wait_ge(s_c1.h,
                                              at(f"p1_vevdone:{ps_}:{pog}:3"))
                            elif s == 0 and wi == 0:
                                e.wait_ge(s_c1.h, at(f"p1a_evac:{15}"))
                            if w == "v" and tq == 1 and og >= 1:
                                e.wait_ge(s_c1.h, at(f"p1_vevdone:{s}:{og - 1}:3"))
                            for dc in range(16):
                                for ol in range(4):
                                    dst = (QUAR(ol) if tq % 2 == 0
                                           else SETB[ol][:])
                                    ins = e.matmul(
                                        dst,
                                        p1_wog[par][:, 512 * dc + 128 * ol:
                                                    512 * dc + 128 * ol + 128],
                                        p1_xq[xpar][:, 512 * dc:512 * dc + 512],
                                        start=(dc == 0), stop=(dc == 15))
                            mark(s_t1, ins, f"p1_mm:{s}:{wi}:{og}:{tq}", 1)
                        if w == "v":
                            for ol in range(4):
                                e.wait_ge(s_c1.h, at(f"p1_evac:{s}:{wi}:{og}:3:{ol}"))
                                if ol == 0:
                                    e.wait_ge(s_c1.h,
                                              at(f"p1_evac:{s}:{wi}:{og}:3:1"))
                                for kc in range(16):
                                    grp = kc // 4
                                    dstp = (ps_c, ps_d)[grp % 2]
                                    vsl = 4 * ol + grp
                                    if kc % 4 == 0 and vsl >= 2:
                                        e.wait_ge(s_c1.h,
                                                  at(f"p1_vev:{s}:{og}:"
                                                     f"{(vsl - 2) // 4}:"
                                                     f"{(vsl - 2) % 4}"))
                                    ins = e.matmul(
                                        dstp[:, 128 * (kc % 4):128 * (kc % 4) + 128],
                                        p1_vfm[:, 2048 * ol + 128 * kc:
                                               2048 * ol + 128 * kc + 128],
                                        c_id16, start=True, stop=True,
                                        skip_group_check=True)
                                    if kc % 4 == 3:
                                        mark(s_t1, ins, f"p1_vtr:{s}:{og}:{ol}:{grp}", 1)
                M2[f"p1_vtrdone:{s}"] = s_t1.n
            if upto < 2:
                return

            sc_bank = [None] * 4
            for s in range(SPC):
                for g in range(4):
                    nblk, keys, qts = QG_PLANS[g]
                    zmap, mmap = CHUNK_MAPS[g]
                    qgi = s * 4 + g
                    def emit_bc(phi):
                        e.wait_ge(s_v2.h, at(f"p2_rc:{phi}"))
                        ins = e.matmul(ps_f[:, 0:512], c_ones1, p2_rdr,
                                       start=True, stop=True)
                        mark(s_t2, ins, f"p2_bc:{phi}", 1)

                    for h in range(NH):
                        par = h % 2
                        hi = qgi * NH + h
                        dwait(e, f"p2_q:{hi}")
                        dwait(e, f"p2_k:{hi}")
                        if h >= 1:
                            emit_bc(hi - 1)
                        for c in range(nblk):
                            b = c % 4
                            if hi == 0 and c < 4:
                                e.wait_ge(s_c1.h, at("p1_evacdone:1:2:3:3"))
                            elif c % 4 == 0 and c >= 4:
                                e.wait_ge(s_c2.h, at(f"p2_ex:{hi}:{c - 1}"))
                            if h == 0 and c == 0 and qgi >= 1:
                                e.wait_ge(s_v2.h, at(f"p2_r1:{qgi * 4 - 1}"))
                            ins = e.matmul(
                                ps_big[:, 512 * b:512 * b + 512],
                                p2_k[par][:, keys[c]:keys[c] + 128],
                                p2_q[par][:, 0:512],
                                start=True, stop=True, skip_group_check=True)
                            mark(s_t2, ins, f"p2_sc:{hi}:{c}", 1)
                        if any(mmap):
                            e.wait_ge(s_v2.h, at(f"p2_mk:{hi}"))
                        if hi >= 1:
                            e.wait_ge(s_c2.h, at(f"p2_bcev:{hi - 1}"))
                        e.wait_ge(s_c2.h, at(f"p2_ex:{hi}:{max(nblk - 5, 0)}"))
                        for c in range(nblk):
                            if c == nblk - 4:
                                e.wait_ge(s_c2.h,
                                          at(f"p2_ex:{hi}:{nblk - 1}"))
                            ins = e.matmul(ps_f[0:1, 0:512], c_onesd,
                                           p2_pt[:, 512 * c:512 * c + 512],
                                           start=(c == 0),
                                           stop=(c == nblk - 1),
                                           skip_group_check=True)
                        mark(s_t2, ins, f"p2_den:{hi}", 1)
                        dwait(e, f"p2_v:{hi}")
                        if hi >= 1:
                            e.wait_ge(s_v2.h, at(f"p2_pvev:{hi - 1}"))
                        for c in range(nblk):
                            vc = c if g <= 1 else (0 if c == 0 else
                                                   4 * (g - 1) + c - 1)
                            ins = e.matmul(ps_e[:],
                                           p2_v[par][:, 128 * vc:128 * vc + 128],
                                           p2_pt[:, 512 * c:512 * c + 512],
                                           start=(c == 0), stop=(c == nblk - 1))
                        mark(s_t2, ins, f"p2_pv:{hi}", 1)
                    emit_bc(qgi * NH + NH - 1)
                    e.wait_ge(s_c2.h,
                              at(f"p2_ex:{qgi * NH + NH - 1}:{nblk - 1}"))
                    e.wait_ge(s_v2.h, at(f"p2_pvev:{qgi * NH + NH - 1}"))
                    if qgi >= 1:
                        e.wait_ge(s_v2.h, at(f"p2_r1:{qgi * 4 - 1}"))
                    for og in range(4):
                        par = og % 2
                        dwait(e, f"p2_wo:{qgi}:{og}:15")
                        if og >= 1:
                            e.wait_ge(s_c2.h, at(f"p2_woev:{qgi}:{og - 1}"))
                        for dc in range(16):
                            for ol in range(4):
                                ins = e.matmul(
                                    QUAR(ol),
                                    p1_wog[par][:, 512 * dc + 128 * ol:
                                                512 * dc + 128 * ol + 128],
                                    p2_afm[:, 512 * dc:512 * dc + 512],
                                    start=(dc == 0), stop=(dc == 15))
                        mark(s_t2, ins, f"p2_womm:{qgi}:{og}", 1)
                    for tc in range(4):
                        n = qgi * 4 + tc
                        e.wait_ge(s_c2.h, at(f"p2_woev:{qgi}:3"))
                        if tc >= 1:
                            e.wait_ge(s_v2.h, at(f"p2_r1:{n - 1}"))
                        for oc in range(16):
                            ins = e.matmul(ps_big[:, 128 * oc:128 * oc + 128],
                                           p2_wof[:, 512 * oc + 128 * tc:
                                                  512 * oc + 128 * tc + 128],
                                           c_id16, start=True, stop=True,
                                           skip_group_check=True)
                        mark(s_t2, ins, f"p2_wot:{n}", 1)
                        e.wait_ge(s_c2.h, at(f"p2_x1b:{n}"))
                        if n == 0:
                            e.wait_ge(s_c1.h, at("p1_vevdone:1:3:3"))
                        for dc in range(16):
                            grp = dc // 4
                            dstp = (ps_c, ps_d)[grp % 2]
                            if dc % 4 == 0 and grp >= 2:
                                e.wait_ge(s_c2.h, at(f"p2_x1ev:{n}:{grp - 2}"))
                            ins = e.matmul(
                                dstp[:, 128 * (dc % 4):128 * (dc % 4) + 128],
                                p2_x1b[:, 128 * dc:128 * dc + 128],
                                c_id16, start=True, stop=True,
                                skip_group_check=True)
                            if dc % 4 == 3:
                                mark(s_t2, ins, f"p2_x1tr:{n}:{grp}", 1)
            if upto < 3:
                return

            for tt in range(NTT):
                dwait(e, f"p3_x1j:{tt}:15")
                for usb in range(22):
                    par = usb % 2
                    dwait(e, f"p3_wg:{tt}:{usb}:1")
                    dwait(e, f"p3_wu:{tt}:{usb}:1")
                    for ol in range(2):
                        isl = 2 * usb + ol
                        gpar = isl % 2
                        gdst = ps_c[:] if gpar == 0 else ps_big[:, 0:512]
                        udst = ps_d[:] if gpar == 0 else ps_big[:, 512:1024]
                        if isl >= 2:
                            e.wait_ge(s_c3.h, at(f"p3_silu:{tt}:{isl - 2}"))
                            e.wait_ge(s_v3.h, at(f"p3_h:{tt}:{isl - 2}"))
                        elif tt >= 1:
                            e.wait_ge(s_c3.h, at(f"p3_ytevdone:{tt - 1}"))
                            e.wait_ge(s_c3.h, at(f"p3_silu:{tt - 1}:{isl + 42}"))
                            e.wait_ge(s_v3.h, at(f"p3_h:{tt - 1}:{isl + 42}"))
                        else:
                            e.wait_ge(s_c2.h, at("p2_lastx1ev"))
                        for kk in range(16):
                            cc, j = divmod(kk, 8)
                            ins = e.matmul(
                                gdst,
                                p3_wg[par][:, 2048 * cc + 256 * j + 128 * ol:
                                           2048 * cc + 256 * j + 128 * ol + 128],
                                p3_x1j[:, 512 * kk:512 * kk + 512],
                                start=(kk == 0), stop=(kk == 15))
                        mark(s_t3, ins, f"p3_gmm:{tt}:{isl}", 1)
                        for kk in range(16):
                            cc, j = divmod(kk, 8)
                            ins = e.matmul(
                                udst,
                                p3_wu[par][:, 2048 * cc + 256 * j + 128 * ol:
                                           2048 * cc + 256 * j + 128 * ol + 128],
                                p3_x1j[:, 512 * kk:512 * kk + 512],
                                start=(kk == 0), stop=(kk == 15))
                        mark(s_t3, ins, f"p3_umm:{tt}:{isl}", 1)
                    M2[f"p3_gumm:{tt}:{usb}"] = s_t3.n
                M2[f"p3_gudone:{tt}"] = s_t3.n
                for ds in range(16):
                    ob, r = divmod(ds, 4)
                    ocp, half = divmod(r, 2)
                    par = ds % 2
                    dwait(e, f"p3_wd:{tt}:{ds}:1")
                    if half == 0:
                        e.wait_ge(s_v3.h, at(f"p3_h:{tt}:43"))
                        pb = 4 * ob + 2 * ocp
                        if pb >= 2:
                            e.wait_ge(s_c3.h, at(f"p3_yev:{tt}:{pb - 1}"))
                        elif tt >= 1:
                            e.wait_ge(s_c3.h, at(f"p3_yev:{tt - 1}:{pb + 15}"))
                    for oc in range(2):
                        ydst = (ps_e, ps_f)[oc]
                        for it in range(22):
                            ins = e.matmul(
                                ydst[:],
                                p3_wd[par][:, 256 * it + 128 * oc:
                                           256 * it + 128 * oc + 128],
                                p3_h[:, 512 * (22 * half + it):
                                     512 * (22 * half + it) + 512],
                                start=(half == 0 and it == 0),
                                stop=(half == 1 and it == 21))
                        if half == 1:
                            mark(s_t3, ins,
                                 f"p3_ymm:{tt}:{4 * ob + 2 * ocp + oc}", 1)
                        elif oc == 1:
                            mark(s_t3, ins, f"p3_dmm:{tt}:{ds}", 1)
                    if half == 1:
                        M2[f"p3_dmm:{tt}:{ds}"] = s_t3.n
                for tc in range(4):
                    n = tt * 4 + tc
                    e.wait_ge(s_c3.h, at(f"p3_yev:{tt}:15"))
                    if tc >= 1:
                        e.wait_ge(s_c3.h, at(f"p3_ytev:{n - 1}:3"))
                    for oc in range(16):
                        grp = oc // 4
                        dstp = (ps_c, ps_d)[grp % 2]
                        if oc % 4 == 0 and grp >= 2:
                            e.wait_ge(s_c3.h, at(f"p3_ytev:{n}:{grp - 2}"))
                        ins = e.matmul(
                            dstp[:, 128 * (oc % 4):128 * (oc % 4) + 128],
                            p3_yf[:, 512 * oc + 128 * tc:512 * oc + 128 * tc + 128],
                            c_id16, start=True, stop=True,
                            skip_group_check=True)
                        if oc % 4 == 3:
                            mark(s_t3, ins, f"p3_yt:{n}:{grp}", 1)

        @blk.scalar
        def _(e):


            dwait(e, "cst")
            e.mul(c_bqs, c_bias["q"], QSCALE)
            e.drain()

            def emit_xq(s, wi, og, tq):
                xpar = tq % 2
                if wi == 0 and og == 0:
                    dwait(e, f"p1a_st:{s}:{tq}:15")
                for dc in range(16):
                    dmark(e.dma_start(
                        out=p1_xq[xpar][:, 512 * dc:512 * dc + 512],
                        in_=s_xfm[s, 128 * dc:128 * dc + 128,
                                  512 * tq:512 * tq + 512]),
                        f"p1_xq:{s}:{wi}:{og}:{tq}:{dc}")

            for s in range(SPC):
                for tt in range(16):
                    n = s * 16 + tt
                    dwait(e, f"p1a_x:{n}")
                    if n >= 2:
                        e.wait_ge(s_t1.h, at(f"p1a_mm:{n - 2}"))
                    mark(s_c1, e.activation(p1_xb[n % 2], p1_x32[n % 2],
                                            AF.Copy), f"p1a_cast:{n}", 1)
                    e.wait_ge(s_t1.h, at(f"p1a_mm:{n}"))
                    gq = n // 4
                    if gq >= 1:
                        dwait(e, f"p1a_st:{(gq - 1) // 4}:{(gq - 1) % 4}:15")
                    mark(s_c1, e.activation(
                        p1_stx[:, 2048 * (tt % 4):2048 * (tt % 4) + 2048],
                        ps_big[:], AF.Copy), f"p1a_evac:{n}", 1)
                emit_xq(s, 0, 0, 0)
                emit_xq(s, 0, 0, 1)
                for wi, w in enumerate("qkv"):
                    for og in range(4):
                        for tq in range(4):
                            e.wait_ge(s_t1.h, at(f"p1_mm:{s}:{wi}:{og}:{tq}"))
                            for ol in range(4):
                                src = (QUAR(ol) if tq % 2 == 0 else SETB[ol][:])
                                oc = 4 * og + ol
                                if tq == 0:
                                    if w == "v":
                                        if og >= 1:
                                            e.wait_ge(s_t1.h,
                                                      at(f"p1_vtr:{s}:{og - 1}:{ol}:3"))
                                        elif s == 1:
                                            e.wait_ge(s_t1.h,
                                                      at(f"p1_vtr:0:3:{ol}:3"))
                                    else:
                                        po = og - 1
                                        pwi, psx = wi, s
                                        if po < 0:
                                            pwi -= 1
                                            po = 3
                                        if pwi < 0:
                                            psx -= 1
                                            pwi = 1
                                        if psx >= 0:
                                            dwait(e, f"p1_st:{psx}:{pwi}:{4 * po + 3}")
                                dst = (p1_vfm if w == "v" else p1_stg)
                                if (w == "v" and s == 0 and og == 0
                                        and tq == 0 and ol == 0):
                                    dwait(e, "p0b:done")
                                if w == "q":
                                    ins = e.activation(
                                        dst[:, 2048 * ol + 512 * tq:
                                            2048 * ol + 512 * tq + 512],
                                        src, AF.Identity,
                                        bias=c_bqs[:, oc:oc + 1], scale=QSCALE)
                                else:
                                    ins = e.activation(
                                        dst[:, 2048 * ol + 512 * tq:
                                            2048 * ol + 512 * tq + 512],
                                        src, AF.Identity,
                                        bias=c_bias[w][:, oc:oc + 1], scale=1.0)
                                mark(s_c1, ins, f"p1_evac:{s}:{wi}:{og}:{tq}:{ol}", 1)
                            M2[f"p1_evacdone:{s}:{wi}:{og}:{tq}"] = s_c1.n
                            u2 = (wi * 4 + og) * 4 + tq + 2
                            if u2 < 48:
                                emit_xq(s, u2 // 16, (u2 // 4) % 4, u2 % 4)
                            if tq == 3 and w != "v":
                                e.drain()
                                for ol in range(4):
                                    oc = 4 * og + ol
                                    dstt = s_q if w == "q" else s_k
                                    dmark(e.dma_start(
                                        out=dstt[s, 128 * oc:128 * oc + 128, :],
                                        in_=p1_stg[:, 2048 * ol:
                                                   2048 * ol + 2048]),
                                        f"p1_st:{s}:{wi}:{oc}")
                        if w == "v":
                            for ol in range(4):
                                oc = 4 * og + ol
                                if oc >= 1:
                                    dwait(e, f"p1_stv:{s}:{oc - 1}")
                                elif s == 1:
                                    dwait(e, "p1_stv:0:15")
                                for grp in range(4):
                                    e.wait_ge(s_t1.h, at(f"p1_vtr:{s}:{og}:{ol}:{grp}"))
                                    dstp = (ps_c, ps_d)[grp % 2]
                                    mark(s_c1, e.activation(
                                        p1_stv[:, 512 * grp:512 * grp + 512],
                                        dstp[:], AF.Copy),
                                        f"p1_vev:{s}:{og}:{ol}:{grp}", 1)
                                M2[f"p1_vevdone:{s}:{og}:{ol}"] = s_c1.n
                                e.drain()
                                dmark(e.dma_start(out=s_v[s, oc], in_=p1_stv),
                                      f"p1_stv:{s}:{oc}")
            M2["p1st:f"] = d_p1st.n
            M2["p1stv:f"] = d_p1stv.n
            M2["p1ast:f"] = d_p1ast.n
            if upto < 2:
                return

            for s in range(SPC):
                for g in range(4):
                    nblk, keys, qts = QG_PLANS[g]
                    qgi = s * 4 + g
                    for h in range(NH):
                        hi = qgi * NH + h
                        for c in range(nblk):
                            e.wait_ge(s_t2.h, at(f"p2_sc:{hi}:{c}"))
                            b = c % 4
                            le = 4
                            if g == 1 and c >= 1:
                                le = min(c + 1, 4)
                            elif g >= 2 and c >= 1:
                                le = min(c, 4)
                            ins = e.activation(
                                p2_pt[:, 512 * c:512 * c + 128 * le],
                                ps_big[:, 512 * b:512 * b + 128 * le],
                                AF.Exp, bias=c_neg8, scale=1.0)
                            if le < 4:
                                ins = e.activation(
                                    p2_pt[:, 512 * c + 128 * le:512 * c + 512],
                                    ps_big[:, 512 * b + 128 * le:512 * b + 512],
                                    AF.Copy, scale=0.0)
                            mark(s_c2, ins, f"p2_ex:{hi}:{c}", 1)
                        e.wait_ge(s_t2.h, at(f"p2_bc:{hi}"))
                        if hi >= 1:
                            e.wait_ge(s_v2.h, at(f"p2_pvev:{hi - 1}"))
                        mark(s_c2, e.activation(p2_bc, ps_f[:, 0:512], AF.Copy),
                             f"p2_bcev:{hi}", 1)
                    if qgi >= 1:
                        e.wait_ge(s_t2.h, at(f"p2_wot:{qgi * 4 - 1}"))
                    for og in range(4):
                        e.wait_ge(s_t2.h, at(f"p2_womm:{qgi}:{og}"))
                        for ol in range(4):
                            oc = 4 * og + ol
                            ins = e.activation(
                                p2_wof[:, 512 * oc:512 * oc + 512],
                                QUAR(ol), AF.Identity,
                                bias=c_bias["o"][:, oc:oc + 1], scale=1.0)
                        mark(s_c2, ins, f"p2_woev:{qgi}:{og}", 1)
                    for tc in range(4):
                        n = qgi * 4 + tc
                        e.wait_ge(s_v2.h, at(f"p2_r1:{n}"))
                        e.activation(p2_x1b, p2_r1, AF.Square,
                                     accum_out=st_ssq)
                        e.activation(p2_r1, p2_r1, AF.Identity,
                                     accum_out=st_sm)
                        e.drain()
                        e.activation(st_m2, st_sm, AF.Square, scale=1.0 / D)
                        e.drain()
                        e.activation(st_m2, st_m2, AF.Identity, scale=-1.0,
                                     bias=c_eps1[:, 0:1])
                        e.drain()
                        e.activation(st_rstd, st_ssq, AF.Ln, bias=st_m2,
                                     scale=1.0 / D)
                        e.drain()
                        e.activation(st_rstd, st_rstd, AF.Exp, scale=-0.5)
                        e.drain()
                        e.activation(st_rstdn, st_rstd, AF.Identity,
                                     scale=-1.0 / D)
                        e.drain()
                        e.activation(st_nmr, st_sm, AF.Identity,
                                     scale=st_rstdn)
                        e.drain()
                        mark(s_c2, e.activation(p2_x1b, p2_r1, AF.Identity,
                                                bias=st_nmr, scale=st_rstd),
                             f"p2_x1b:{n}", 1)
                        if n >= 1:
                            dwait(e, f"p2_stx1t:{n - 1}")
                        e.drain()
                        mark(s_c2, e.activation(p2_r1, p2_r1, AF.Identity,
                                                bias=st_nmr, scale=st_rstd),
                             f"p2_x1tt:{n}", 1)
                        if n >= 1:
                            dwait(e, f"p2_stx1f:{n - 1}")
                        for grp in range(4):
                            e.wait_ge(s_t2.h, at(f"p2_x1tr:{n}:{grp}"))
                            dstp = (ps_c, ps_d)[grp % 2]
                            mark(s_c2, e.activation(
                                p2_stx1[:, 512 * grp:512 * grp + 512],
                                dstp[:], AF.Copy), f"p2_x1ev:{n}:{grp}", 1)
                        M2[f"p2_x1f:{n}"] = s_c2.n
                    M2[f"p2_x1evdone:{qgi}"] = s_c2.n
            M2["p2_lastx1ev"] = s_c2.n
            if upto < 3:
                return

            for tt in range(NTT):
                for isl in range(NIT):
                    e.wait_ge(s_t3.h, at(f"p3_gmm:{tt}:{isl}"))
                    gpar = isl % 2
                    gsrc = ps_c[:] if gpar == 0 else ps_big[:, 0:512]
                    if isl >= 1:
                        e.wait_ge(s_v3.h, at(f"p3_h:{tt}:{isl - 1}"))
                    mark(s_c3, e.activation(p3_sg, gsrc, AF.Silu),
                         f"p3_silu:{tt}:{isl}", 1)
                for o128 in range(16):
                    e.wait_ge(s_t3.h, at(f"p3_ymm:{tt}:{o128}"))
                    if tt >= 1 and o128 == 0:
                        e.wait_ge(s_t3.h, at(f"p3_yt:{tt * 4 - 1}:3"))
                    ysrc = (ps_e, ps_f)[o128 % 2]
                    mark(s_c3, e.activation(
                        p3_yf[:, 512 * o128:512 * o128 + 512], ysrc[:], AF.Copy),
                        f"p3_yev:{tt}:{o128}", 1)
                for tc in range(4):
                    n = tt * 4 + tc
                    e.drain()
                    for grp in range(4):
                        e.wait_ge(s_t3.h, at(f"p3_yt:{n}:{grp}"))
                        dstp = (ps_c, ps_d)[grp % 2]
                        mark(s_c3, e.activation(
                            p3_yt[:, 512 * grp:512 * grp + 512], dstp[:],
                            AF.Copy), f"p3_ytev:{n}:{grp}", 1)
                    if n >= 1:
                        dwait(e, f"out:{n - 1}")
                    e.drain()
                    e.activation(p3_out, p3_yt, AF.Square, accum_out=st_ssq)
                    e.drain()
                    e.activation(st_rstd, st_ssq, AF.Ln, bias=c_eps2,
                                 scale=1.0 / D)
                    e.drain()
                    mark(s_c3, e.activation(st_rstd, st_rstd, AF.Exp,
                                            scale=-0.5), f"p3_rms:{n}", 1)
                    e.wait_ge(s_v3.h, at(f"p3_r2:{n}"))
                    e.activation(p3_out, p3_yt, AF.Square, accum_out=st_ssq)
                    e.activation(p3_yt, p3_yt, AF.Identity, accum_out=st_sm)
                    e.drain()
                    e.activation(st_m2, st_sm, AF.Square, scale=1.0 / D)
                    e.drain()
                    e.activation(st_m2, st_m2, AF.Identity, scale=-1.0,
                                 bias=c_eps1[:, 0:1])
                    e.drain()
                    e.activation(st_rstd, st_ssq, AF.Ln, bias=st_m2,
                                 scale=1.0 / D)
                    e.drain()
                    e.activation(st_rstd, st_rstd, AF.Exp, scale=-0.5)
                    e.drain()
                    e.activation(st_rstdn, st_rstd, AF.Identity,
                                 scale=-1.0 / D)
                    e.drain()
                    e.activation(st_nmr, st_sm, AF.Identity, scale=st_rstdn)
                    e.drain()
                    mark(s_c3, e.activation(p3_out, p3_yt, AF.Identity,
                                            bias=st_nmr, scale=st_rstd),
                         f"p3_out:{n}", 1)
                M2[f"p3_ytevdone:{tt}"] = s_c3.n

        @blk.vector
        def _(e):
            def lut_pass(src_words, shift, rows, cols, half=False):
                r = rows
                e.tensor_scalar(dq_mi[r, :], src_words[r, cols], shift, 7,
                                ALU.logical_shift_right, ALU.bitwise_and)
                e.tensor_scalar(dq_si[r, :], src_words[r, cols], shift + 3, 1,
                                ALU.logical_shift_right, ALU.bitwise_and)
                e.drain()
                if half:
                    e.tensor_scalar(dq_r4[r, :], dq_mi[r, :], 1.0, -2.0,
                                    ALU.mult, ALU.add)
                    e.tensor_scalar(dq_r6[r, :], dq_mi[r, :], 2.0, -8.0,
                                    ALU.mult, ALU.add)
                    e.tensor_scalar(dq_m[r, :], dq_mi[r, :], 0.5, 0.0,
                                    ALU.mult, ALU.add)
                else:
                    e.tensor_scalar(dq_r4[r, :], dq_mi[r, :], 2.0, -4.0,
                                    ALU.mult, ALU.add)
                    e.tensor_scalar(dq_r6[r, :], dq_mi[r, :], 4.0, -16.0,
                                    ALU.mult, ALU.add)
                    e.tensor_copy(dq_m[r, :], dq_mi[r, :])
                e.tensor_scalar(dq_s[r, :], dq_si[r, :], -2.0, 1.0,
                                ALU.mult, ALU.add)
                e.drain()
                e.tensor_tensor(dq_r4[r, :], dq_r4[r, :], dq_r6[r, :], ALU.max)
                e.drain()
                e.tensor_tensor(dq_r4[r, :], dq_m[r, :], dq_r4[r, :], ALU.max)
                e.drain()
                e.tensor_tensor(dq_r4[r, :], dq_r4[r, :], dq_s[r, :], ALU.mult)
                e.drain()

            dq_jobs = []

            def _gu_job(n, j):
                def run():
                    par = n % 2
                    if j == 0:
                        dwait(e, f"p0_pk:{n}")
                        dwait(e, f"p0_sc:{n}")
                        if n >= 2:
                            dwait(e, f"p0_gu:{n - 2}")
                    lut_pass(dq_pk[par], 4 * j, slice(0, 128), slice(0, 512),
                             True)
                    ins = e.tensor_tensor(dq_out[par][:, 512 * j:512 * j + 512],
                                          dq_r4[:, 0:512], dq_sc[par], ALU.mult)
                    if j == 7:
                        mark(s_dq, ins, f"dqv:{n}", 1)
                return run

            def _gu_dn_bar():
                def run():
                    dwait(e, "p0gu:done")
                return run

            def _dn_job(t, q):
                def run():
                    a, j, mixed = DN_TILES[t]
                    par = t % 2
                    qpar = q % 2
                    if q == 0:
                        e.wait_ge(d_pk[par].h, at(f"p0_pkdone:{t}"))
                    dwait(e, f"p0_scd:{t}:{q}")
                    if 4 * t + q >= 2:
                        pt, pq = (t, q - 2) if q >= 2 else (t - 1, q + 2)
                        dwait(e, f"p0_dnst:{pt}:{pq}")
                    cols = slice(512 * q, 512 * q + 512)
                    if mixed:
                        lut_pass(dq_pk[par], 4 * j, slice(0, 64), cols, True)
                        lut_pass(dq_pk[par], 4 * (j + 1), slice(64, 128), cols,
                                 True)
                    else:
                        lut_pass(dq_pk[par], 4 * j, slice(0, 128), cols, True)
                    ins = e.tensor_tensor(
                        dq_out[par][:, 512 * qpar:512 * qpar + 512],
                        dq_r4[:, 0:512], dq_sc[qpar], ALU.mult)
                    mark(s_dq, ins, f"dqv:d{t}:{q}", 1)
                return run

            for n in range(NGU):
                for j in range(8):
                    _gu_job(n, j)()
            _gu_dn_bar()()
            for t in range(NIT):
                for q in range(4):
                    dq_jobs.append(_dn_job(t, q))
            dq_jobs.reverse()

            def dq_step(k):
                for _ in range(k):
                    if dq_jobs:
                        dq_jobs.pop()()

            def dq_drain():
                while dq_jobs:
                    dq_jobs.pop()()
                M2["dqv:all"] = s_dq.n

            if upto < 2:
                dq_drain()

            if upto >= 2:
                for s in range(SPC):
                    for g in range(4):
                        nblk, keys, qts = QG_PLANS[g]
                        qgi = s * 4 + g
                        zmap, mmap = CHUNK_MAPS[g]
                        for h in range(NH):
                            hi = qgi * NH + h
                            last = None
                            for c in range(nblk):
                                if not mmap[c]:
                                    continue
                                e.wait_ge(s_c2.h, at(f"p2_ex:{hi}:{c}"))
                                for (l, w) in mmap[c]:
                                    mt = {"tri": c_tri, "trisink": c_trisink,
                                          "sink": c_sink}[w]
                                    sub = p2_pt[:, 512 * c + 128 * l:
                                                512 * c + 128 * l + 128]
                                    last = e.tensor_tensor(sub, sub, mt,
                                                           ALU.mult)
                            if last is not None:
                                mark(s_v2, last, f"p2_mk:{hi}", 1)
                            e.wait_ge(s_t2.h, at(f"p2_den:{hi}"))
                            mark(s_v2, e.reciprocal(p2_rdr, ps_f[0:1, 0:512]),
                                 f"p2_rc:{hi}", 1)
                            e.wait_ge(s_t2.h, at(f"p2_pv:{hi}"))
                            e.wait_ge(s_c2.h, at(f"p2_bcev:{hi}"))
                            if qgi >= 1 and h == 0:
                                e.wait_ge(s_t2.h, at(f"p2_womm:{qgi - 1}:3"))
                            mark(s_v2, e.tensor_tensor(
                                p2_afm[:, 512 * h:512 * h + 512],
                                ps_e[:], p2_bc, ALU.mult),
                                f"p2_pvev:{hi}", 1)
                            dq_step(1)
                        for tc in range(4):
                            n = qgi * 4 + tc
                            e.wait_ge(s_t2.h, at(f"p2_wot:{n}"))
                            dwait(e, f"p2_x:{qgi}:{tc}")
                            if n >= 1:
                                dwait(e, f"p2_stx1t:{n - 1}")
                            mark(s_v2, e.tensor_tensor(p2_r1, ps_big[:],
                                                       p2_x32[tc % 2],
                                                       ALU.add),
                                 f"p2_r1:{n}", 1)
                            dq_step(2)
                dq_drain()

            if upto >= 3:
                for tt in range(NTT):
                    for isl in range(NIT):
                        e.wait_ge(s_c3.h, at(f"p3_silu:{tt}:{isl}"))
                        e.wait_ge(s_t3.h, at(f"p3_umm:{tt}:{isl}"))
                        if isl == 0 and tt >= 1:
                            e.wait_ge(s_t3.h, at(f"p3_ymm:{tt - 1}:15"))
                        gpar = isl % 2
                        usrc = ps_d[:] if gpar == 0 else ps_big[:, 512:1024]
                        mark(s_v3, e.tensor_tensor(
                            p3_h[:, 512 * isl:512 * isl + 512],
                            p3_sg, usrc, ALU.mult), f"p3_h:{tt}:{isl}", 1)
                    for tc in range(4):
                        n = tt * 4 + tc
                        e.wait_ge(s_c3.h, at(f"p3_rms:{n}"))
                        dwait(e, f"p3_x1r:{n}")
                        mark(s_v3, e.scalar_tensor_tensor(
                            p3_yt, p3_yt, st_rstd, p3_x1r,
                            ALU.mult, ALU.add), f"p3_r2:{n}", 1)

    return nc, M2



def _perm():
    k = np.arange(FI)
    return 8 * (k % DWRD) + (k // DWRD)


def _host_prep(inputs):
    f32 = np.float32
    x = np.asarray(inputs["L0_attn_hidden_states"], dtype=f32).reshape(
        BATCH * SEQ, D)
    h = {}
    for w, key in (("q", "L0_attn_Wq"), ("k", "L0_attn_Wk"),
                   ("v", "L0_attn_Wv"), ("o", "L0_attn_Wo")):
        h[f"w{w}t"] = np.ascontiguousarray(np.asarray(inputs[key], dtype=f32).T)
    for w, key in (("q", "L0_attn_bq"), ("k", "L0_attn_bk"),
                   ("v", "L0_attn_bv"), ("o", "L0_attn_bo")):
        h[f"b{w}"] = np.ascontiguousarray(
            np.asarray(inputs[key], dtype=f32).reshape(16, 128).T)

    perm = _perm()
    gup = np.asarray(inputs["L0_ffn_gate_up_packed"])
    gp = np.empty_like(gup)
    gp[:FI] = gup[:FI][perm]
    gp[FI:] = gup[FI:][perm]
    h["gut"] = np.ascontiguousarray(gp.T)
    gs = np.asarray(inputs["L0_ffn_gate_up_scales"])
    gsp = np.empty_like(gs)
    gsp[:, :FI] = gs[:, :FI][:, perm]
    gsp[:, FI:] = gs[:, FI:][:, perm]
    g16 = gsp.astype(np.float16) if gsp.dtype != np.float16 else gsp
    sgu = np.empty((2, NOB, 128, 512), np.float16)
    rows = np.arange(128) // 16
    for cc in range(2):
        for ob in range(NOB):
            sgu[cc, ob] = g16[8 * cc + rows, 512 * ob:512 * ob + 512]
    h["sgu"] = sgu
    h["dnt"] = np.ascontiguousarray(np.asarray(inputs["L0_ffn_down_packed"]).T)
    dsc16 = np.asarray(inputs["L0_ffn_down_scales"]).astype(np.float16)
    pos = np.arange(NIT * 128)
    grp = (8 * (pos % DWRD) + pos // DWRD) // 128
    h["dnsb"] = np.ascontiguousarray(dsc16[grp].reshape(NIT, 128, D))

    h["idm"] = np.eye(128, dtype=f32)
    r = np.arange(128)
    c = np.arange(128)
    h["mT_tri"] = np.ascontiguousarray(
        (c[None, :] >= r[:, None]).astype(f32).T)
    h["mT_trisink"] = np.ascontiguousarray(
        ((c[None, :] < 4) | (c[None, :] >= r[:, None])).astype(f32).T)
    h["mT_sink"] = np.ascontiguousarray(
        np.broadcast_to((c < 4).astype(f32), (128, 128)).T.copy())
    h["ones1"] = np.ones((1, 128), dtype=f32)
    h["onesd"] = np.ones((128, 1), dtype=f32)
    h["negsh"] = np.full((128, 1), -SHIFT, dtype=f32)
    h["eps1"] = np.full((128, 1), LN_EPS, dtype=f32)
    h["eps2"] = np.full((128, 1), RMS_EPS, dtype=f32)

    return x, h


_CACHED = {}


def _get_nc(dbg=(), upto=3):
    key = (tuple(sorted(dbg)), upto)
    if key not in _CACHED:
        _, M = build_program({}, dbg=dbg, upto=upto)
        nc, _ = build_program(M, dbg=dbg, upto=upto)
        _CACHED[key] = nc
    return _CACHED[key]


def _make_in_maps(inputs):
    x, shared = _host_prep(inputs)
    in_maps = []
    for c in range(NCORES):
        m = dict(shared)
        m["x"] = np.ascontiguousarray(x[c * TOK:(c + 1) * TOK])
        in_maps.append(m)
    return in_maps


def kernel(**inputs):
    nc = _get_nc()
    in_maps = _make_in_maps(inputs)
    res = run_bass_kernel_spmd(nc, in_maps, core_ids=list(range(NCORES)))
    out = np.concatenate([r["y"] for r in res.results], axis=0)
    return out.reshape(BATCH, SEQ, D)


# version 19
# speedup vs baseline: 1.0310x; 1.0310x over previous
from contextlib import ExitStack

import numpy as np
import ml_dtypes

import concourse.bass as bass
import concourse.mybir as mybir
from concourse.bass_utils import run_bass_kernel_spmd

F32 = mybir.dt.float32
BF16 = mybir.dt.bfloat16
F16 = mybir.dt.float16
I32 = mybir.dt.int32
AF = mybir.ActivationFunctionType
ALU = mybir.AluOpType

D = 2048
NH = 16
HD = 128
SEQ = 2048
BATCH = 16
NCORES = 8
SPC = 2
TOK = SPC * SEQ
QSCALE = float(HD) ** -0.5
SHIFT = 8.0

FI = 5632
GU = 2 * FI
GW = 256
DWRD = 704
NIT = 44
NOB = 22
NGU = 2 * NOB
TTOK = 512
NTT = TOK // TTOK

LN_EPS = 1e-5
RMS_EPS = 1e-6

DN_TILES = []
for _t in range(NIT):
    _a, _j = (128 * _t) % DWRD, (128 * _t) // DWRD
    DN_TILES.append((_a, _j, (128 * _t + 127) // DWRD != _j))


def _qg_plan(g):
    if g <= 1:
        nblk = 16
        keys = [128 * c for c in range(nblk)]
    else:
        kfg = 4 * (g - 1)
        nblk = 16 - kfg + 1
        keys = [0] + [128 * (kfg + c - 1) for c in range(1, nblk)]
    qts = []
    for l in range(4):
        masks, zeros = [], []
        if g == 0:
            accum = (0, nblk)
            runs = [(0, nblk)]
        elif g == 1:
            if l == 0:
                masks = [(0, "trisink")]
                accum = (1, nblk)
                runs = [(0, nblk)]
            else:
                masks = [(0, "sink"), (l, "tri")]
                zeros = list(range(1, l))
                accum = (l + 1, nblk)
                runs = [(0, 1), (l, nblk)]
        else:
            masks = [(0, "sink"), (l + 1, "tri")]
            zeros = list(range(1, l + 1))
            accum = (l + 2, nblk)
            runs = [(0, 1), (l + 1, nblk)]
        mms = []
        for (c0, c1) in runs:
            c = c0
            while c < c1:
                ce = min(c1, (c // 4 + 1) * 4)
                mms.append((c, ce))
                c = ce
        qts.append(dict(masks=masks, zeros=zeros, accum=accum, mms=mms))
    return nblk, keys, qts


QG_PLANS = [_qg_plan(g) for g in range(4)]
NGRP = [(QG_PLANS[g][0] + 3) // 4 for g in range(4)]


def _chunk_maps(g):
    nblk, keys, qts = QG_PLANS[g]
    zmap = [[] for _ in range(nblk)]
    mmap = [[] for _ in range(nblk)]
    for l in range(4):
        for c in qts[l]["zeros"]:
            zmap[c].append(l)
        for (c, w) in qts[l]["masks"]:
            mmap[c].append((l, w))
    return zmap, mmap


CHUNK_MAPS = [_chunk_maps(g) for g in range(4)]


def _lastgrp(qi):
    g = (qi // 64) % 4
    return NGRP[g] - 1


def build_program(M, dbg=(), upto=3):
    M2 = {}

    def at(tag):
        return M.get(tag, 0)

    nc = bass.Bass("TRN2", target_bir_lowering=False, debug=False)

    def din(name, shape, dtype):
        return nc.dram_tensor(name, shape, dtype, kind="ExternalInput")

    def scratch(name, shape, dtype):
        kind = "ExternalOutput" if name in dbg else "Internal"
        return nc.dram_tensor(name, shape, dtype, kind=kind)

    x_in = din("x", [TOK, D], F32)
    w_t = {w: din(f"w{w}t", [D, D], F32) for w in "qkvo"}
    b_in = {w: din(f"b{w}", [128, 16], F32) for w in "qkvo"}
    gut = din("gut", [GW, GU], I32)
    sgu = din("sgu", [2, NOB, 128, 512], F16)
    dnt = din("dnt", [DWRD, D], I32)
    dnsb = din("dnsb", [NIT, 128, 2048], F16)
    idm = din("idm", [128, 128], F32)
    m_tri = din("mT_tri", [128, 128], F32)
    m_trisink = din("mT_trisink", [128, 128], F32)
    m_sink = din("mT_sink", [128, 128], F32)
    ones1 = din("ones1", [1, 128], F32)
    onesd = din("onesd", [128, 1], F32)
    negsh = din("negsh", [128, 1], F32)
    eps1 = din("eps1", [128, 1], F32)
    eps2 = din("eps2", [128, 1], F32)

    y_out = nc.dram_tensor("y", [TOK, D], F32, kind="ExternalOutput")

    s_wbf = {w: scratch(f"s_w{w}", [D, D], BF16) for w in "qkvo"}
    s_xfm = scratch("s_xfm", [SPC, D, SEQ], BF16)
    s_q = scratch("s_q", [SPC, D, SEQ], BF16)
    s_k = scratch("s_k", [SPC, D, SEQ], BF16)
    s_v = scratch("s_v", [SPC, NH, 128, SEQ], BF16)
    s_wgu = scratch("s_wgu", [2, NOB, 8, 128, 512], BF16)
    s_wdn = scratch("s_wdn", [NIT, 128, 2048], BF16)
    s_x1t = scratch("s_x1t", [TOK, D], F32)
    s_x1f = scratch("s_x1f", [D, TOK], BF16)

    with ExitStack() as ctx:
        AR_BYTES = 204 * 1024
        arena_h = ctx.enter_context(
            nc.sbuf_tensor("arena", [128, AR_BYTES // 2], BF16))
        ARENA_ADDR = nc.lookup_mloc(arena_h).addr
        ESZ = {F32: 4, I32: 4, BF16: 2, F16: 2}

        class Cur:
            def __init__(self, base):
                self.o = base

            def take(self, nbytes, align=64):
                self.o = (self.o + align - 1) // align * align
                r = self.o
                self.o += nbytes
                return r

        _tn = [0]

        def tile(cur, rows, cols, dtype):
            esz = ESZ[dtype]
            off = cur.take(cols * esz)
            _tn[0] += 1
            h = nc.alloc_sbuf_tensor_at(f"tl{_tn[0]}", [rows, cols], dtype,
                                        offset=ARENA_ADDR + off)
            return h[:, :]

        def tile_at(byte_off, rows, cols, dtype):
            _tn[0] += 1
            h = nc.alloc_sbuf_tensor_at(f"tl{_tn[0]}", [rows, cols], dtype,
                                        offset=ARENA_ADDR + byte_off)
            return h[:, :]

        R0 = Cur(0)
        c_id32 = tile(R0, 128, 128, F32)
        c_id16 = tile(R0, 128, 128, BF16)
        c_tri = tile(R0, 128, 128, BF16)
        c_trisink = tile(R0, 128, 128, BF16)
        c_sink = tile(R0, 128, 128, BF16)
        c_ones1 = tile(R0, 1, 128, F32)
        c_onesd = tile(R0, 128, 1, BF16)
        c_neg8 = tile(R0, 128, 1, F32)
        c_eps1 = tile(R0, 128, 1, F32)
        c_eps2 = tile(R0, 128, 1, F32)
        c_bias = {w: tile(R0, 128, 16, F32) for w in "qkvo"}
        c_bqs = tile(R0, 128, 16, F32)
        st_bn = tile(R0, 128, 24, F32)
        st_ag = tile(R0, 128, 2, F32)
        st_var = tile(R0, 128, 1, F32)
        st_rstd = tile(R0, 128, 1, F32)
        st_nmr = tile(R0, 128, 1, F32)
        st_sm = tile(R0, 128, 1, F32)
        st_ssq = tile(R0, 128, 1, F32)
        st_m2 = tile(R0, 128, 1, F32)
        st_rstdn = tile(R0, 128, 1, F32)
        c_stv2 = tile(R0, 128, 2048, BF16)
        p2_dn = tile(R0, 128, 4, F32)
        p2_kp = tile(R0, 128, 2, F32)
        p2_rd = tile(R0, 128, 1, F32)
        p2_rdr = tile(R0, 1, 512, F32)
        p2_bc = tile(R0, 128, 512, F32)
        assert R0.o <= 15 * 1024, R0.o

        RD_BASE = 15 * 1024
        RD = Cur(RD_BASE)
        dq_pk = [tile(RD, 128, 2048, I32) for _ in range(2)]
        dq_sc = [tile(RD, 128, 512, F32) for _ in range(2)]
        dq_out = [tile(RD, 128, 4096, BF16) for _ in range(2)]
        dq_mi = tile(RD, 128, 512, I32)
        dq_si = tile(RD, 128, 512, I32)
        dq_m = tile(RD, 128, 512, F32)
        dq_s = tile(RD, 128, 512, F32)
        dq_r4 = tile(RD, 128, 512, F32)
        dq_r6 = tile(RD, 128, 512, F32)
        assert RD.o <= 64 * 1024, RD.o
        RDP3 = Cur(RD_BASE)
        p3_wd = [tile(RDP3, 128, 22 * 256, BF16) for _ in range(2)]
        assert RDP3.o <= 64 * 1024, RDP3.o

        RP_BASE = 64 * 1024
        RP = Cur(RP_BASE)
        p1_wog = [tile(RP, 128, 16 * 512, BF16) for _ in range(2)]
        WOG_END = RP.o
        p1_x32 = [tile(RP, 128, 2048, F32) for _ in range(2)]
        p1_xb = [tile(RP, 128, 2048, BF16) for _ in range(2)]
        STX_OFF = (RP.o + 63) // 64 * 64
        p1_stx = tile(RP, 128, 4 * 2048, BF16)
        XQ_OFF = (RP.o + 63) // 64 * 64
        p1_xq = [tile(RP, 128, 16 * 512, BF16) for _ in range(2)]
        STG_OFF = (RP.o + 63) // 64 * 64
        p1_stg = tile(RP, 128, 4 * 2048, BF16)
        VFM_OFF = (RP.o + 63) // 64 * 64
        p1_vfm = tile(RP, 128, 4 * 2048, BF16)
        STV_OFF = (RP.o + 63) // 64 * 64
        p1_stv = tile(RP, 128, 2048, BF16)
        assert RP.o <= AR_BYTES, RP.o
        p1_stvb = [p1_stv, c_stv2]

        wb16 = [tile_at(VFM_OFF, 128, 2048, BF16),
                tile_at(VFM_OFF + 4096, 128, 2048, BF16)]

        RP2 = Cur(WOG_END)
        p2_q = [tile(RP2, 128, 512, BF16) for _ in range(2)]
        p2_k = [tile(RP2, 128, 2048, BF16) for _ in range(2)]
        p2_v = [tile(RP2, 128, 2048, BF16) for _ in range(2)]
        p2_pr = [tile(RP2, 128, 2048, BF16) for _ in range(2)]
        p2_pt = tile(RP2, 128, 16 * 512, BF16)
        p2_afm = tile(RP2, 128, 16 * 512, BF16)
        p2_wof = tile(RP2, 128, 16 * 512, BF16)
        p2_x32 = [tile(RP2, 128, 2048, F32) for _ in range(2)]
        p2_r1 = tile(RP2, 128, 2048, F32)
        p2_x1b = tile(RP2, 128, 2048, BF16)
        p2_stx1 = tile(RP2, 128, 2048, BF16)
        assert RP2.o <= AR_BYTES, RP2.o

        RP3 = Cur(RP_BASE)
        p3_x1j = tile(RP3, 128, 16 * 512, BF16)
        p3_wg = [tile(RP3, 128, 16 * 256, BF16) for _ in range(2)]
        p3_wu = [tile(RP3, 128, 16 * 256, BF16) for _ in range(2)]
        p3_sg = tile(RP3, 128, 512, BF16)
        p3_h = tile(RP3, 128, NIT * 512, BF16)
        p3_yf = tile(RP3, 128, 16 * 512, BF16)
        p3_yt = tile(RP3, 128, 2048, F32)
        p3_x1r = tile(RP3, 128, 2048, F32)
        p3_out = tile(RP3, 128, 2048, F32)
        assert RP3.o <= AR_BYTES, RP3.o

        psa = lambda n, s: ctx.enter_context(nc.psum_tensor(n, s, F32))
        ps_big = psa("ps_big", [128, 2048])
        ps_c = psa("ps_c", [128, 512])
        ps_d = psa("ps_d", [128, 512])
        ps_e = psa("ps_e", [128, 512])
        ps_f = psa("ps_f", [128, 512])
        QUAR = lambda i: ps_big[:, 512 * i:512 * i + 512]
        SETB = [ps_c, ps_d, ps_e, ps_f]

        def mksem(name):
            class Ct:
                __slots__ = ("h", "n", "name")
            c = Ct()
            c.h = ctx.enter_context(nc.semaphore(name))
            c.n = 0
            c.name = name
            return c

        def mark(c, instr, tag=None, k=1):
            instr.then_inc(c.h, k)
            c.n += k
            if tag is not None:
                M2[tag] = c.n
            return c.n

        s_pret = mksem("s_pret")
        s_prec = mksem("s_prec")
        s_c1 = mksem("s_c1")
        s_t1 = mksem("s_t1")
        s_t2 = mksem("s_t2")
        s_v2 = mksem("s_v2")
        s_c2 = mksem("s_c2")
        s_t3 = mksem("s_t3")
        s_v3 = mksem("s_v3")
        s_c3 = mksem("s_c3")
        s_dq = mksem("s_dq")

        d_cst = mksem("d_cst")
        d_gusb = [mksem(f"d_gusb{i}") for i in range(2)]
        d_indb = [mksem(f"d_indb{i}") for i in range(2)]
        d_pres = [mksem(f"d_pres{i}") for i in range(4)]
        d_presd = [mksem(f"d_presd{i}") for i in range(2)]
        d_p0bld = [mksem(f"d_p0bld{i}") for i in range(2)]
        d_p0bst = [mksem(f"d_p0bst{i}") for i in range(2)]
        d_p1ax = [mksem(f"d_p1ax{i}") for i in range(2)]
        d_p1ast = mksem("d_p1ast")
        d_p1w = [mksem(f"d_p1w{i}") for i in range(2)]
        d_p1xq = [mksem(f"d_p1xq{i}") for i in range(2)]
        d_p1st = mksem("d_p1st")
        d_p1stv = mksem("d_p1stv")
        d_p2k = [mksem(f"d_p2k{i}") for i in range(2)]
        d_p2v = [mksem(f"d_p2v{i}") for i in range(2)]
        d_p2q = [mksem(f"d_p2q{i}") for i in range(2)]
        d_p2wo = [mksem(f"d_p2wo{i}") for i in range(2)]
        d_p2x = [mksem(f"d_p2x{i}") for i in range(2)]
        d_p2s1t = mksem("d_p2s1t")
        d_p2s1f = mksem("d_p2s1f")
        d_p3x1j = mksem("d_p3x1j")
        d_p3wg = [mksem(f"d_p3wg{i}") for i in range(2)]
        d_p3wu = [mksem(f"d_p3wu{i}") for i in range(2)]
        d_p3wd = [mksem(f"d_p3wd{i}") for i in range(2)]
        d_p3x1r = mksem("d_p3x1r")
        d_out = mksem("d_out")
        d_poolc = mksem("d_poolc")
        d_pk = [mksem(f"d_pk{i}") for i in range(2)]
        d_psc = [mksem(f"d_psc{i}") for i in range(2)]
        d_gust = [mksem(f"d_gust{i}") for i in range(2)]
        d_dnst = [mksem(f"d_dnst{i}") for i in range(2)]

        def droute(tag):
            p = tag.split(":")
            f = p[0]
            if f == "cst":
                return d_cst
            if f == "gusb":
                return d_gusb[int(p[1]) % 2]
            if f == "indb":
                return d_indb[int(p[1]) % 2]
            if f == "pres":
                k = p[1]
                if k.startswith("d"):
                    return d_presd[int(k[1:]) % 2]
                return d_pres[int(k) % 4]
            if f == "p0b_ld":
                return d_p0bld[int(p[1]) % 2]
            if f == "p0b_st":
                return d_p0bst[int(p[1]) % 2]
            if f == "p1a_x":
                return d_p1ax[int(p[1]) % 2]
            if f == "p1a_st":
                return d_p1ast
            if f == "p1_w":
                return d_p1w[int(p[3]) % 2]
            if f == "p1_xq":
                return d_p1xq[int(p[4]) % 2]
            if f == "p1_st":
                return d_p1st
            if f == "p1_stv":
                return d_p1stv
            if f == "p2_k":
                return d_p2k[int(p[1]) % 2]
            if f == "p2_v":
                return d_p2v[int(p[1]) % 2]
            if f == "p2_q":
                return d_p2q[int(p[1]) % 2]
            if f == "p2_wo":
                return d_p2wo[int(p[2]) % 2]
            if f == "p2_x":
                return d_p2x[int(p[2]) % 2]
            if f == "p2_stx1t":
                return d_p2s1t
            if f == "p2_stx1f":
                return d_p2s1f
            if f == "p3_x1j":
                return d_p3x1j
            if f == "p3_wg":
                return d_p3wg[int(p[2]) % 2]
            if f == "p3_wu":
                return d_p3wu[int(p[2]) % 2]
            if f == "p3_wd":
                return d_p3wd[int(p[2]) % 2]
            if f == "p3_x1r":
                return d_p3x1r
            if f == "out":
                return d_out
            if f == "poolcst":
                return d_poolc
            if f in ("p0_pk", "p0_pkd"):
                return d_pk[int(p[1]) % 2]
            if f == "p0_sc":
                return d_psc[int(p[1]) % 2]
            if f == "p0_scd":
                return d_psc[int(p[2]) % 2]
            if f == "p0_gu":
                return d_gust[int(p[1]) % 2]
            if f == "p0_dnst":
                return d_dnst[int(p[2]) % 2]
            raise KeyError(tag)

        SPECIALS = {
            "pres:all": [(d_pres[i], f"pres:f{i}") for i in range(4)],
            "p0b:done": [(d_p0bst[0], "p0bst:f0"), (d_p0bst[1], "p0bst:f1")],
            "p1:done": [(d_p1st, "p1st:f"), (d_p1stv, "p1stv:f"),
                        (d_p1ast, "p1ast:f"),
                        (d_p0bst[0], "p0bst:f0"), (d_p0bst[1], "p0bst:f1")],
            "p2:done": [(d_p2s1t, "p2s1t:f"), (d_p2s1f, "p2s1f:f")],
            "p0gu:done": [(d_gust[0], "gust:f0"), (d_gust[1], "gust:f1")],
            "p0:done": [(d_gust[0], "gust:f0"), (d_gust[1], "gust:f1"),
                        (d_dnst[0], "dnst:f0"), (d_dnst[1], "dnst:f1")],
        }

        def dmark(instr, tag):
            return mark(droute(tag), instr, tag, 16)

        def dwait(e, tag):
            if tag in SPECIALS:
                for c, t in SPECIALS[tag]:
                    e.wait_ge(c.h, at(t))
            else:
                e.wait_ge(droute(tag).h, at(tag))

        blk = ctx.enter_context(nc.Block())

        @blk.sync
        def _(e):
            for dst_t, src_t in ((c_id32, idm), (c_ones1, ones1),
                                 (c_neg8, negsh),
                                 (c_eps1, eps1), (c_eps2, eps2)):
                dmark(e.dma_start(out=dst_t, in_=src_t[:]), "cst")
            for w in "qkvo":
                dmark(e.dma_start(out=c_bias[w], in_=b_in[w][:]), "cst")
            M2["cst"] = d_cst.n



            for s in range(SPC):
                for tt in range(16):
                    n = s * 16 + tt
                    if n >= 2:
                        e.wait_ge(s_c1.h, at(f"p1a_cast:{n - 2}"))
                    dmark(e.dma_start(out=p1_x32[n % 2],
                                      in_=x_in[2048 * s + 128 * tt:
                                               2048 * s + 128 * tt + 128, :]),
                          f"p1a_x:{n}")
                    if tt % 4 == 3:
                        tq = tt // 4
                        e.wait_ge(s_c1.h, at(f"p1a_evac:{n}"))
                        for dc in range(16):
                            src = p1_stx.rearrange("p (k c t) -> p k c t",
                                                   k=4, c=16)[:, :, dc, :]
                            dmark(e.dma_start(
                                out=s_xfm[s, 128 * dc:128 * dc + 128,
                                          512 * tq:512 * tq + 512], in_=src),
                                f"p1a_st:{s}:{tq}:{dc}")
                for wi, w in enumerate("qkv"):
                    dwait(e, f"p0b_st:{16 * wi + 15}")
                    dwait(e, f"p0b_st:{16 * wi + 14}")
                    for og in range(4):
                        par = og % 2
                        u0 = (s * 3 + wi) * 4 + og
                        if u0 >= 2:
                            pu = u0 - 2
                            ps_, r = divmod(pu, 12)
                            pw, po = divmod(r, 4)
                            e.wait_ge(s_t1.h, at(f"p1_mm:{ps_}:{pw}:{po}:3"))
                        for dc in range(16):
                            dmark(e.dma_start(
                                out=p1_wog[par][:, 512 * dc:512 * dc + 512],
                                in_=s_wbf[w][128 * dc:128 * dc + 128,
                                             512 * og:512 * og + 512]),
                                f"p1_w:{s}:{wi}:{og}:{dc}")
                        for ol in range(4):
                            oc = 4 * og + ol
                            e.wait_ge(s_c1.h, at(f"p1_evac:{s}:{wi}:{og}:3:{ol}"))
                            dstt = {"q": s_q, "k": s_k}.get(w)
                            if dstt is not None:
                                dmark(e.dma_start(
                                    out=dstt[s, 128 * oc:128 * oc + 128, :],
                                    in_=p1_stg[:, 2048 * ol:2048 * ol + 2048]),
                                    f"p1_st:{s}:{wi}:{oc}")
            M2["p1st:f"] = d_p1st.n
            M2["p1stv:f"] = d_p1stv.n
            M2["p1ast:f"] = d_p1ast.n
            if upto < 2:
                dwait(e, "p1:done")
                dwait(e, "p0b:done")
                dwait(e, "p0:done")
                return

            dwait(e, "p1:done")
            for s in range(SPC):
                for g in range(4):
                    qgi = s * 4 + g
                    for h in range(NH):
                        par = h % 2
                        hi = qgi * NH + h
                        if hi >= 2:
                            e.wait_ge(s_t2.h, at(f"p2_pv:{hi - 2}"))
                        dmark(e.dma_start(out=p2_k[par],
                                          in_=s_k[s, 128 * h:128 * h + 128, :]),
                              f"p2_k:{hi}")
                        dmark(e.dma_start(out=p2_v[par], in_=s_v[s, h]),
                              f"p2_v:{hi}")
                        dmark(e.dma_start(out=p2_q[par],
                                          in_=s_q[s, 128 * h:128 * h + 128,
                                                  512 * g:512 * g + 512]),
                              f"p2_q:{hi}")
                    for og in range(4):
                        par = og % 2
                        if qgi == 0 and og < 2:
                            e.wait_ge(s_t1.h, at("p1_mm:1:2:3:3"))
                        if og >= 2:
                            e.wait_ge(s_t2.h, at(f"p2_womm:{qgi}:{og - 2}"))
                        elif qgi >= 1:
                            e.wait_ge(s_t2.h, at(f"p2_womm:{qgi - 1}:{og + 2}"))
                        for dc in range(16):
                            dmark(e.dma_start(
                                out=p1_wog[par][:, 512 * dc:512 * dc + 512],
                                in_=s_wbf["o"][128 * dc:128 * dc + 128,
                                               512 * og:512 * og + 512]),
                                f"p2_wo:{qgi}:{og}:{dc}")
                    def emit_x1store(pn):
                        pt0 = (2048 * (pn // 16) + 512 * ((pn // 4) % 4)
                               + 128 * (pn % 4))
                        e.wait_ge(s_c2.h, at(f"p2_x1tt:{pn}"))
                        dmark(e.dma_start(out=s_x1t[pt0:pt0 + 128, :],
                                          in_=p2_r1),
                              f"p2_stx1t:{pn}")
                        e.wait_ge(s_c2.h, at(f"p2_x1f:{pn}"))
                        dst = s_x1f.rearrange("(c p) t -> p c t", p=128)[
                            :, :, pt0:pt0 + 128]
                        dmark(e.dma_start(out=dst,
                                          in_=p2_stx1.rearrange(
                                              "p (c t) -> p c t", c=16)),
                              f"p2_stx1f:{pn}")

                    for tc in range(4):
                        n = qgi * 4 + tc
                        t0 = 2048 * s + 512 * g + 128 * tc
                        if n >= 1:
                            emit_x1store(n - 1)
                        if n >= 2:
                            e.wait_ge(s_v2.h, at(f"p2_r1:{n - 2}"))
                        dmark(e.dma_start(out=p2_x32[tc % 2],
                                          in_=x_in[t0:t0 + 128, :]),
                              f"p2_x:{qgi}:{tc}")
            emit_x1store(SPC * 16 - 1)
            M2["p2s1t:f"] = d_p2s1t.n
            M2["p2s1f:f"] = d_p2s1f.n
            if upto < 3:
                dwait(e, "p2:done")
                dwait(e, "p1:done")
                dwait(e, "p0:done")
                return

            dwait(e, "p2:done")
            for tt in range(NTT):
                if tt >= 1:
                    e.wait_ge(s_t3.h, at(f"p3_gudone:{tt - 1}"))
                for cc in range(2):
                    for j in range(8):
                        k = 8 * cc + j
                        src = s_x1f.rearrange("(w r) t -> w r t", r=8)[
                            128 * cc:128 * cc + 128, j,
                            TTOK * tt:TTOK * tt + TTOK]
                        dmark(e.dma_start(out=p3_x1j[:, 512 * k:512 * k + 512],
                                          in_=src), f"p3_x1j:{tt}:{k}")
                for usb in range(22):
                    gp, sbi = divmod(usb, 2)
                    par = usb % 2
                    if usb >= 2:
                        e.wait_ge(s_t3.h, at(f"p3_gumm:{tt}:{usb - 2}"))
                    elif tt >= 1:
                        e.wait_ge(s_t3.h, at(f"p3_gumm:{tt - 1}:{usb + 20}"))
                    dwait(e, "p0gu:done")
                    for cc in range(2):
                        srcg = s_wgu[cc, gp].rearrange("j p c -> p j c")[
                            :, :, 256 * sbi:256 * sbi + 256]
                        srcu = s_wgu[cc, 11 + gp].rearrange("j p c -> p j c")[
                            :, :, 256 * sbi:256 * sbi + 256]
                        dmark(e.dma_start(
                            out=p3_wg[par].rearrange("p (k c) -> p k c", c=256)[
                                :, 8 * cc:8 * cc + 8, :], in_=srcg),
                            f"p3_wg:{tt}:{usb}:{cc}")
                        dmark(e.dma_start(
                            out=p3_wu[par].rearrange("p (k c) -> p k c", c=256)[
                                :, 8 * cc:8 * cc + 8, :], in_=srcu),
                            f"p3_wu:{tt}:{usb}:{cc}")
                for ds in range(16):
                    ob, r = divmod(ds, 4)
                    ocp, half = divmod(r, 2)
                    par = ds % 2
                    if ds >= 2:
                        e.wait_ge(s_t3.h, at(f"p3_dmm:{tt}:{ds - 2}"))
                    elif tt >= 1:
                        e.wait_ge(s_t3.h, at(f"p3_dmm:{tt - 1}:{ds + 14}"))
                    dwait(e, "p0:done")
                    col0 = 512 * ob + 256 * ocp
                    for q in range(2):
                        src = s_wdn[22 * half + 11 * q:22 * half + 11 * q + 11,
                                    :, col0:col0 + 256].rearrange(
                                        "t p c -> p t c")
                        dmark(e.dma_start(
                            out=p3_wd[par].rearrange("p (i c) -> p i c", c=256)[
                                :, 11 * q:11 * q + 11, :], in_=src),
                            f"p3_wd:{tt}:{ds}:{q}")
                def emit_ostore(pn):
                    pt0 = TTOK * (pn // 4) + 128 * (pn % 4)
                    e.wait_ge(s_c3.h, at(f"p3_out:{pn}"))
                    dmark(e.dma_start(out=y_out[pt0:pt0 + 128, :], in_=p3_out),
                          f"out:{pn}")

                for tc in range(4):
                    n = tt * 4 + tc
                    t0 = TTOK * tt + 128 * tc
                    if n >= 1:
                        emit_ostore(n - 1)
                        e.wait_ge(s_v3.h, at(f"p3_r2:{n - 1}"))
                    dmark(e.dma_start(out=p3_x1r, in_=s_x1t[t0:t0 + 128, :]),
                          f"p3_x1r:{n}")
            emit_ostore(NTT * 4 - 1)
            e.wait_ge(d_out.h, d_out.n)

        @blk.gpsimd
        def _(e):
            e.dma_start(out=c_id16, in_=idm[:]).then_inc(d_poolc.h, 16)
            e.dma_start(out=c_tri, in_=m_tri[:]).then_inc(d_poolc.h, 16)
            e.dma_start(out=c_trisink, in_=m_trisink[:]).then_inc(d_poolc.h, 16)
            e.dma_start(out=c_onesd, in_=onesd[:]).then_inc(d_poolc.h, 16)
            d_poolc.n = 64
            dmark(e.dma_start(out=c_sink, in_=m_sink[:]), "poolcst")
            M2["poolcst"] = d_poolc.n

            def p0b_tile(n):
                wi_, dt = divmod(n, 16)
                w = "qkvo"[wi_]
                if n >= 2:
                    dwait(e, f"p0b_st:{n - 2}")
                dmark(e.dma_start(out=wb16[n % 2],
                                  in_=w_t[w][128 * dt:128 * dt + 128, :]),
                      f"p0b_ld:{n}")
                dwait(e, f"p0b_ld:{n}")
                dmark(e.dma_start(out=s_wbf[w][128 * dt:128 * dt + 128, :],
                                  in_=wb16[n % 2]), f"p0b_st:{n}")

            for n in range(48):
                p0b_tile(n)

            for n in range(NGU):
                cc, ob = divmod(n, NOB)
                par = n % 2
                for _w in range(4):
                    if 48 + 4 * n + _w < 64:
                        p0b_tile(48 + 4 * n + _w)
                if 30 <= n < 38:
                    for _v in range(2):
                        oc = 2 * (n - 30) + _v
                        e.wait_ge(s_c1.h,
                                  at(f"p1_vevdone:0:{oc // 4}:{oc % 4}"))
                        dmark(e.dma_start(out=s_v[0, oc],
                                          in_=p1_stvb[oc % 2]),
                              f"p1_stv:0:{oc}")
                if n >= 2:
                    e.wait_ge(s_dq.h, at(f"dqv:{n - 2}"))
                dmark(e.dma_start(out=dq_pk[par][:, 0:512],
                                  in_=gut[128 * cc:128 * cc + 128,
                                          512 * ob:512 * ob + 512]),
                      f"p0_pk:{n}")
                dmark(e.dma_start(out=dq_sc[par], in_=sgu[cc, ob]),
                      f"p0_sc:{n}")
                e.wait_ge(s_dq.h, at(f"dqv:{n}"))
                dmark(e.dma_start(
                    out=s_wgu[cc, ob].rearrange("j p c -> p j c"),
                    in_=dq_out[par].rearrange("p (j c) -> p j c", j=8)),
                    f"p0_gu:{n}")
            M2["p0bst:f0"] = d_p0bst[0].n
            M2["p0bst:f1"] = d_p0bst[1].n
            for oc in range(16):
                e.wait_ge(s_c1.h, at(f"p1_vevdone:1:{oc // 4}:{oc % 4}"))
                dmark(e.dma_start(out=s_v[1, oc], in_=p1_stvb[oc % 2]),
                      f"p1_stv:1:{oc}")
            for t in range(NIT):
                a, j, mixed = DN_TILES[t]
                par = t % 2
                if t >= 2:
                    e.wait_ge(s_dq.h, at(f"dqv:d{t - 2}:3"))
                else:
                    e.wait_ge(s_dq.h, at(f"dqv:{NGU - 2 + t}"))
                if mixed:
                    dmark(e.dma_start(out=dq_pk[par][0:64, :],
                                      in_=dnt[a:a + 64, :]), f"p0_pkd:{t}")
                    dmark(e.dma_start(out=dq_pk[par][64:128, :],
                                      in_=dnt[0:64, :]), f"p0_pkd:{t}")
                else:
                    dmark(e.dma_start(out=dq_pk[par], in_=dnt[a:a + 128, :]),
                          f"p0_pkd:{t}")
                M2[f"p0_pkdone:{t}"] = d_pk[par].n
                for q in range(4):
                    qpar = q % 2
                    if 4 * t + q >= 2:
                        pt, pq = (t, q - 2) if q >= 2 else (t - 1, q + 2)
                        e.wait_ge(s_dq.h, at(f"dqv:d{pt}:{pq}"))
                    else:
                        e.wait_ge(s_dq.h, at(f"dqv:{NGU - 2 + qpar}"))
                    dmark(e.dma_start(
                        out=dq_sc[qpar],
                        in_=dnsb[t, :, 512 * q:512 * q + 512]),
                        f"p0_scd:{t}:{q}")
                    e.wait_ge(s_dq.h, at(f"dqv:d{t}:{q}"))
                    dmark(e.dma_start(
                        out=s_wdn[t, :, 512 * q:512 * q + 512],
                        in_=dq_out[par][:, 512 * qpar:512 * qpar + 512]),
                        f"p0_dnst:{t}:{q}")
            M2["gust:f0"] = d_gust[0].n
            M2["gust:f1"] = d_gust[1].n
            M2["dnst:f0"] = d_dnst[0].n
            M2["dnst:f1"] = d_dnst[1].n

        @blk.tensor
        def _(e):
            dwait(e, "cst")
            dwait(e, "poolcst")
            for s in range(SPC):
                for tt in range(16):
                    n = s * 16 + tt
                    e.wait_ge(s_c1.h, at(f"p1a_cast:{n}"))
                    if n >= 1:
                        e.wait_ge(s_c1.h, at(f"p1a_evac:{n - 1}"))
                    if s == 1 and tt == 0:
                        e.wait_ge(s_c1.h, at("p1_evacdone:0:2:3:3"))
                    for dc in range(16):
                        ins = e.matmul(ps_big[:, 128 * dc:128 * dc + 128],
                                       p1_xb[n % 2][:, 128 * dc:128 * dc + 128],
                                       c_id16, start=True, stop=True,
                                       skip_group_check=True)
                    mark(s_t1, ins, f"p1a_mm:{n}", 1)
                for wi, w in enumerate("qkv"):
                    for og in range(4):
                        par = og % 2
                        dwait(e, f"p1_w:{s}:{wi}:{og}:15")
                        for tq in range(4):
                            xpar = tq % 2
                            dwait(e, f"p1_xq:{s}:{wi}:{og}:{tq}:15")
                            u = ((s * 3 + wi) * 4 + og) * 4 + tq
                            if u >= 2:
                                pu = u - 2
                                pl = pu % 4
                                pog = (pu // 4) % 4
                                pw = (pu // 16) % 3
                                ps_ = pu // 48
                                e.wait_ge(s_c1.h,
                                          at(f"p1_evacdone:{ps_}:{pw}:{pog}:{pl}"))
                                if pw == 2 and pl == 3:
                                    e.wait_ge(s_c1.h,
                                              at(f"p1_vevdone:{ps_}:{pog}:3"))
                            elif s == 0 and wi == 0:
                                e.wait_ge(s_c1.h, at(f"p1a_evac:{15}"))
                            if w == "v" and tq == 1 and og >= 1:
                                e.wait_ge(s_c1.h, at(f"p1_vevdone:{s}:{og - 1}:3"))
                            for dc in range(16):
                                for ol in range(4):
                                    dst = (QUAR(ol) if tq % 2 == 0
                                           else SETB[ol][:])
                                    ins = e.matmul(
                                        dst,
                                        p1_wog[par][:, 512 * dc + 128 * ol:
                                                    512 * dc + 128 * ol + 128],
                                        p1_xq[xpar][:, 512 * dc:512 * dc + 512],
                                        start=(dc == 0), stop=(dc == 15))
                            mark(s_t1, ins, f"p1_mm:{s}:{wi}:{og}:{tq}", 1)
                        if w == "v":
                            for ol in range(4):
                                e.wait_ge(s_c1.h, at(f"p1_evac:{s}:{wi}:{og}:3:{ol}"))
                                if ol == 0:
                                    e.wait_ge(s_c1.h,
                                              at(f"p1_evac:{s}:{wi}:{og}:3:1"))
                                for kc in range(16):
                                    grp = kc // 4
                                    dstp = (ps_c, ps_d)[grp % 2]
                                    vsl = 4 * ol + grp
                                    if kc % 4 == 0 and vsl >= 2:
                                        e.wait_ge(s_c1.h,
                                                  at(f"p1_vev:{s}:{og}:"
                                                     f"{(vsl - 2) // 4}:"
                                                     f"{(vsl - 2) % 4}"))
                                    ins = e.matmul(
                                        dstp[:, 128 * (kc % 4):128 * (kc % 4) + 128],
                                        p1_vfm[:, 2048 * ol + 128 * kc:
                                               2048 * ol + 128 * kc + 128],
                                        c_id16, start=True, stop=True,
                                        skip_group_check=True)
                                    if kc % 4 == 3:
                                        mark(s_t1, ins, f"p1_vtr:{s}:{og}:{ol}:{grp}", 1)
                M2[f"p1_vtrdone:{s}"] = s_t1.n
            if upto < 2:
                return

            sc_bank = [None] * 4
            for s in range(SPC):
                for g in range(4):
                    nblk, keys, qts = QG_PLANS[g]
                    zmap, mmap = CHUNK_MAPS[g]
                    qgi = s * 4 + g
                    def emit_bc(phi):
                        e.wait_ge(s_v2.h, at(f"p2_rc:{phi}"))
                        ins = e.matmul(ps_f[:, 0:512], c_ones1, p2_rdr,
                                       start=True, stop=True)
                        mark(s_t2, ins, f"p2_bc:{phi}", 1)

                    for h in range(NH):
                        par = h % 2
                        hi = qgi * NH + h
                        dwait(e, f"p2_q:{hi}")
                        dwait(e, f"p2_k:{hi}")
                        if h >= 1:
                            emit_bc(hi - 1)
                        for c in range(nblk):
                            b = c % 4
                            if hi == 0 and c < 4:
                                e.wait_ge(s_c1.h, at("p1_evacdone:1:2:3:3"))
                            elif c % 4 == 0 and c >= 4:
                                e.wait_ge(s_c2.h, at(f"p2_ex:{hi}:{c - 1}"))
                            if h == 0 and c == 0 and qgi >= 1:
                                e.wait_ge(s_v2.h, at(f"p2_r1:{qgi * 4 - 1}"))
                            ins = e.matmul(
                                ps_big[:, 512 * b:512 * b + 512],
                                p2_k[par][:, keys[c]:keys[c] + 128],
                                p2_q[par][:, 0:512],
                                start=True, stop=True, skip_group_check=True)
                            mark(s_t2, ins, f"p2_sc:{hi}:{c}", 1)
                        if any(mmap):
                            e.wait_ge(s_v2.h, at(f"p2_mk:{hi}"))
                        if hi >= 1:
                            e.wait_ge(s_c2.h, at(f"p2_bcev:{hi - 1}"))
                        e.wait_ge(s_c2.h, at(f"p2_ex:{hi}:{max(nblk - 5, 0)}"))
                        for c in range(nblk):
                            if c == nblk - 4:
                                e.wait_ge(s_c2.h,
                                          at(f"p2_ex:{hi}:{nblk - 1}"))
                            ins = e.matmul(ps_f[0:1, 0:512], c_onesd,
                                           p2_pt[:, 512 * c:512 * c + 512],
                                           start=(c == 0),
                                           stop=(c == nblk - 1),
                                           skip_group_check=True)
                        mark(s_t2, ins, f"p2_den:{hi}", 1)
                        dwait(e, f"p2_v:{hi}")
                        if hi >= 1:
                            e.wait_ge(s_v2.h, at(f"p2_pvev:{hi - 1}"))
                        for c in range(nblk):
                            vc = c if g <= 1 else (0 if c == 0 else
                                                   4 * (g - 1) + c - 1)
                            ins = e.matmul(ps_e[:],
                                           p2_v[par][:, 128 * vc:128 * vc + 128],
                                           p2_pt[:, 512 * c:512 * c + 512],
                                           start=(c == 0), stop=(c == nblk - 1))
                        mark(s_t2, ins, f"p2_pv:{hi}", 1)
                    emit_bc(qgi * NH + NH - 1)
                    e.wait_ge(s_c2.h,
                              at(f"p2_ex:{qgi * NH + NH - 1}:{nblk - 1}"))
                    e.wait_ge(s_v2.h, at(f"p2_pvev:{qgi * NH + NH - 1}"))
                    if qgi >= 1:
                        e.wait_ge(s_v2.h, at(f"p2_r1:{qgi * 4 - 1}"))
                    for og in range(4):
                        par = og % 2
                        dwait(e, f"p2_wo:{qgi}:{og}:15")
                        if og >= 1:
                            e.wait_ge(s_c2.h, at(f"p2_woev:{qgi}:{og - 1}"))
                        for dc in range(16):
                            for ol in range(4):
                                ins = e.matmul(
                                    QUAR(ol),
                                    p1_wog[par][:, 512 * dc + 128 * ol:
                                                512 * dc + 128 * ol + 128],
                                    p2_afm[:, 512 * dc:512 * dc + 512],
                                    start=(dc == 0), stop=(dc == 15))
                        mark(s_t2, ins, f"p2_womm:{qgi}:{og}", 1)
                    for tc in range(4):
                        n = qgi * 4 + tc
                        e.wait_ge(s_c2.h, at(f"p2_woev:{qgi}:3"))
                        if tc >= 1:
                            e.wait_ge(s_v2.h, at(f"p2_r1:{n - 1}"))
                        for oc in range(16):
                            ins = e.matmul(ps_big[:, 128 * oc:128 * oc + 128],
                                           p2_wof[:, 512 * oc + 128 * tc:
                                                  512 * oc + 128 * tc + 128],
                                           c_id16, start=True, stop=True,
                                           skip_group_check=True)
                        mark(s_t2, ins, f"p2_wot:{n}", 1)
                        e.wait_ge(s_c2.h, at(f"p2_x1b:{n}"))
                        if n == 0:
                            e.wait_ge(s_c1.h, at("p1_vevdone:1:3:3"))
                        for dc in range(16):
                            grp = dc // 4
                            dstp = (ps_c, ps_d)[grp % 2]
                            if dc % 4 == 0 and grp >= 2:
                                e.wait_ge(s_c2.h, at(f"p2_x1ev:{n}:{grp - 2}"))
                            ins = e.matmul(
                                dstp[:, 128 * (dc % 4):128 * (dc % 4) + 128],
                                p2_x1b[:, 128 * dc:128 * dc + 128],
                                c_id16, start=True, stop=True,
                                skip_group_check=True)
                            if dc % 4 == 3:
                                mark(s_t2, ins, f"p2_x1tr:{n}:{grp}", 1)
            if upto < 3:
                return

            for tt in range(NTT):
                dwait(e, f"p3_x1j:{tt}:15")
                for usb in range(22):
                    par = usb % 2
                    dwait(e, f"p3_wg:{tt}:{usb}:1")
                    dwait(e, f"p3_wu:{tt}:{usb}:1")
                    for ol in range(2):
                        isl = 2 * usb + ol
                        gpar = isl % 2
                        gdst = ps_c[:] if gpar == 0 else ps_big[:, 0:512]
                        udst = ps_d[:] if gpar == 0 else ps_big[:, 512:1024]
                        if isl >= 2:
                            e.wait_ge(s_c3.h, at(f"p3_silu:{tt}:{isl - 2}"))
                            e.wait_ge(s_v3.h, at(f"p3_h:{tt}:{isl - 2}"))
                        elif tt >= 1:
                            e.wait_ge(s_c3.h, at(f"p3_ytevdone:{tt - 1}"))
                            e.wait_ge(s_c3.h, at(f"p3_silu:{tt - 1}:{isl + 42}"))
                            e.wait_ge(s_v3.h, at(f"p3_h:{tt - 1}:{isl + 42}"))
                        else:
                            e.wait_ge(s_c2.h, at("p2_lastx1ev"))
                        for kk in range(16):
                            cc, j = divmod(kk, 8)
                            ins = e.matmul(
                                gdst,
                                p3_wg[par][:, 2048 * cc + 256 * j + 128 * ol:
                                           2048 * cc + 256 * j + 128 * ol + 128],
                                p3_x1j[:, 512 * kk:512 * kk + 512],
                                start=(kk == 0), stop=(kk == 15))
                        mark(s_t3, ins, f"p3_gmm:{tt}:{isl}", 1)
                        for kk in range(16):
                            cc, j = divmod(kk, 8)
                            ins = e.matmul(
                                udst,
                                p3_wu[par][:, 2048 * cc + 256 * j + 128 * ol:
                                           2048 * cc + 256 * j + 128 * ol + 128],
                                p3_x1j[:, 512 * kk:512 * kk + 512],
                                start=(kk == 0), stop=(kk == 15))
                        mark(s_t3, ins, f"p3_umm:{tt}:{isl}", 1)
                    M2[f"p3_gumm:{tt}:{usb}"] = s_t3.n
                M2[f"p3_gudone:{tt}"] = s_t3.n
                for ds in range(16):
                    ob, r = divmod(ds, 4)
                    ocp, half = divmod(r, 2)
                    par = ds % 2
                    dwait(e, f"p3_wd:{tt}:{ds}:1")
                    if half == 0:
                        e.wait_ge(s_v3.h, at(f"p3_h:{tt}:43"))
                        pb = 4 * ob + 2 * ocp
                        if pb >= 2:
                            e.wait_ge(s_c3.h, at(f"p3_yev:{tt}:{pb - 1}"))
                        elif tt >= 1:
                            e.wait_ge(s_c3.h, at(f"p3_yev:{tt - 1}:{pb + 15}"))
                    for oc in range(2):
                        ydst = (ps_e, ps_f)[oc]
                        for it in range(22):
                            ins = e.matmul(
                                ydst[:],
                                p3_wd[par][:, 256 * it + 128 * oc:
                                           256 * it + 128 * oc + 128],
                                p3_h[:, 512 * (22 * half + it):
                                     512 * (22 * half + it) + 512],
                                start=(half == 0 and it == 0),
                                stop=(half == 1 and it == 21))
                        if half == 1:
                            mark(s_t3, ins,
                                 f"p3_ymm:{tt}:{4 * ob + 2 * ocp + oc}", 1)
                        elif oc == 1:
                            mark(s_t3, ins, f"p3_dmm:{tt}:{ds}", 1)
                    if half == 1:
                        M2[f"p3_dmm:{tt}:{ds}"] = s_t3.n
                for tc in range(4):
                    n = tt * 4 + tc
                    e.wait_ge(s_c3.h, at(f"p3_yev:{tt}:15"))
                    if tc >= 1:
                        e.wait_ge(s_c3.h, at(f"p3_ytev:{n - 1}:3"))
                    for oc in range(16):
                        grp = oc // 4
                        dstp = (ps_c, ps_d)[grp % 2]
                        if oc % 4 == 0 and grp >= 2:
                            e.wait_ge(s_c3.h, at(f"p3_ytev:{n}:{grp - 2}"))
                        ins = e.matmul(
                            dstp[:, 128 * (oc % 4):128 * (oc % 4) + 128],
                            p3_yf[:, 512 * oc + 128 * tc:512 * oc + 128 * tc + 128],
                            c_id16, start=True, stop=True,
                            skip_group_check=True)
                        if oc % 4 == 3:
                            mark(s_t3, ins, f"p3_yt:{n}:{grp}", 1)

        @blk.scalar
        def _(e):


            dwait(e, "cst")
            e.mul(c_bqs, c_bias["q"], QSCALE)
            e.drain()

            def emit_xq(s, wi, og, tq):
                xpar = tq % 2
                if wi == 0 and og == 0:
                    dwait(e, f"p1a_st:{s}:{tq}:15")
                for dc in range(16):
                    dmark(e.dma_start(
                        out=p1_xq[xpar][:, 512 * dc:512 * dc + 512],
                        in_=s_xfm[s, 128 * dc:128 * dc + 128,
                                  512 * tq:512 * tq + 512]),
                        f"p1_xq:{s}:{wi}:{og}:{tq}:{dc}")

            for s in range(SPC):
                for tt in range(16):
                    n = s * 16 + tt
                    dwait(e, f"p1a_x:{n}")
                    if n >= 2:
                        e.wait_ge(s_t1.h, at(f"p1a_mm:{n - 2}"))
                    mark(s_c1, e.activation(p1_xb[n % 2], p1_x32[n % 2],
                                            AF.Copy), f"p1a_cast:{n}", 1)
                    e.wait_ge(s_t1.h, at(f"p1a_mm:{n}"))
                    gq = n // 4
                    if gq >= 1:
                        dwait(e, f"p1a_st:{(gq - 1) // 4}:{(gq - 1) % 4}:15")
                    mark(s_c1, e.activation(
                        p1_stx[:, 2048 * (tt % 4):2048 * (tt % 4) + 2048],
                        ps_big[:], AF.Copy), f"p1a_evac:{n}", 1)
                emit_xq(s, 0, 0, 0)
                emit_xq(s, 0, 0, 1)
                for wi, w in enumerate("qkv"):
                    for og in range(4):
                        for tq in range(4):
                            e.wait_ge(s_t1.h, at(f"p1_mm:{s}:{wi}:{og}:{tq}"))
                            for ol in range(4):
                                src = (QUAR(ol) if tq % 2 == 0 else SETB[ol][:])
                                oc = 4 * og + ol
                                if tq == 0:
                                    if w == "v":
                                        if og >= 1:
                                            e.wait_ge(s_t1.h,
                                                      at(f"p1_vtr:{s}:{og - 1}:{ol}:3"))
                                        elif s == 1:
                                            e.wait_ge(s_t1.h,
                                                      at(f"p1_vtr:0:3:{ol}:3"))
                                    else:
                                        po = og - 1
                                        pwi, psx = wi, s
                                        if po < 0:
                                            pwi -= 1
                                            po = 3
                                        if pwi < 0:
                                            psx -= 1
                                            pwi = 1
                                        if psx >= 0:
                                            dwait(e, f"p1_st:{psx}:{pwi}:{4 * po + 3}")
                                dst = (p1_vfm if w == "v" else p1_stg)
                                if (w == "v" and s == 0 and og == 0
                                        and tq == 0 and ol == 0):
                                    dwait(e, "p0b:done")
                                if w == "q":
                                    ins = e.activation(
                                        dst[:, 2048 * ol + 512 * tq:
                                            2048 * ol + 512 * tq + 512],
                                        src, AF.Identity,
                                        bias=c_bqs[:, oc:oc + 1], scale=QSCALE)
                                else:
                                    ins = e.activation(
                                        dst[:, 2048 * ol + 512 * tq:
                                            2048 * ol + 512 * tq + 512],
                                        src, AF.Identity,
                                        bias=c_bias[w][:, oc:oc + 1], scale=1.0)
                                mark(s_c1, ins, f"p1_evac:{s}:{wi}:{og}:{tq}:{ol}", 1)
                            M2[f"p1_evacdone:{s}:{wi}:{og}:{tq}"] = s_c1.n
                            u2 = (wi * 4 + og) * 4 + tq + 2
                            if u2 < 48:
                                emit_xq(s, u2 // 16, (u2 // 4) % 4, u2 % 4)
                        if w == "v":
                            for ol in range(4):
                                oc = 4 * og + ol
                                if oc >= 1:
                                    dwait(e, f"p1_stv:{s}:{oc - 1}")
                                elif s == 1:
                                    dwait(e, "p1_stv:0:15")
                                for grp in range(4):
                                    e.wait_ge(s_t1.h, at(f"p1_vtr:{s}:{og}:{ol}:{grp}"))
                                    dstp = (ps_c, ps_d)[grp % 2]
                                    mark(s_c1, e.activation(
                                        p1_stvb[oc % 2][:, 512 * grp:
                                                        512 * grp + 512],
                                        dstp[:], AF.Copy),
                                        f"p1_vev:{s}:{og}:{ol}:{grp}", 1)
                                M2[f"p1_vevdone:{s}:{og}:{ol}"] = s_c1.n
            if upto < 2:
                return

            for s in range(SPC):
                for g in range(4):
                    nblk, keys, qts = QG_PLANS[g]
                    qgi = s * 4 + g
                    for h in range(NH):
                        hi = qgi * NH + h
                        for c in range(nblk):
                            e.wait_ge(s_t2.h, at(f"p2_sc:{hi}:{c}"))
                            b = c % 4
                            le = 4
                            if g == 1 and c >= 1:
                                le = min(c + 1, 4)
                            elif g >= 2 and c >= 1:
                                le = min(c, 4)
                            ins = e.activation(
                                p2_pt[:, 512 * c:512 * c + 128 * le],
                                ps_big[:, 512 * b:512 * b + 128 * le],
                                AF.Exp, bias=c_neg8, scale=1.0)
                            if le < 4:
                                ins = e.activation(
                                    p2_pt[:, 512 * c + 128 * le:512 * c + 512],
                                    ps_big[:, 512 * b + 128 * le:512 * b + 512],
                                    AF.Copy, scale=0.0)
                            mark(s_c2, ins, f"p2_ex:{hi}:{c}", 1)
                        e.wait_ge(s_t2.h, at(f"p2_bc:{hi}"))
                        if hi >= 1:
                            e.wait_ge(s_v2.h, at(f"p2_pvev:{hi - 1}"))
                        mark(s_c2, e.activation(p2_bc, ps_f[:, 0:512], AF.Copy),
                             f"p2_bcev:{hi}", 1)
                    if qgi >= 1:
                        e.wait_ge(s_t2.h, at(f"p2_wot:{qgi * 4 - 1}"))
                    for og in range(4):
                        e.wait_ge(s_t2.h, at(f"p2_womm:{qgi}:{og}"))
                        for ol in range(4):
                            oc = 4 * og + ol
                            ins = e.activation(
                                p2_wof[:, 512 * oc:512 * oc + 512],
                                QUAR(ol), AF.Identity,
                                bias=c_bias["o"][:, oc:oc + 1], scale=1.0)
                        mark(s_c2, ins, f"p2_woev:{qgi}:{og}", 1)
                    for tc in range(4):
                        n = qgi * 4 + tc
                        e.wait_ge(s_v2.h, at(f"p2_r1:{n}"))
                        e.activation(p2_x1b, p2_r1, AF.Square,
                                     accum_out=st_ssq)
                        e.activation(p2_r1, p2_r1, AF.Identity,
                                     accum_out=st_sm)
                        e.drain()
                        e.activation(st_m2, st_sm, AF.Square, scale=1.0 / D)
                        e.drain()
                        e.activation(st_m2, st_m2, AF.Identity, scale=-1.0,
                                     bias=c_eps1[:, 0:1])
                        e.drain()
                        e.activation(st_rstd, st_ssq, AF.Ln, bias=st_m2,
                                     scale=1.0 / D)
                        e.drain()
                        e.activation(st_rstd, st_rstd, AF.Exp, scale=-0.5)
                        e.drain()
                        e.activation(st_rstdn, st_rstd, AF.Identity,
                                     scale=-1.0 / D)
                        e.drain()
                        e.activation(st_nmr, st_sm, AF.Identity,
                                     scale=st_rstdn)
                        e.drain()
                        mark(s_c2, e.activation(p2_x1b, p2_r1, AF.Identity,
                                                bias=st_nmr, scale=st_rstd),
                             f"p2_x1b:{n}", 1)
                        if n >= 1:
                            dwait(e, f"p2_stx1t:{n - 1}")
                        e.drain()
                        mark(s_c2, e.activation(p2_r1, p2_r1, AF.Identity,
                                                bias=st_nmr, scale=st_rstd),
                             f"p2_x1tt:{n}", 1)
                        if n >= 1:
                            dwait(e, f"p2_stx1f:{n - 1}")
                        for grp in range(4):
                            e.wait_ge(s_t2.h, at(f"p2_x1tr:{n}:{grp}"))
                            dstp = (ps_c, ps_d)[grp % 2]
                            mark(s_c2, e.activation(
                                p2_stx1[:, 512 * grp:512 * grp + 512],
                                dstp[:], AF.Copy), f"p2_x1ev:{n}:{grp}", 1)
                        M2[f"p2_x1f:{n}"] = s_c2.n
                    M2[f"p2_x1evdone:{qgi}"] = s_c2.n
            M2["p2_lastx1ev"] = s_c2.n
            if upto < 3:
                return

            for tt in range(NTT):
                for isl in range(NIT):
                    e.wait_ge(s_t3.h, at(f"p3_gmm:{tt}:{isl}"))
                    gpar = isl % 2
                    gsrc = ps_c[:] if gpar == 0 else ps_big[:, 0:512]
                    if isl >= 1:
                        e.wait_ge(s_v3.h, at(f"p3_h:{tt}:{isl - 1}"))
                    mark(s_c3, e.activation(p3_sg, gsrc, AF.Silu),
                         f"p3_silu:{tt}:{isl}", 1)
                for o128 in range(16):
                    e.wait_ge(s_t3.h, at(f"p3_ymm:{tt}:{o128}"))
                    if tt >= 1 and o128 == 0:
                        e.wait_ge(s_t3.h, at(f"p3_yt:{tt * 4 - 1}:3"))
                    ysrc = (ps_e, ps_f)[o128 % 2]
                    mark(s_c3, e.activation(
                        p3_yf[:, 512 * o128:512 * o128 + 512], ysrc[:], AF.Copy),
                        f"p3_yev:{tt}:{o128}", 1)
                for tc in range(4):
                    n = tt * 4 + tc
                    e.drain()
                    for grp in range(4):
                        e.wait_ge(s_t3.h, at(f"p3_yt:{n}:{grp}"))
                        dstp = (ps_c, ps_d)[grp % 2]
                        mark(s_c3, e.activation(
                            p3_yt[:, 512 * grp:512 * grp + 512], dstp[:],
                            AF.Copy), f"p3_ytev:{n}:{grp}", 1)
                    if n >= 1:
                        dwait(e, f"out:{n - 1}")
                    e.drain()
                    e.activation(p3_out, p3_yt, AF.Square, accum_out=st_ssq)
                    e.drain()
                    e.activation(st_rstd, st_ssq, AF.Ln, bias=c_eps2,
                                 scale=1.0 / D)
                    e.drain()
                    mark(s_c3, e.activation(st_rstd, st_rstd, AF.Exp,
                                            scale=-0.5), f"p3_rms:{n}", 1)
                    e.wait_ge(s_v3.h, at(f"p3_r2:{n}"))
                    e.activation(p3_out, p3_yt, AF.Square, accum_out=st_ssq)
                    e.activation(p3_yt, p3_yt, AF.Identity, accum_out=st_sm)
                    e.drain()
                    e.activation(st_m2, st_sm, AF.Square, scale=1.0 / D)
                    e.drain()
                    e.activation(st_m2, st_m2, AF.Identity, scale=-1.0,
                                 bias=c_eps1[:, 0:1])
                    e.drain()
                    e.activation(st_rstd, st_ssq, AF.Ln, bias=st_m2,
                                 scale=1.0 / D)
                    e.drain()
                    e.activation(st_rstd, st_rstd, AF.Exp, scale=-0.5)
                    e.drain()
                    e.activation(st_rstdn, st_rstd, AF.Identity,
                                 scale=-1.0 / D)
                    e.drain()
                    e.activation(st_nmr, st_sm, AF.Identity, scale=st_rstdn)
                    e.drain()
                    mark(s_c3, e.activation(p3_out, p3_yt, AF.Identity,
                                            bias=st_nmr, scale=st_rstd),
                         f"p3_out:{n}", 1)
                M2[f"p3_ytevdone:{tt}"] = s_c3.n

        @blk.vector
        def _(e):
            def lut_pass(src_words, shift, rows, cols, half=False):
                r = rows
                e.tensor_scalar(dq_mi[r, :], src_words[r, cols], shift, 7,
                                ALU.logical_shift_right, ALU.bitwise_and)
                e.tensor_scalar(dq_si[r, :], src_words[r, cols], shift + 3, 1,
                                ALU.logical_shift_right, ALU.bitwise_and)
                e.drain()
                if half:
                    e.tensor_scalar(dq_r4[r, :], dq_mi[r, :], 1.0, -2.0,
                                    ALU.mult, ALU.add)
                    e.tensor_scalar(dq_r6[r, :], dq_mi[r, :], 2.0, -8.0,
                                    ALU.mult, ALU.add)
                    e.tensor_scalar(dq_m[r, :], dq_mi[r, :], 0.5, 0.0,
                                    ALU.mult, ALU.add)
                else:
                    e.tensor_scalar(dq_r4[r, :], dq_mi[r, :], 2.0, -4.0,
                                    ALU.mult, ALU.add)
                    e.tensor_scalar(dq_r6[r, :], dq_mi[r, :], 4.0, -16.0,
                                    ALU.mult, ALU.add)
                    e.tensor_copy(dq_m[r, :], dq_mi[r, :])
                e.tensor_scalar(dq_s[r, :], dq_si[r, :], -2.0, 1.0,
                                ALU.mult, ALU.add)
                e.drain()
                e.tensor_tensor(dq_r4[r, :], dq_r4[r, :], dq_r6[r, :], ALU.max)
                e.drain()
                e.tensor_tensor(dq_r4[r, :], dq_m[r, :], dq_r4[r, :], ALU.max)
                e.drain()
                e.tensor_tensor(dq_r4[r, :], dq_r4[r, :], dq_s[r, :], ALU.mult)
                e.drain()

            dq_jobs = []

            def _gu_job(n, j):
                def run():
                    par = n % 2
                    if j == 0:
                        dwait(e, f"p0_pk:{n}")
                        dwait(e, f"p0_sc:{n}")
                        if n >= 2:
                            dwait(e, f"p0_gu:{n - 2}")
                    lut_pass(dq_pk[par], 4 * j, slice(0, 128), slice(0, 512),
                             True)
                    ins = e.tensor_tensor(dq_out[par][:, 512 * j:512 * j + 512],
                                          dq_r4[:, 0:512], dq_sc[par], ALU.mult)
                    if j == 7:
                        mark(s_dq, ins, f"dqv:{n}", 1)
                return run

            def _gu_dn_bar():
                def run():
                    dwait(e, "p0gu:done")
                return run

            def _dn_job(t, q):
                def run():
                    a, j, mixed = DN_TILES[t]
                    par = t % 2
                    qpar = q % 2
                    if q == 0:
                        e.wait_ge(d_pk[par].h, at(f"p0_pkdone:{t}"))
                    dwait(e, f"p0_scd:{t}:{q}")
                    if 4 * t + q >= 2:
                        pt, pq = (t, q - 2) if q >= 2 else (t - 1, q + 2)
                        dwait(e, f"p0_dnst:{pt}:{pq}")
                    cols = slice(512 * q, 512 * q + 512)
                    if mixed:
                        lut_pass(dq_pk[par], 4 * j, slice(0, 64), cols, True)
                        lut_pass(dq_pk[par], 4 * (j + 1), slice(64, 128), cols,
                                 True)
                    else:
                        lut_pass(dq_pk[par], 4 * j, slice(0, 128), cols, True)
                    ins = e.tensor_tensor(
                        dq_out[par][:, 512 * qpar:512 * qpar + 512],
                        dq_r4[:, 0:512], dq_sc[qpar], ALU.mult)
                    mark(s_dq, ins, f"dqv:d{t}:{q}", 1)
                return run

            for n in range(NGU):
                for j in range(8):
                    _gu_job(n, j)()
            _gu_dn_bar()()
            for t in range(NIT):
                for q in range(4):
                    dq_jobs.append(_dn_job(t, q))
            dq_jobs.reverse()

            def dq_step(k):
                for _ in range(k):
                    if dq_jobs:
                        dq_jobs.pop()()

            def dq_drain():
                while dq_jobs:
                    dq_jobs.pop()()
                M2["dqv:all"] = s_dq.n

            if upto < 2:
                dq_drain()

            if upto >= 2:
                for s in range(SPC):
                    for g in range(4):
                        nblk, keys, qts = QG_PLANS[g]
                        qgi = s * 4 + g
                        zmap, mmap = CHUNK_MAPS[g]
                        for h in range(NH):
                            hi = qgi * NH + h
                            last = None
                            for c in range(nblk):
                                if not mmap[c]:
                                    continue
                                e.wait_ge(s_c2.h, at(f"p2_ex:{hi}:{c}"))
                                for (l, w) in mmap[c]:
                                    mt = {"tri": c_tri, "trisink": c_trisink,
                                          "sink": c_sink}[w]
                                    sub = p2_pt[:, 512 * c + 128 * l:
                                                512 * c + 128 * l + 128]
                                    last = e.tensor_tensor(sub, sub, mt,
                                                           ALU.mult)
                            if last is not None:
                                mark(s_v2, last, f"p2_mk:{hi}", 1)
                            e.wait_ge(s_t2.h, at(f"p2_den:{hi}"))
                            mark(s_v2, e.reciprocal(p2_rdr, ps_f[0:1, 0:512]),
                                 f"p2_rc:{hi}", 1)
                            e.wait_ge(s_t2.h, at(f"p2_pv:{hi}"))
                            e.wait_ge(s_c2.h, at(f"p2_bcev:{hi}"))
                            if qgi >= 1 and h == 0:
                                e.wait_ge(s_t2.h, at(f"p2_womm:{qgi - 1}:3"))
                            mark(s_v2, e.tensor_tensor(
                                p2_afm[:, 512 * h:512 * h + 512],
                                ps_e[:], p2_bc, ALU.mult),
                                f"p2_pvev:{hi}", 1)
                            dq_step(1)
                        for tc in range(4):
                            n = qgi * 4 + tc
                            e.wait_ge(s_t2.h, at(f"p2_wot:{n}"))
                            dwait(e, f"p2_x:{qgi}:{tc}")
                            if n >= 1:
                                dwait(e, f"p2_stx1t:{n - 1}")
                            mark(s_v2, e.tensor_tensor(p2_r1, ps_big[:],
                                                       p2_x32[tc % 2],
                                                       ALU.add),
                                 f"p2_r1:{n}", 1)
                            dq_step(2)
                dq_drain()

            if upto >= 3:
                for tt in range(NTT):
                    for isl in range(NIT):
                        e.wait_ge(s_c3.h, at(f"p3_silu:{tt}:{isl}"))
                        e.wait_ge(s_t3.h, at(f"p3_umm:{tt}:{isl}"))
                        if isl == 0 and tt >= 1:
                            e.wait_ge(s_t3.h, at(f"p3_ymm:{tt - 1}:15"))
                        gpar = isl % 2
                        usrc = ps_d[:] if gpar == 0 else ps_big[:, 512:1024]
                        mark(s_v3, e.tensor_tensor(
                            p3_h[:, 512 * isl:512 * isl + 512],
                            p3_sg, usrc, ALU.mult), f"p3_h:{tt}:{isl}", 1)
                    for tc in range(4):
                        n = tt * 4 + tc
                        e.wait_ge(s_c3.h, at(f"p3_rms:{n}"))
                        dwait(e, f"p3_x1r:{n}")
                        mark(s_v3, e.scalar_tensor_tensor(
                            p3_yt, p3_yt, st_rstd, p3_x1r,
                            ALU.mult, ALU.add), f"p3_r2:{n}", 1)

    return nc, M2



def _perm():
    k = np.arange(FI)
    return 8 * (k % DWRD) + (k // DWRD)


def _host_prep(inputs):
    f32 = np.float32
    x = np.asarray(inputs["L0_attn_hidden_states"], dtype=f32).reshape(
        BATCH * SEQ, D)
    h = {}
    for w, key in (("q", "L0_attn_Wq"), ("k", "L0_attn_Wk"),
                   ("v", "L0_attn_Wv"), ("o", "L0_attn_Wo")):
        h[f"w{w}t"] = np.ascontiguousarray(np.asarray(inputs[key], dtype=f32).T)
    for w, key in (("q", "L0_attn_bq"), ("k", "L0_attn_bk"),
                   ("v", "L0_attn_bv"), ("o", "L0_attn_bo")):
        h[f"b{w}"] = np.ascontiguousarray(
            np.asarray(inputs[key], dtype=f32).reshape(16, 128).T)

    perm = _perm()
    gup = np.asarray(inputs["L0_ffn_gate_up_packed"])
    gp = np.empty_like(gup)
    gp[:FI] = gup[:FI][perm]
    gp[FI:] = gup[FI:][perm]
    h["gut"] = np.ascontiguousarray(gp.T)
    gs = np.asarray(inputs["L0_ffn_gate_up_scales"])
    gsp = np.empty_like(gs)
    gsp[:, :FI] = gs[:, :FI][:, perm]
    gsp[:, FI:] = gs[:, FI:][:, perm]
    g16 = gsp.astype(np.float16) if gsp.dtype != np.float16 else gsp
    sgu = np.empty((2, NOB, 128, 512), np.float16)
    rows = np.arange(128) // 16
    for cc in range(2):
        for ob in range(NOB):
            sgu[cc, ob] = g16[8 * cc + rows, 512 * ob:512 * ob + 512]
    h["sgu"] = sgu
    h["dnt"] = np.ascontiguousarray(np.asarray(inputs["L0_ffn_down_packed"]).T)
    dsc16 = np.asarray(inputs["L0_ffn_down_scales"]).astype(np.float16)
    pos = np.arange(NIT * 128)
    grp = (8 * (pos % DWRD) + pos // DWRD) // 128
    h["dnsb"] = np.ascontiguousarray(dsc16[grp].reshape(NIT, 128, D))

    h["idm"] = np.eye(128, dtype=f32)
    r = np.arange(128)
    c = np.arange(128)
    h["mT_tri"] = np.ascontiguousarray(
        (c[None, :] >= r[:, None]).astype(f32).T)
    h["mT_trisink"] = np.ascontiguousarray(
        ((c[None, :] < 4) | (c[None, :] >= r[:, None])).astype(f32).T)
    h["mT_sink"] = np.ascontiguousarray(
        np.broadcast_to((c < 4).astype(f32), (128, 128)).T.copy())
    h["ones1"] = np.ones((1, 128), dtype=f32)
    h["onesd"] = np.ones((128, 1), dtype=f32)
    h["negsh"] = np.full((128, 1), -SHIFT, dtype=f32)
    h["eps1"] = np.full((128, 1), LN_EPS, dtype=f32)
    h["eps2"] = np.full((128, 1), RMS_EPS, dtype=f32)

    return x, h


_CACHED = {}


def _get_nc(dbg=(), upto=3):
    key = (tuple(sorted(dbg)), upto)
    if key not in _CACHED:
        _, M = build_program({}, dbg=dbg, upto=upto)
        nc, _ = build_program(M, dbg=dbg, upto=upto)
        _CACHED[key] = nc
    return _CACHED[key]


def _make_in_maps(inputs):
    x, shared = _host_prep(inputs)
    in_maps = []
    for c in range(NCORES):
        m = dict(shared)
        m["x"] = np.ascontiguousarray(x[c * TOK:(c + 1) * TOK])
        in_maps.append(m)
    return in_maps


def kernel(**inputs):
    nc = _get_nc()
    in_maps = _make_in_maps(inputs)
    res = run_bass_kernel_spmd(nc, in_maps, core_ids=list(range(NCORES)))
    out = np.concatenate([r["y"] for r in res.results], axis=0)
    return out.reshape(BATCH, SEQ, D)


# version 20
# speedup vs baseline: 1.1489x; 1.1143x over previous
from contextlib import ExitStack

import numpy as np
import ml_dtypes

import concourse.bass as bass
import concourse.mybir as mybir
from concourse.bass_utils import run_bass_kernel_spmd

F32 = mybir.dt.float32
BF16 = mybir.dt.bfloat16
F16 = mybir.dt.float16
I32 = mybir.dt.int32
AF = mybir.ActivationFunctionType
ALU = mybir.AluOpType

D = 2048
NH = 16
HD = 128
SEQ = 2048
BATCH = 16
NCORES = 8
SPC = 2
TOK = SPC * SEQ
QSCALE = float(HD) ** -0.5
SHIFT = 8.0

FI = 5632
GU = 2 * FI
GW = 256
DWRD = 704
NIT = 44
NOB = 22
NGU = 2 * NOB
TTOK = 512
NTT = TOK // TTOK

LN_EPS = 1e-5
RMS_EPS = 1e-6

DN_TILES = []
for _t in range(NIT):
    _a, _j = (128 * _t) % DWRD, (128 * _t) // DWRD
    DN_TILES.append((_a, _j, (128 * _t + 127) // DWRD != _j))


def _qg_plan(g):
    if g <= 1:
        nblk = 16
        keys = [128 * c for c in range(nblk)]
    else:
        kfg = 4 * (g - 1)
        nblk = 16 - kfg + 1
        keys = [0] + [128 * (kfg + c - 1) for c in range(1, nblk)]
    qts = []
    for l in range(4):
        masks, zeros = [], []
        if g == 0:
            accum = (0, nblk)
            runs = [(0, nblk)]
        elif g == 1:
            if l == 0:
                masks = [(0, "trisink")]
                accum = (1, nblk)
                runs = [(0, nblk)]
            else:
                masks = [(0, "sink"), (l, "tri")]
                zeros = list(range(1, l))
                accum = (l + 1, nblk)
                runs = [(0, 1), (l, nblk)]
        else:
            masks = [(0, "sink"), (l + 1, "tri")]
            zeros = list(range(1, l + 1))
            accum = (l + 2, nblk)
            runs = [(0, 1), (l + 1, nblk)]
        mms = []
        for (c0, c1) in runs:
            c = c0
            while c < c1:
                ce = min(c1, (c // 4 + 1) * 4)
                mms.append((c, ce))
                c = ce
        qts.append(dict(masks=masks, zeros=zeros, accum=accum, mms=mms))
    return nblk, keys, qts


QG_PLANS = [_qg_plan(g) for g in range(4)]
NGRP = [(QG_PLANS[g][0] + 3) // 4 for g in range(4)]


def _chunk_maps(g):
    nblk, keys, qts = QG_PLANS[g]
    zmap = [[] for _ in range(nblk)]
    mmap = [[] for _ in range(nblk)]
    for l in range(4):
        for c in qts[l]["zeros"]:
            zmap[c].append(l)
        for (c, w) in qts[l]["masks"]:
            mmap[c].append((l, w))
    return zmap, mmap


CHUNK_MAPS = [_chunk_maps(g) for g in range(4)]


def _lastgrp(qi):
    g = (qi // 64) % 4
    return NGRP[g] - 1


def build_program(M, dbg=(), upto=3):
    M2 = {}

    def at(tag):
        return M.get(tag, 0)

    nc = bass.Bass("TRN2", target_bir_lowering=False, debug=False)

    def din(name, shape, dtype):
        return nc.dram_tensor(name, shape, dtype, kind="ExternalInput")

    def scratch(name, shape, dtype):
        kind = "ExternalOutput" if name in dbg else "Internal"
        return nc.dram_tensor(name, shape, dtype, kind=kind)

    x_in = din("x", [TOK, D], F32)
    w_t = {w: din(f"w{w}t", [D, D], F32) for w in "qkvo"}
    b_in = {w: din(f"b{w}", [128, 16], F32) for w in "qkvo"}
    gut = din("gut", [GW, GU], I32)
    sgu = din("sgu", [2, NOB, 128, 512], F16)
    dnt = din("dnt", [DWRD, D], I32)
    dnsb = din("dnsb", [NIT, 128, 2048], F16)
    idm = din("idm", [128, 128], F32)
    m_tri = din("mT_tri", [128, 128], F32)
    m_trisink = din("mT_trisink", [128, 128], F32)
    m_sink = din("mT_sink", [128, 128], F32)
    ones1 = din("ones1", [1, 128], F32)
    onesd = din("onesd", [128, 1], F32)
    negsh = din("negsh", [128, 1], F32)
    eps1 = din("eps1", [128, 1], F32)
    eps2 = din("eps2", [128, 1], F32)

    y_out = nc.dram_tensor("y", [TOK, D], F32, kind="ExternalOutput")

    s_wbf = {w: scratch(f"s_w{w}", [D, D], BF16) for w in "qkvo"}
    s_xfm = scratch("s_xfm", [SPC, D, SEQ], BF16)
    s_q = scratch("s_q", [SPC, D, SEQ], BF16)
    s_k = scratch("s_k", [SPC, D, SEQ], BF16)
    s_v = scratch("s_v", [SPC, NH, 128, SEQ], BF16)
    s_wgu = scratch("s_wgu", [2, NOB, 8, 128, 512], BF16)
    s_wdn = scratch("s_wdn", [NIT, 128, 2048], BF16)
    s_x1t = scratch("s_x1t", [TOK, D], F32)
    s_x1f = scratch("s_x1f", [D, TOK], BF16)

    with ExitStack() as ctx:
        AR_BYTES = 204 * 1024
        arena_h = ctx.enter_context(
            nc.sbuf_tensor("arena", [128, AR_BYTES // 2], BF16))
        ARENA_ADDR = nc.lookup_mloc(arena_h).addr
        ESZ = {F32: 4, I32: 4, BF16: 2, F16: 2}

        class Cur:
            def __init__(self, base):
                self.o = base

            def take(self, nbytes, align=64):
                self.o = (self.o + align - 1) // align * align
                r = self.o
                self.o += nbytes
                return r

        _tn = [0]

        def tile(cur, rows, cols, dtype):
            esz = ESZ[dtype]
            off = cur.take(cols * esz)
            _tn[0] += 1
            h = nc.alloc_sbuf_tensor_at(f"tl{_tn[0]}", [rows, cols], dtype,
                                        offset=ARENA_ADDR + off)
            return h[:, :]

        def tile_at(byte_off, rows, cols, dtype):
            _tn[0] += 1
            h = nc.alloc_sbuf_tensor_at(f"tl{_tn[0]}", [rows, cols], dtype,
                                        offset=ARENA_ADDR + byte_off)
            return h[:, :]

        R0 = Cur(0)
        c_id32 = tile(R0, 128, 128, F32)
        c_id16 = tile(R0, 128, 128, BF16)
        c_tri = tile(R0, 128, 128, BF16)
        c_trisink = tile(R0, 128, 128, BF16)
        c_sink = tile(R0, 128, 128, BF16)
        c_ones1 = tile(R0, 1, 128, F32)
        c_onesd = tile(R0, 128, 1, BF16)
        c_neg8 = tile(R0, 128, 1, F32)
        c_eps1 = tile(R0, 128, 1, F32)
        c_eps2 = tile(R0, 128, 1, F32)
        c_bias = {w: tile(R0, 128, 16, F32) for w in "qkvo"}
        c_bqs = tile(R0, 128, 16, F32)
        st_bn = tile(R0, 128, 24, F32)
        st_ag = tile(R0, 128, 2, F32)
        st_var = tile(R0, 128, 1, F32)
        st_rstd = tile(R0, 128, 1, F32)
        st_nmr = tile(R0, 128, 1, F32)
        st_sm = tile(R0, 128, 1, F32)
        st_ssq = tile(R0, 128, 1, F32)
        st_m2 = tile(R0, 128, 1, F32)
        st_rstdn = tile(R0, 128, 1, F32)
        p2_dn = tile(R0, 128, 4, F32)
        p2_kp = tile(R0, 128, 2, F32)
        p2_rd = tile(R0, 128, 1, F32)
        p2_rdr = tile(R0, 1, 512, F32)
        p2_bc = tile(R0, 128, 512, F32)
        assert R0.o <= 15 * 1024, R0.o

        RD_BASE = 15 * 1024
        RD = Cur(RD_BASE)
        dq_pk = [tile(RD, 128, 2048, I32) for _ in range(2)]
        dq_sc = [tile(RD, 128, 512, F32) for _ in range(2)]
        dq_out = [tile(RD, 128, 4096, BF16) for _ in range(2)]
        dq_mi = tile(RD, 128, 512, I32)
        dq_si = tile(RD, 128, 512, I32)
        dq_m = tile(RD, 128, 512, F32)
        dq_s = tile(RD, 128, 512, F32)
        dq_r4 = tile(RD, 128, 512, F32)
        dq_r6 = tile(RD, 128, 512, F32)
        assert RD.o <= 64 * 1024, RD.o
        RDP3 = Cur(RD_BASE)
        p3_wd = [tile(RDP3, 128, 22 * 256, BF16) for _ in range(2)]
        assert RDP3.o <= 64 * 1024, RDP3.o

        RP_BASE = 64 * 1024
        RP = Cur(RP_BASE)
        p1_wog = [tile(RP, 128, 16 * 512, BF16) for _ in range(2)]
        WOG_END = RP.o
        p1_x32 = [tile(RP, 128, 2048, F32) for _ in range(2)]
        p1_xb = [tile(RP, 128, 2048, BF16) for _ in range(2)]
        STX_OFF = (RP.o + 63) // 64 * 64
        p1_stx = tile(RP, 128, 4 * 2048, BF16)
        XQ_OFF = (RP.o + 63) // 64 * 64
        p1_xq = [tile(RP, 128, 16 * 512, BF16) for _ in range(2)]
        STG_OFF = (RP.o + 63) // 64 * 64
        p1_stg = tile(RP, 128, 4 * 2048, BF16)
        VFM_OFF = (RP.o + 63) // 64 * 64
        p1_vfm = tile(RP, 128, 4 * 2048, BF16)
        STV_OFF = (RP.o + 63) // 64 * 64
        p1_stv = tile(RP, 128, 2048, BF16)
        assert RP.o <= AR_BYTES, RP.o

        wb16 = [tile_at(VFM_OFF, 128, 2048, BF16),
                tile_at(VFM_OFF + 4096, 128, 2048, BF16)]

        RP2 = Cur(WOG_END)
        p2_q = [tile(RP2, 128, 512, BF16) for _ in range(2)]
        p2_k = [tile(RP2, 128, 2048, BF16) for _ in range(2)]
        p2_v = [tile(RP2, 128, 2048, BF16) for _ in range(2)]
        p2_pr = [tile(RP2, 128, 2048, BF16) for _ in range(2)]
        p2_pt = tile(RP2, 128, 16 * 512, BF16)
        p2_afm = tile(RP2, 128, 16 * 512, BF16)
        p2_wof = tile(RP2, 128, 16 * 512, BF16)
        p2_x32 = [tile(RP2, 128, 2048, F32) for _ in range(2)]
        p2_r1 = tile(RP2, 128, 2048, F32)
        p2_x1b = tile(RP2, 128, 2048, BF16)
        p2_stx1 = tile(RP2, 128, 2048, BF16)
        assert RP2.o <= AR_BYTES, RP2.o

        RP3 = Cur(RP_BASE)
        p3_x1j = tile(RP3, 128, 16 * 512, BF16)
        p3_wg = [tile(RP3, 128, 16 * 256, BF16) for _ in range(2)]
        p3_wu = [tile(RP3, 128, 16 * 256, BF16) for _ in range(2)]
        p3_sg = tile(RP3, 128, 512, BF16)
        p3_h = tile(RP3, 128, NIT * 512, BF16)
        p3_yf = tile(RP3, 128, 16 * 512, BF16)
        p3_yt = tile(RP3, 128, 2048, F32)
        p3_x1r = tile(RP3, 128, 2048, F32)
        p3_out = tile(RP3, 128, 2048, F32)
        assert RP3.o <= AR_BYTES, RP3.o

        psa = lambda n, s: ctx.enter_context(nc.psum_tensor(n, s, F32))
        ps_big = psa("ps_big", [128, 2048])
        ps_c = psa("ps_c", [128, 512])
        ps_d = psa("ps_d", [128, 512])
        ps_e = psa("ps_e", [128, 512])
        ps_f = psa("ps_f", [128, 512])
        QUAR = lambda i: ps_big[:, 512 * i:512 * i + 512]
        SETB = [ps_c, ps_d, ps_e, ps_f]

        def mksem(name):
            class Ct:
                __slots__ = ("h", "n", "name")
            c = Ct()
            c.h = ctx.enter_context(nc.semaphore(name))
            c.n = 0
            c.name = name
            return c

        def mark(c, instr, tag=None, k=1):
            instr.then_inc(c.h, k)
            c.n += k
            if tag is not None:
                M2[tag] = c.n
            return c.n

        s_pret = mksem("s_pret")
        s_prec = mksem("s_prec")
        s_c1 = mksem("s_c1")
        s_t1 = mksem("s_t1")
        s_t2 = mksem("s_t2")
        s_v2 = mksem("s_v2")
        s_c2 = mksem("s_c2")
        s_t3 = mksem("s_t3")
        s_v3 = mksem("s_v3")
        s_c3 = mksem("s_c3")
        s_dq = mksem("s_dq")

        d_cst = mksem("d_cst")
        d_gusb = [mksem(f"d_gusb{i}") for i in range(2)]
        d_indb = [mksem(f"d_indb{i}") for i in range(2)]
        d_pres = [mksem(f"d_pres{i}") for i in range(4)]
        d_presd = [mksem(f"d_presd{i}") for i in range(2)]
        d_p0bld = [mksem(f"d_p0bld{i}") for i in range(2)]
        d_p0bst = [mksem(f"d_p0bst{i}") for i in range(2)]
        d_p1ax = [mksem(f"d_p1ax{i}") for i in range(2)]
        d_p1ast = mksem("d_p1ast")
        d_p1w = [mksem(f"d_p1w{i}") for i in range(2)]
        d_p1xq = [mksem(f"d_p1xq{i}") for i in range(2)]
        d_p1st = mksem("d_p1st")
        d_p1stv = mksem("d_p1stv")
        d_p2k = [mksem(f"d_p2k{i}") for i in range(2)]
        d_p2v = [mksem(f"d_p2v{i}") for i in range(2)]
        d_p2q = [mksem(f"d_p2q{i}") for i in range(2)]
        d_p2wo = [mksem(f"d_p2wo{i}") for i in range(2)]
        d_p2x = [mksem(f"d_p2x{i}") for i in range(2)]
        d_p2s1t = mksem("d_p2s1t")
        d_p2s1f = mksem("d_p2s1f")
        d_p3x1j = mksem("d_p3x1j")
        d_p3wg = [mksem(f"d_p3wg{i}") for i in range(2)]
        d_p3wu = [mksem(f"d_p3wu{i}") for i in range(2)]
        d_p3wd = [mksem(f"d_p3wd{i}") for i in range(2)]
        d_p3x1r = mksem("d_p3x1r")
        d_out = mksem("d_out")
        d_poolc = mksem("d_poolc")
        d_pk = [mksem(f"d_pk{i}") for i in range(2)]
        d_psc = [mksem(f"d_psc{i}") for i in range(2)]
        d_gust = [mksem(f"d_gust{i}") for i in range(2)]
        d_dnst = [mksem(f"d_dnst{i}") for i in range(2)]

        def droute(tag):
            p = tag.split(":")
            f = p[0]
            if f == "cst":
                return d_cst
            if f == "gusb":
                return d_gusb[int(p[1]) % 2]
            if f == "indb":
                return d_indb[int(p[1]) % 2]
            if f == "pres":
                k = p[1]
                if k.startswith("d"):
                    return d_presd[int(k[1:]) % 2]
                return d_pres[int(k) % 4]
            if f == "p0b_ld":
                return d_p0bld[int(p[1]) % 2]
            if f == "p0b_st":
                return d_p0bst[int(p[1]) % 2]
            if f == "p1a_x":
                return d_p1ax[int(p[1]) % 2]
            if f == "p1a_st":
                return d_p1ast
            if f == "p1_w":
                return d_p1w[int(p[3]) % 2]
            if f == "p1_xq":
                return d_p1xq[int(p[4]) % 2]
            if f == "p1_st":
                return d_p1st
            if f == "p1_stv":
                return d_p1stv
            if f == "p2_k":
                return d_p2k[int(p[1]) % 2]
            if f == "p2_v":
                return d_p2v[int(p[1]) % 2]
            if f == "p2_q":
                return d_p2q[int(p[1]) % 2]
            if f == "p2_wo":
                return d_p2wo[int(p[2]) % 2]
            if f == "p2_x":
                return d_p2x[int(p[2]) % 2]
            if f == "p2_stx1t":
                return d_p2s1t
            if f == "p2_stx1f":
                return d_p2s1f
            if f == "p3_x1j":
                return d_p3x1j
            if f == "p3_wg":
                return d_p3wg[int(p[2]) % 2]
            if f == "p3_wu":
                return d_p3wu[int(p[2]) % 2]
            if f == "p3_wd":
                return d_p3wd[int(p[2]) % 2]
            if f == "p3_x1r":
                return d_p3x1r
            if f == "out":
                return d_out
            if f == "poolcst":
                return d_poolc
            if f in ("p0_pk", "p0_pkd"):
                return d_pk[int(p[1]) % 2]
            if f == "p0_sc":
                return d_psc[int(p[1]) % 2]
            if f == "p0_scd":
                return d_psc[int(p[2]) % 2]
            if f == "p0_gu":
                return d_gust[int(p[1]) % 2]
            if f == "p0_dnst":
                return d_dnst[int(p[2]) % 2]
            raise KeyError(tag)

        SPECIALS = {
            "pres:all": [(d_pres[i], f"pres:f{i}") for i in range(4)],
            "p0b:done": [(d_p0bst[0], "p0bst:f0"), (d_p0bst[1], "p0bst:f1")],
            "p1:done": [(d_p1st, "p1st:f"), (d_p1stv, "p1stv:f"),
                        (d_p1ast, "p1ast:f"),
                        (d_p0bst[0], "p0bst:f0"), (d_p0bst[1], "p0bst:f1")],
            "p2:done": [(d_p2s1t, "p2s1t:f"), (d_p2s1f, "p2s1f:f")],
            "p0gu:done": [(d_gust[0], "gust:f0"), (d_gust[1], "gust:f1")],
            "p0:done": [(d_gust[0], "gust:f0"), (d_gust[1], "gust:f1"),
                        (d_dnst[0], "dnst:f0"), (d_dnst[1], "dnst:f1")],
        }

        def dmark(instr, tag):
            return mark(droute(tag), instr, tag, 16)

        def dwait(e, tag):
            if tag in SPECIALS:
                for c, t in SPECIALS[tag]:
                    e.wait_ge(c.h, at(t))
            else:
                e.wait_ge(droute(tag).h, at(tag))

        blk = ctx.enter_context(nc.Block())

        @blk.sync
        def _(e):
            for dst_t, src_t in ((c_id32, idm), (c_ones1, ones1),
                                 (c_neg8, negsh),
                                 (c_eps1, eps1), (c_eps2, eps2)):
                dmark(e.dma_start(out=dst_t, in_=src_t[:]), "cst")
            for w in "qkvo":
                dmark(e.dma_start(out=c_bias[w], in_=b_in[w][:]), "cst")
            M2["cst"] = d_cst.n



            for s in range(SPC):
                for tt in range(16):
                    n = s * 16 + tt
                    if n >= 2:
                        e.wait_ge(s_c1.h, at(f"p1a_cast:{n - 2}"))
                    dmark(e.dma_start(out=p1_x32[n % 2],
                                      in_=x_in[2048 * s + 128 * tt:
                                               2048 * s + 128 * tt + 128, :]),
                          f"p1a_x:{n}")
                    if tt % 4 == 3:
                        tq = tt // 4
                        e.wait_ge(s_c1.h, at(f"p1a_evac:{n}"))
                        for dc in range(16):
                            src = p1_stx.rearrange("p (k c t) -> p k c t",
                                                   k=4, c=16)[:, :, dc, :]
                            dmark(e.dma_start(
                                out=s_xfm[s, 128 * dc:128 * dc + 128,
                                          512 * tq:512 * tq + 512], in_=src),
                                f"p1a_st:{s}:{tq}:{dc}")
                for wi, w in enumerate("qkv"):
                    if s == 1 and wi == 2:
                        e.wait_ge(s_t1.h, at("p1a_mm:31"))
                        for phi in range(2):
                            dmark(e.dma_start(
                                out=p2_k[phi],
                                in_=s_k[0, 128 * phi:128 * phi + 128, :]),
                                f"p2_k:{phi}")
                            dmark(e.dma_start(out=p2_v[phi], in_=s_v[0, phi]),
                                  f"p2_v:{phi}")
                            dmark(e.dma_start(out=p2_q[phi],
                                              in_=s_q[0, 128 * phi:
                                                      128 * phi + 128,
                                                      0:512]),
                                  f"p2_q:{phi}")
                    dwait(e, f"p0b_st:{16 * wi + 15}")
                    dwait(e, f"p0b_st:{16 * wi + 14}")
                    for og in range(4):
                        par = og % 2
                        u0 = (s * 3 + wi) * 4 + og
                        if u0 >= 2:
                            pu = u0 - 2
                            ps_, r = divmod(pu, 12)
                            pw, po = divmod(r, 4)
                            e.wait_ge(s_t1.h, at(f"p1_mm:{ps_}:{pw}:{po}:3"))
                        for dc in range(16):
                            dmark(e.dma_start(
                                out=p1_wog[par][:, 512 * dc:512 * dc + 512],
                                in_=s_wbf[w][128 * dc:128 * dc + 128,
                                             512 * og:512 * og + 512]),
                                f"p1_w:{s}:{wi}:{og}:{dc}")
                        for ol in range(4):
                            oc = 4 * og + ol
                            e.wait_ge(s_c1.h, at(f"p1_evac:{s}:{wi}:{og}:3:{ol}"))
                            dstt = {"q": s_q, "k": s_k}.get(w)
                            if dstt is not None:
                                dmark(e.dma_start(
                                    out=dstt[s, 128 * oc:128 * oc + 128, :],
                                    in_=p1_stg[:, 2048 * ol:2048 * ol + 2048]),
                                    f"p1_st:{s}:{wi}:{oc}")
                        if w == "v":
                            for ol in range(4):
                                oc = 4 * og + ol
                                e.wait_ge(s_c1.h, at(f"p1_vevdone:{s}:{og}:{ol}"))
                                dmark(e.dma_start(out=s_v[s, oc], in_=p1_stv),
                                      f"p1_stv:{s}:{oc}")
            M2["p1st:f"] = d_p1st.n
            M2["p1stv:f"] = d_p1stv.n
            M2["p1ast:f"] = d_p1ast.n
            if upto < 2:
                dwait(e, "p1:done")
                dwait(e, "p0b:done")
                dwait(e, "p0:done")
                return

            dwait(e, "p1:done")
            for s in range(SPC):
                for g in range(4):
                    qgi = s * 4 + g
                    for h in range(NH):
                        par = h % 2
                        hi = qgi * NH + h
                        if hi >= 2:
                            e.wait_ge(s_t2.h, at(f"p2_pv:{hi - 2}"))
                            dmark(e.dma_start(
                                out=p2_k[par],
                                in_=s_k[s, 128 * h:128 * h + 128, :]),
                                f"p2_k:{hi}")
                            dmark(e.dma_start(out=p2_v[par], in_=s_v[s, h]),
                                  f"p2_v:{hi}")
                            dmark(e.dma_start(out=p2_q[par],
                                              in_=s_q[s, 128 * h:
                                                      128 * h + 128,
                                                      512 * g:512 * g + 512]),
                                  f"p2_q:{hi}")
                    for og in range(4):
                        par = og % 2
                        if qgi == 0 and og < 2:
                            e.wait_ge(s_t1.h, at("p1_mm:1:2:3:3"))
                        if og >= 2:
                            e.wait_ge(s_t2.h, at(f"p2_womm:{qgi}:{og - 2}"))
                        elif qgi >= 1:
                            e.wait_ge(s_t2.h, at(f"p2_womm:{qgi - 1}:{og + 2}"))
                        for dc in range(16):
                            dmark(e.dma_start(
                                out=p1_wog[par][:, 512 * dc:512 * dc + 512],
                                in_=s_wbf["o"][128 * dc:128 * dc + 128,
                                               512 * og:512 * og + 512]),
                                f"p2_wo:{qgi}:{og}:{dc}")
                    def emit_x1store(pn):
                        pt0 = (2048 * (pn // 16) + 512 * ((pn // 4) % 4)
                               + 128 * (pn % 4))
                        e.wait_ge(s_c2.h, at(f"p2_x1tt:{pn}"))
                        dmark(e.dma_start(out=s_x1t[pt0:pt0 + 128, :],
                                          in_=p2_r1),
                              f"p2_stx1t:{pn}")
                        e.wait_ge(s_c2.h, at(f"p2_x1f:{pn}"))
                        dst = s_x1f.rearrange("(c p) t -> p c t", p=128)[
                            :, :, pt0:pt0 + 128]
                        dmark(e.dma_start(out=dst,
                                          in_=p2_stx1.rearrange(
                                              "p (c t) -> p c t", c=16)),
                              f"p2_stx1f:{pn}")

                    for tc in range(4):
                        n = qgi * 4 + tc
                        t0 = 2048 * s + 512 * g + 128 * tc
                        if n >= 1:
                            emit_x1store(n - 1)
                        if n >= 2:
                            e.wait_ge(s_v2.h, at(f"p2_r1:{n - 2}"))
                        dmark(e.dma_start(out=p2_x32[tc % 2],
                                          in_=x_in[t0:t0 + 128, :]),
                              f"p2_x:{qgi}:{tc}")
            emit_x1store(SPC * 16 - 1)
            M2["p2s1t:f"] = d_p2s1t.n
            M2["p2s1f:f"] = d_p2s1f.n
            if upto < 3:
                dwait(e, "p2:done")
                dwait(e, "p1:done")
                dwait(e, "p0:done")
                return

            dwait(e, "p2:done")
            for tt in range(NTT):
                if tt >= 1:
                    e.wait_ge(s_t3.h, at(f"p3_gudone:{tt - 1}"))
                for cc in range(2):
                    for j in range(8):
                        k = 8 * cc + j
                        src = s_x1f.rearrange("(w r) t -> w r t", r=8)[
                            128 * cc:128 * cc + 128, j,
                            TTOK * tt:TTOK * tt + TTOK]
                        dmark(e.dma_start(out=p3_x1j[:, 512 * k:512 * k + 512],
                                          in_=src), f"p3_x1j:{tt}:{k}")
                for usb in range(22):
                    gp, sbi = divmod(usb, 2)
                    par = usb % 2
                    if usb >= 2:
                        e.wait_ge(s_t3.h, at(f"p3_gumm:{tt}:{usb - 2}"))
                    elif tt >= 1:
                        e.wait_ge(s_t3.h, at(f"p3_gumm:{tt - 1}:{usb + 20}"))
                    dwait(e, "p0gu:done")
                    for cc in range(2):
                        srcg = s_wgu[cc, gp].rearrange("j p c -> p j c")[
                            :, :, 256 * sbi:256 * sbi + 256]
                        srcu = s_wgu[cc, 11 + gp].rearrange("j p c -> p j c")[
                            :, :, 256 * sbi:256 * sbi + 256]
                        dmark(e.dma_start(
                            out=p3_wg[par].rearrange("p (k c) -> p k c", c=256)[
                                :, 8 * cc:8 * cc + 8, :], in_=srcg),
                            f"p3_wg:{tt}:{usb}:{cc}")
                        dmark(e.dma_start(
                            out=p3_wu[par].rearrange("p (k c) -> p k c", c=256)[
                                :, 8 * cc:8 * cc + 8, :], in_=srcu),
                            f"p3_wu:{tt}:{usb}:{cc}")
                for ds in range(16):
                    ob, r = divmod(ds, 4)
                    ocp, half = divmod(r, 2)
                    par = ds % 2
                    if ds >= 2:
                        e.wait_ge(s_t3.h, at(f"p3_dmm:{tt}:{ds - 2}"))
                    elif tt >= 1:
                        e.wait_ge(s_t3.h, at(f"p3_dmm:{tt - 1}:{ds + 14}"))
                    dwait(e, "p0:done")
                    col0 = 512 * ob + 256 * ocp
                    for q in range(2):
                        src = s_wdn[22 * half + 11 * q:22 * half + 11 * q + 11,
                                    :, col0:col0 + 256].rearrange(
                                        "t p c -> p t c")
                        dmark(e.dma_start(
                            out=p3_wd[par].rearrange("p (i c) -> p i c", c=256)[
                                :, 11 * q:11 * q + 11, :], in_=src),
                            f"p3_wd:{tt}:{ds}:{q}")
                def emit_ostore(pn):
                    pt0 = TTOK * (pn // 4) + 128 * (pn % 4)
                    e.wait_ge(s_c3.h, at(f"p3_out:{pn}"))
                    dmark(e.dma_start(out=y_out[pt0:pt0 + 128, :], in_=p3_out),
                          f"out:{pn}")

                for tc in range(4):
                    n = tt * 4 + tc
                    t0 = TTOK * tt + 128 * tc
                    if n >= 1:
                        emit_ostore(n - 1)
                        e.wait_ge(s_v3.h, at(f"p3_r2:{n - 1}"))
                    dmark(e.dma_start(out=p3_x1r, in_=s_x1t[t0:t0 + 128, :]),
                          f"p3_x1r:{n}")
            emit_ostore(NTT * 4 - 1)
            e.wait_ge(d_out.h, d_out.n)

        @blk.gpsimd
        def _(e):
            e.dma_start(out=c_id16, in_=idm[:]).then_inc(d_poolc.h, 16)
            e.dma_start(out=c_tri, in_=m_tri[:]).then_inc(d_poolc.h, 16)
            e.dma_start(out=c_trisink, in_=m_trisink[:]).then_inc(d_poolc.h, 16)
            e.dma_start(out=c_onesd, in_=onesd[:]).then_inc(d_poolc.h, 16)
            d_poolc.n = 64
            dmark(e.dma_start(out=c_sink, in_=m_sink[:]), "poolcst")
            M2["poolcst"] = d_poolc.n

            def p0b_tile(n):
                wi_, dt = divmod(n, 16)
                w = "qkvo"[wi_]
                if n >= 2:
                    dwait(e, f"p0b_st:{n - 2}")
                dmark(e.dma_start(out=wb16[n % 2],
                                  in_=w_t[w][128 * dt:128 * dt + 128, :]),
                      f"p0b_ld:{n}")
                dwait(e, f"p0b_ld:{n}")
                dmark(e.dma_start(out=s_wbf[w][128 * dt:128 * dt + 128, :],
                                  in_=wb16[n % 2]), f"p0b_st:{n}")

            for n in range(48):
                p0b_tile(n)

            for n in range(NGU):
                cc, ob = divmod(n, NOB)
                par = n % 2
                for _w in range(4):
                    if 48 + 4 * n + _w < 64:
                        p0b_tile(48 + 4 * n + _w)
                if n >= 2:
                    e.wait_ge(s_dq.h, at(f"dqv:{n - 2}"))
                dmark(e.dma_start(out=dq_pk[par][:, 0:512],
                                  in_=gut[128 * cc:128 * cc + 128,
                                          512 * ob:512 * ob + 512]),
                      f"p0_pk:{n}")
                dmark(e.dma_start(out=dq_sc[par], in_=sgu[cc, ob]),
                      f"p0_sc:{n}")
                e.wait_ge(s_dq.h, at(f"dqv:{n}"))
                dmark(e.dma_start(
                    out=s_wgu[cc, ob].rearrange("j p c -> p j c"),
                    in_=dq_out[par].rearrange("p (j c) -> p j c", j=8)),
                    f"p0_gu:{n}")
            M2["p0bst:f0"] = d_p0bst[0].n
            M2["p0bst:f1"] = d_p0bst[1].n
            for t in range(NIT):
                a, j, mixed = DN_TILES[t]
                par = t % 2
                if t >= 2:
                    e.wait_ge(s_dq.h, at(f"dqv:d{t - 2}:3"))
                else:
                    e.wait_ge(s_dq.h, at(f"dqv:{NGU - 2 + t}"))
                if mixed:
                    dmark(e.dma_start(out=dq_pk[par][0:64, :],
                                      in_=dnt[a:a + 64, :]), f"p0_pkd:{t}")
                    dmark(e.dma_start(out=dq_pk[par][64:128, :],
                                      in_=dnt[0:64, :]), f"p0_pkd:{t}")
                else:
                    dmark(e.dma_start(out=dq_pk[par], in_=dnt[a:a + 128, :]),
                          f"p0_pkd:{t}")
                M2[f"p0_pkdone:{t}"] = d_pk[par].n
                for q in range(4):
                    qpar = q % 2
                    if 4 * t + q >= 2:
                        pt, pq = (t, q - 2) if q >= 2 else (t - 1, q + 2)
                        e.wait_ge(s_dq.h, at(f"dqv:d{pt}:{pq}"))
                    else:
                        e.wait_ge(s_dq.h, at(f"dqv:{NGU - 2 + qpar}"))
                    dmark(e.dma_start(
                        out=dq_sc[qpar],
                        in_=dnsb[t, :, 512 * q:512 * q + 512]),
                        f"p0_scd:{t}:{q}")
                    e.wait_ge(s_dq.h, at(f"dqv:d{t}:{q}"))
                    dmark(e.dma_start(
                        out=s_wdn[t, :, 512 * q:512 * q + 512],
                        in_=dq_out[par][:, 512 * qpar:512 * qpar + 512]),
                        f"p0_dnst:{t}:{q}")
            M2["gust:f0"] = d_gust[0].n
            M2["gust:f1"] = d_gust[1].n
            M2["dnst:f0"] = d_dnst[0].n
            M2["dnst:f1"] = d_dnst[1].n

        @blk.tensor
        def _(e):
            dwait(e, "cst")
            dwait(e, "poolcst")
            for s in range(SPC):
                for tt in range(16):
                    n = s * 16 + tt
                    e.wait_ge(s_c1.h, at(f"p1a_cast:{n}"))
                    if n >= 1:
                        e.wait_ge(s_c1.h, at(f"p1a_evac:{n - 1}"))
                    if s == 1 and tt == 0:
                        e.wait_ge(s_c1.h, at("p1_evacdone:0:2:3:3"))
                    for dc in range(16):
                        ins = e.matmul(ps_big[:, 128 * dc:128 * dc + 128],
                                       p1_xb[n % 2][:, 128 * dc:128 * dc + 128],
                                       c_id16, start=True, stop=True,
                                       skip_group_check=True)
                    mark(s_t1, ins, f"p1a_mm:{n}", 1)
                for wi, w in enumerate("qkv"):
                    for og in range(4):
                        par = og % 2
                        dwait(e, f"p1_w:{s}:{wi}:{og}:15")
                        for tq in range(4):
                            xpar = tq % 2
                            dwait(e, f"p1_xq:{s}:{wi}:{og}:{tq}:15")
                            u = ((s * 3 + wi) * 4 + og) * 4 + tq
                            if u >= 2:
                                pu = u - 2
                                pl = pu % 4
                                pog = (pu // 4) % 4
                                pw = (pu // 16) % 3
                                ps_ = pu // 48
                                e.wait_ge(s_c1.h,
                                          at(f"p1_evacdone:{ps_}:{pw}:{pog}:{pl}"))
                                if pw == 2 and pl == 3:
                                    e.wait_ge(s_c1.h,
                                              at(f"p1_vevdone:{ps_}:{pog}:3"))
                            elif s == 0 and wi == 0:
                                e.wait_ge(s_c1.h, at(f"p1a_evac:{15}"))
                            if w == "v" and tq == 1 and og >= 1:
                                e.wait_ge(s_c1.h, at(f"p1_vevdone:{s}:{og - 1}:3"))
                            for dc in range(16):
                                for ol in range(4):
                                    dst = (QUAR(ol) if tq % 2 == 0
                                           else SETB[ol][:])
                                    ins = e.matmul(
                                        dst,
                                        p1_wog[par][:, 512 * dc + 128 * ol:
                                                    512 * dc + 128 * ol + 128],
                                        p1_xq[xpar][:, 512 * dc:512 * dc + 512],
                                        start=(dc == 0), stop=(dc == 15))
                            mark(s_t1, ins, f"p1_mm:{s}:{wi}:{og}:{tq}", 1)
                        if w == "v":
                            for ol in range(4):
                                e.wait_ge(s_c1.h, at(f"p1_evac:{s}:{wi}:{og}:3:{ol}"))
                                if ol == 0:
                                    e.wait_ge(s_c1.h,
                                              at(f"p1_evac:{s}:{wi}:{og}:3:1"))
                                for kc in range(16):
                                    grp = kc // 4
                                    dstp = (ps_c, ps_d)[grp % 2]
                                    vsl = 4 * ol + grp
                                    if kc % 4 == 0 and vsl >= 2:
                                        e.wait_ge(s_c1.h,
                                                  at(f"p1_vev:{s}:{og}:"
                                                     f"{(vsl - 2) // 4}:"
                                                     f"{(vsl - 2) % 4}"))
                                    ins = e.matmul(
                                        dstp[:, 128 * (kc % 4):128 * (kc % 4) + 128],
                                        p1_vfm[:, 2048 * ol + 128 * kc:
                                               2048 * ol + 128 * kc + 128],
                                        c_id16, start=True, stop=True,
                                        skip_group_check=True)
                                    if kc % 4 == 3:
                                        mark(s_t1, ins, f"p1_vtr:{s}:{og}:{ol}:{grp}", 1)
                M2[f"p1_vtrdone:{s}"] = s_t1.n
            if upto < 2:
                return

            sc_bank = [None] * 4
            for s in range(SPC):
                for g in range(4):
                    nblk, keys, qts = QG_PLANS[g]
                    zmap, mmap = CHUNK_MAPS[g]
                    qgi = s * 4 + g
                    def emit_bc(phi):
                        e.wait_ge(s_v2.h, at(f"p2_rc:{phi}"))
                        ins = e.matmul(ps_f[:, 0:512], c_ones1, p2_rdr,
                                       start=True, stop=True)
                        mark(s_t2, ins, f"p2_bc:{phi}", 1)

                    for h in range(NH):
                        par = h % 2
                        hi = qgi * NH + h
                        dwait(e, f"p2_q:{hi}")
                        dwait(e, f"p2_k:{hi}")
                        if h >= 1:
                            emit_bc(hi - 1)
                        for c in range(nblk):
                            b = c % 4
                            if hi == 0 and c < 4:
                                e.wait_ge(s_c1.h, at("p1_evacdone:1:2:3:3"))
                            elif c % 4 == 0 and c >= 4:
                                e.wait_ge(s_c2.h, at(f"p2_ex:{hi}:{c - 1}"))
                            if h == 0 and c == 0 and qgi >= 1:
                                e.wait_ge(s_v2.h, at(f"p2_r1:{qgi * 4 - 1}"))
                            ins = e.matmul(
                                ps_big[:, 512 * b:512 * b + 512],
                                p2_k[par][:, keys[c]:keys[c] + 128],
                                p2_q[par][:, 0:512],
                                start=True, stop=True, skip_group_check=True)
                            mark(s_t2, ins, f"p2_sc:{hi}:{c}", 1)
                        if any(mmap):
                            e.wait_ge(s_v2.h, at(f"p2_mk:{hi}"))
                        if hi >= 1:
                            e.wait_ge(s_c2.h, at(f"p2_bcev:{hi - 1}"))
                        e.wait_ge(s_c2.h, at(f"p2_ex:{hi}:{max(nblk - 5, 0)}"))
                        for c in range(nblk):
                            if c == nblk - 4:
                                e.wait_ge(s_c2.h,
                                          at(f"p2_ex:{hi}:{nblk - 1}"))
                            ins = e.matmul(ps_f[0:1, 0:512], c_onesd,
                                           p2_pt[:, 512 * c:512 * c + 512],
                                           start=(c == 0),
                                           stop=(c == nblk - 1),
                                           skip_group_check=True)
                        mark(s_t2, ins, f"p2_den:{hi}", 1)
                        dwait(e, f"p2_v:{hi}")
                        if hi >= 1:
                            e.wait_ge(s_v2.h, at(f"p2_pvev:{hi - 1}"))
                        for c in range(nblk):
                            vc = c if g <= 1 else (0 if c == 0 else
                                                   4 * (g - 1) + c - 1)
                            ins = e.matmul(ps_e[:],
                                           p2_v[par][:, 128 * vc:128 * vc + 128],
                                           p2_pt[:, 512 * c:512 * c + 512],
                                           start=(c == 0), stop=(c == nblk - 1))
                        mark(s_t2, ins, f"p2_pv:{hi}", 1)
                    emit_bc(qgi * NH + NH - 1)
                    e.wait_ge(s_c2.h,
                              at(f"p2_ex:{qgi * NH + NH - 1}:{nblk - 1}"))
                    e.wait_ge(s_v2.h, at(f"p2_pvev:{qgi * NH + NH - 1}"))
                    if qgi >= 1:
                        e.wait_ge(s_v2.h, at(f"p2_r1:{qgi * 4 - 1}"))
                    for og in range(4):
                        par = og % 2
                        dwait(e, f"p2_wo:{qgi}:{og}:15")
                        if og >= 1:
                            e.wait_ge(s_c2.h, at(f"p2_woev:{qgi}:{og - 1}"))
                        for dc in range(16):
                            for ol in range(4):
                                ins = e.matmul(
                                    QUAR(ol),
                                    p1_wog[par][:, 512 * dc + 128 * ol:
                                                512 * dc + 128 * ol + 128],
                                    p2_afm[:, 512 * dc:512 * dc + 512],
                                    start=(dc == 0), stop=(dc == 15))
                        mark(s_t2, ins, f"p2_womm:{qgi}:{og}", 1)
                    for tc in range(4):
                        n = qgi * 4 + tc
                        e.wait_ge(s_c2.h, at(f"p2_woev:{qgi}:3"))
                        if tc >= 1:
                            e.wait_ge(s_v2.h, at(f"p2_r1:{n - 1}"))
                        for oc in range(16):
                            ins = e.matmul(ps_big[:, 128 * oc:128 * oc + 128],
                                           p2_wof[:, 512 * oc + 128 * tc:
                                                  512 * oc + 128 * tc + 128],
                                           c_id16, start=True, stop=True,
                                           skip_group_check=True)
                        mark(s_t2, ins, f"p2_wot:{n}", 1)
                        e.wait_ge(s_c2.h, at(f"p2_x1b:{n}"))
                        if n == 0:
                            e.wait_ge(s_c1.h, at("p1_vevdone:1:3:3"))
                        for dc in range(16):
                            grp = dc // 4
                            dstp = (ps_c, ps_d)[grp % 2]
                            if dc % 4 == 0 and grp >= 2:
                                e.wait_ge(s_c2.h, at(f"p2_x1ev:{n}:{grp - 2}"))
                            ins = e.matmul(
                                dstp[:, 128 * (dc % 4):128 * (dc % 4) + 128],
                                p2_x1b[:, 128 * dc:128 * dc + 128],
                                c_id16, start=True, stop=True,
                                skip_group_check=True)
                            if dc % 4 == 3:
                                mark(s_t2, ins, f"p2_x1tr:{n}:{grp}", 1)
            if upto < 3:
                return

            for tt in range(NTT):
                dwait(e, f"p3_x1j:{tt}:15")
                for usb in range(22):
                    par = usb % 2
                    dwait(e, f"p3_wg:{tt}:{usb}:1")
                    dwait(e, f"p3_wu:{tt}:{usb}:1")
                    for ol in range(2):
                        isl = 2 * usb + ol
                        gpar = isl % 2
                        gdst = ps_c[:] if gpar == 0 else ps_big[:, 0:512]
                        udst = ps_d[:] if gpar == 0 else ps_big[:, 512:1024]
                        if isl >= 2:
                            e.wait_ge(s_c3.h, at(f"p3_silu:{tt}:{isl - 2}"))
                            e.wait_ge(s_v3.h, at(f"p3_h:{tt}:{isl - 2}"))
                        elif tt >= 1:
                            e.wait_ge(s_c3.h, at(f"p3_ytevdone:{tt - 1}"))
                            e.wait_ge(s_c3.h, at(f"p3_silu:{tt - 1}:{isl + 42}"))
                            e.wait_ge(s_v3.h, at(f"p3_h:{tt - 1}:{isl + 42}"))
                        else:
                            e.wait_ge(s_c2.h, at("p2_lastx1ev"))
                        for kk in range(16):
                            cc, j = divmod(kk, 8)
                            ins = e.matmul(
                                gdst,
                                p3_wg[par][:, 2048 * cc + 256 * j + 128 * ol:
                                           2048 * cc + 256 * j + 128 * ol + 128],
                                p3_x1j[:, 512 * kk:512 * kk + 512],
                                start=(kk == 0), stop=(kk == 15))
                        mark(s_t3, ins, f"p3_gmm:{tt}:{isl}", 1)
                        for kk in range(16):
                            cc, j = divmod(kk, 8)
                            ins = e.matmul(
                                udst,
                                p3_wu[par][:, 2048 * cc + 256 * j + 128 * ol:
                                           2048 * cc + 256 * j + 128 * ol + 128],
                                p3_x1j[:, 512 * kk:512 * kk + 512],
                                start=(kk == 0), stop=(kk == 15))
                        mark(s_t3, ins, f"p3_umm:{tt}:{isl}", 1)
                    M2[f"p3_gumm:{tt}:{usb}"] = s_t3.n
                M2[f"p3_gudone:{tt}"] = s_t3.n
                for ds in range(16):
                    ob, r = divmod(ds, 4)
                    ocp, half = divmod(r, 2)
                    par = ds % 2
                    dwait(e, f"p3_wd:{tt}:{ds}:1")
                    if half == 0:
                        e.wait_ge(s_v3.h, at(f"p3_h:{tt}:43"))
                        pb = 4 * ob + 2 * ocp
                        if pb >= 2:
                            e.wait_ge(s_c3.h, at(f"p3_yev:{tt}:{pb - 1}"))
                        elif tt >= 1:
                            e.wait_ge(s_c3.h, at(f"p3_yev:{tt - 1}:{pb + 15}"))
                    for oc in range(2):
                        ydst = (ps_e, ps_f)[oc]
                        for it in range(22):
                            ins = e.matmul(
                                ydst[:],
                                p3_wd[par][:, 256 * it + 128 * oc:
                                           256 * it + 128 * oc + 128],
                                p3_h[:, 512 * (22 * half + it):
                                     512 * (22 * half + it) + 512],
                                start=(half == 0 and it == 0),
                                stop=(half == 1 and it == 21))
                        if half == 1:
                            mark(s_t3, ins,
                                 f"p3_ymm:{tt}:{4 * ob + 2 * ocp + oc}", 1)
                        elif oc == 1:
                            mark(s_t3, ins, f"p3_dmm:{tt}:{ds}", 1)
                    if half == 1:
                        M2[f"p3_dmm:{tt}:{ds}"] = s_t3.n
                for tc in range(4):
                    n = tt * 4 + tc
                    e.wait_ge(s_c3.h, at(f"p3_yev:{tt}:15"))
                    if tc >= 1:
                        e.wait_ge(s_c3.h, at(f"p3_ytev:{n - 1}:3"))
                    for oc in range(16):
                        grp = oc // 4
                        dstp = (ps_c, ps_d)[grp % 2]
                        if oc % 4 == 0 and grp >= 2:
                            e.wait_ge(s_c3.h, at(f"p3_ytev:{n}:{grp - 2}"))
                        ins = e.matmul(
                            dstp[:, 128 * (oc % 4):128 * (oc % 4) + 128],
                            p3_yf[:, 512 * oc + 128 * tc:512 * oc + 128 * tc + 128],
                            c_id16, start=True, stop=True,
                            skip_group_check=True)
                        if oc % 4 == 3:
                            mark(s_t3, ins, f"p3_yt:{n}:{grp}", 1)

        @blk.scalar
        def _(e):


            dwait(e, "cst")
            e.mul(c_bqs, c_bias["q"], QSCALE)
            e.drain()

            def emit_xq(s, wi, og, tq):
                xpar = tq % 2
                if wi == 0 and og == 0:
                    dwait(e, f"p1a_st:{s}:{tq}:15")
                for dc in range(16):
                    dmark(e.dma_start(
                        out=p1_xq[xpar][:, 512 * dc:512 * dc + 512],
                        in_=s_xfm[s, 128 * dc:128 * dc + 128,
                                  512 * tq:512 * tq + 512]),
                        f"p1_xq:{s}:{wi}:{og}:{tq}:{dc}")

            for s in range(SPC):
                for tt in range(16):
                    n = s * 16 + tt
                    dwait(e, f"p1a_x:{n}")
                    if n >= 2:
                        e.wait_ge(s_t1.h, at(f"p1a_mm:{n - 2}"))
                    mark(s_c1, e.activation(p1_xb[n % 2], p1_x32[n % 2],
                                            AF.Copy), f"p1a_cast:{n}", 1)
                    e.wait_ge(s_t1.h, at(f"p1a_mm:{n}"))
                    gq = n // 4
                    if gq >= 1:
                        dwait(e, f"p1a_st:{(gq - 1) // 4}:{(gq - 1) % 4}:15")
                    mark(s_c1, e.activation(
                        p1_stx[:, 2048 * (tt % 4):2048 * (tt % 4) + 2048],
                        ps_big[:], AF.Copy), f"p1a_evac:{n}", 1)
                emit_xq(s, 0, 0, 0)
                emit_xq(s, 0, 0, 1)
                for wi, w in enumerate("qkv"):
                    for og in range(4):
                        for tq in range(4):
                            e.wait_ge(s_t1.h, at(f"p1_mm:{s}:{wi}:{og}:{tq}"))
                            for ol in range(4):
                                src = (QUAR(ol) if tq % 2 == 0 else SETB[ol][:])
                                oc = 4 * og + ol
                                if tq == 0:
                                    if w == "v":
                                        if og >= 1:
                                            e.wait_ge(s_t1.h,
                                                      at(f"p1_vtr:{s}:{og - 1}:{ol}:3"))
                                        elif s == 1:
                                            e.wait_ge(s_t1.h,
                                                      at(f"p1_vtr:0:3:{ol}:3"))
                                    else:
                                        po = og - 1
                                        pwi, psx = wi, s
                                        if po < 0:
                                            pwi -= 1
                                            po = 3
                                        if pwi < 0:
                                            psx -= 1
                                            pwi = 1
                                        if psx >= 0:
                                            dwait(e, f"p1_st:{psx}:{pwi}:{4 * po + 3}")
                                dst = (p1_vfm if w == "v" else p1_stg)
                                if (w == "v" and s == 0 and og == 0
                                        and tq == 0 and ol == 0):
                                    dwait(e, "p0b:done")
                                if w == "q":
                                    ins = e.activation(
                                        dst[:, 2048 * ol + 512 * tq:
                                            2048 * ol + 512 * tq + 512],
                                        src, AF.Identity,
                                        bias=c_bqs[:, oc:oc + 1], scale=QSCALE)
                                else:
                                    ins = e.activation(
                                        dst[:, 2048 * ol + 512 * tq:
                                            2048 * ol + 512 * tq + 512],
                                        src, AF.Identity,
                                        bias=c_bias[w][:, oc:oc + 1], scale=1.0)
                                mark(s_c1, ins, f"p1_evac:{s}:{wi}:{og}:{tq}:{ol}", 1)
                            M2[f"p1_evacdone:{s}:{wi}:{og}:{tq}"] = s_c1.n
                            u2 = (wi * 4 + og) * 4 + tq + 2
                            if u2 < 48:
                                emit_xq(s, u2 // 16, (u2 // 4) % 4, u2 % 4)
                        if w == "v":
                            for ol in range(4):
                                oc = 4 * og + ol
                                if oc >= 1:
                                    dwait(e, f"p1_stv:{s}:{oc - 1}")
                                elif s == 1:
                                    dwait(e, "p1_stv:0:15")
                                for grp in range(4):
                                    e.wait_ge(s_t1.h, at(f"p1_vtr:{s}:{og}:{ol}:{grp}"))
                                    dstp = (ps_c, ps_d)[grp % 2]
                                    mark(s_c1, e.activation(
                                        p1_stv[:, 512 * grp:512 * grp + 512],
                                        dstp[:], AF.Copy),
                                        f"p1_vev:{s}:{og}:{ol}:{grp}", 1)
                                M2[f"p1_vevdone:{s}:{og}:{ol}"] = s_c1.n
            if upto < 2:
                return

            for s in range(SPC):
                for g in range(4):
                    nblk, keys, qts = QG_PLANS[g]
                    qgi = s * 4 + g
                    for h in range(NH):
                        hi = qgi * NH + h
                        for c in range(nblk):
                            e.wait_ge(s_t2.h, at(f"p2_sc:{hi}:{c}"))
                            b = c % 4
                            le = 4
                            if g == 1 and c >= 1:
                                le = min(c + 1, 4)
                            elif g >= 2 and c >= 1:
                                le = min(c, 4)
                            ins = e.activation(
                                p2_pt[:, 512 * c:512 * c + 128 * le],
                                ps_big[:, 512 * b:512 * b + 128 * le],
                                AF.Exp, bias=c_neg8, scale=1.0)
                            if le < 4:
                                ins = e.activation(
                                    p2_pt[:, 512 * c + 128 * le:512 * c + 512],
                                    ps_big[:, 512 * b + 128 * le:512 * b + 512],
                                    AF.Copy, scale=0.0)
                            mark(s_c2, ins, f"p2_ex:{hi}:{c}", 1)
                        e.wait_ge(s_t2.h, at(f"p2_bc:{hi}"))
                        if hi >= 1:
                            e.wait_ge(s_v2.h, at(f"p2_pvev:{hi - 1}"))
                        mark(s_c2, e.activation(p2_bc, ps_f[:, 0:512], AF.Copy),
                             f"p2_bcev:{hi}", 1)
                    if qgi >= 1:
                        e.wait_ge(s_t2.h, at(f"p2_wot:{qgi * 4 - 1}"))
                    for og in range(4):
                        e.wait_ge(s_t2.h, at(f"p2_womm:{qgi}:{og}"))
                        for ol in range(4):
                            oc = 4 * og + ol
                            ins = e.activation(
                                p2_wof[:, 512 * oc:512 * oc + 512],
                                QUAR(ol), AF.Identity,
                                bias=c_bias["o"][:, oc:oc + 1], scale=1.0)
                        mark(s_c2, ins, f"p2_woev:{qgi}:{og}", 1)
                    for tc in range(4):
                        n = qgi * 4 + tc
                        e.wait_ge(s_v2.h, at(f"p2_r1:{n}"))
                        e.activation(p2_x1b, p2_r1, AF.Square,
                                     accum_out=st_ssq)
                        e.activation(p2_r1, p2_r1, AF.Identity,
                                     accum_out=st_sm)
                        e.drain()
                        e.activation(st_m2, st_sm, AF.Square, scale=1.0 / D)
                        e.drain()
                        e.activation(st_m2, st_m2, AF.Identity, scale=-1.0,
                                     bias=c_eps1[:, 0:1])
                        e.drain()
                        e.activation(st_rstd, st_ssq, AF.Ln, bias=st_m2,
                                     scale=1.0 / D)
                        e.drain()
                        e.activation(st_rstd, st_rstd, AF.Exp, scale=-0.5)
                        e.drain()
                        e.activation(st_rstdn, st_rstd, AF.Identity,
                                     scale=-1.0 / D)
                        e.drain()
                        e.activation(st_nmr, st_sm, AF.Identity,
                                     scale=st_rstdn)
                        e.drain()
                        mark(s_c2, e.activation(p2_x1b, p2_r1, AF.Identity,
                                                bias=st_nmr, scale=st_rstd),
                             f"p2_x1b:{n}", 1)
                        if n >= 1:
                            dwait(e, f"p2_stx1t:{n - 1}")
                        e.drain()
                        mark(s_c2, e.activation(p2_r1, p2_r1, AF.Identity,
                                                bias=st_nmr, scale=st_rstd),
                             f"p2_x1tt:{n}", 1)
                        if n >= 1:
                            dwait(e, f"p2_stx1f:{n - 1}")
                        for grp in range(4):
                            e.wait_ge(s_t2.h, at(f"p2_x1tr:{n}:{grp}"))
                            dstp = (ps_c, ps_d)[grp % 2]
                            mark(s_c2, e.activation(
                                p2_stx1[:, 512 * grp:512 * grp + 512],
                                dstp[:], AF.Copy), f"p2_x1ev:{n}:{grp}", 1)
                        M2[f"p2_x1f:{n}"] = s_c2.n
                    M2[f"p2_x1evdone:{qgi}"] = s_c2.n
            M2["p2_lastx1ev"] = s_c2.n
            if upto < 3:
                return

            for tt in range(NTT):
                for isl in range(NIT):
                    e.wait_ge(s_t3.h, at(f"p3_gmm:{tt}:{isl}"))
                    gpar = isl % 2
                    gsrc = ps_c[:] if gpar == 0 else ps_big[:, 0:512]
                    if isl >= 1:
                        e.wait_ge(s_v3.h, at(f"p3_h:{tt}:{isl - 1}"))
                    mark(s_c3, e.activation(p3_sg, gsrc, AF.Silu),
                         f"p3_silu:{tt}:{isl}", 1)
                for o128 in range(16):
                    e.wait_ge(s_t3.h, at(f"p3_ymm:{tt}:{o128}"))
                    if tt >= 1 and o128 == 0:
                        e.wait_ge(s_t3.h, at(f"p3_yt:{tt * 4 - 1}:3"))
                    ysrc = (ps_e, ps_f)[o128 % 2]
                    mark(s_c3, e.activation(
                        p3_yf[:, 512 * o128:512 * o128 + 512], ysrc[:], AF.Copy),
                        f"p3_yev:{tt}:{o128}", 1)
                for tc in range(4):
                    n = tt * 4 + tc
                    e.drain()
                    for grp in range(4):
                        e.wait_ge(s_t3.h, at(f"p3_yt:{n}:{grp}"))
                        dstp = (ps_c, ps_d)[grp % 2]
                        mark(s_c3, e.activation(
                            p3_yt[:, 512 * grp:512 * grp + 512], dstp[:],
                            AF.Copy), f"p3_ytev:{n}:{grp}", 1)
                    if n >= 1:
                        dwait(e, f"out:{n - 1}")
                    e.drain()
                    e.activation(p3_out, p3_yt, AF.Square, accum_out=st_ssq)
                    e.drain()
                    e.activation(st_rstd, st_ssq, AF.Ln, bias=c_eps2,
                                 scale=1.0 / D)
                    e.drain()
                    mark(s_c3, e.activation(st_rstd, st_rstd, AF.Exp,
                                            scale=-0.5), f"p3_rms:{n}", 1)
                    e.wait_ge(s_v3.h, at(f"p3_r2:{n}"))
                    e.activation(p3_out, p3_yt, AF.Square, accum_out=st_ssq)
                    e.activation(p3_yt, p3_yt, AF.Identity, accum_out=st_sm)
                    e.drain()
                    e.activation(st_m2, st_sm, AF.Square, scale=1.0 / D)
                    e.drain()
                    e.activation(st_m2, st_m2, AF.Identity, scale=-1.0,
                                 bias=c_eps1[:, 0:1])
                    e.drain()
                    e.activation(st_rstd, st_ssq, AF.Ln, bias=st_m2,
                                 scale=1.0 / D)
                    e.drain()
                    e.activation(st_rstd, st_rstd, AF.Exp, scale=-0.5)
                    e.drain()
                    e.activation(st_rstdn, st_rstd, AF.Identity,
                                 scale=-1.0 / D)
                    e.drain()
                    e.activation(st_nmr, st_sm, AF.Identity, scale=st_rstdn)
                    e.drain()
                    mark(s_c3, e.activation(p3_out, p3_yt, AF.Identity,
                                            bias=st_nmr, scale=st_rstd),
                         f"p3_out:{n}", 1)
                M2[f"p3_ytevdone:{tt}"] = s_c3.n

        @blk.vector
        def _(e):
            def lut_pass(src_words, shift, rows, cols, half=False):
                r = rows
                e.tensor_scalar(dq_mi[r, :], src_words[r, cols], shift, 7,
                                ALU.logical_shift_right, ALU.bitwise_and)
                e.tensor_scalar(dq_si[r, :], src_words[r, cols], shift + 3, 1,
                                ALU.logical_shift_right, ALU.bitwise_and)
                e.drain()
                if half:
                    e.tensor_scalar(dq_r4[r, :], dq_mi[r, :], 1.0, -2.0,
                                    ALU.mult, ALU.add)
                    e.tensor_scalar(dq_r6[r, :], dq_mi[r, :], 2.0, -8.0,
                                    ALU.mult, ALU.add)
                    e.tensor_scalar(dq_m[r, :], dq_mi[r, :], 0.5, 0.0,
                                    ALU.mult, ALU.add)
                else:
                    e.tensor_scalar(dq_r4[r, :], dq_mi[r, :], 2.0, -4.0,
                                    ALU.mult, ALU.add)
                    e.tensor_scalar(dq_r6[r, :], dq_mi[r, :], 4.0, -16.0,
                                    ALU.mult, ALU.add)
                    e.tensor_copy(dq_m[r, :], dq_mi[r, :])
                e.tensor_scalar(dq_s[r, :], dq_si[r, :], -2.0, 1.0,
                                ALU.mult, ALU.add)
                e.drain()
                e.tensor_tensor(dq_r4[r, :], dq_r4[r, :], dq_r6[r, :], ALU.max)
                e.drain()
                e.tensor_tensor(dq_r4[r, :], dq_m[r, :], dq_r4[r, :], ALU.max)
                e.drain()
                e.tensor_tensor(dq_r4[r, :], dq_r4[r, :], dq_s[r, :], ALU.mult)
                e.drain()

            dq_jobs = []

            def _gu_job(n, j):
                def run():
                    par = n % 2
                    if j == 0:
                        dwait(e, f"p0_pk:{n}")
                        dwait(e, f"p0_sc:{n}")
                        if n >= 2:
                            dwait(e, f"p0_gu:{n - 2}")
                    lut_pass(dq_pk[par], 4 * j, slice(0, 128), slice(0, 512),
                             True)
                    ins = e.tensor_tensor(dq_out[par][:, 512 * j:512 * j + 512],
                                          dq_r4[:, 0:512], dq_sc[par], ALU.mult)
                    if j == 7:
                        mark(s_dq, ins, f"dqv:{n}", 1)
                return run

            def _gu_dn_bar():
                def run():
                    dwait(e, "p0gu:done")
                return run

            def _dn_job(t, q):
                def run():
                    a, j, mixed = DN_TILES[t]
                    par = t % 2
                    qpar = q % 2
                    if q == 0:
                        e.wait_ge(d_pk[par].h, at(f"p0_pkdone:{t}"))
                    dwait(e, f"p0_scd:{t}:{q}")
                    if 4 * t + q >= 2:
                        pt, pq = (t, q - 2) if q >= 2 else (t - 1, q + 2)
                        dwait(e, f"p0_dnst:{pt}:{pq}")
                    cols = slice(512 * q, 512 * q + 512)
                    if mixed:
                        lut_pass(dq_pk[par], 4 * j, slice(0, 64), cols, True)
                        lut_pass(dq_pk[par], 4 * (j + 1), slice(64, 128), cols,
                                 True)
                    else:
                        lut_pass(dq_pk[par], 4 * j, slice(0, 128), cols, True)
                    ins = e.tensor_tensor(
                        dq_out[par][:, 512 * qpar:512 * qpar + 512],
                        dq_r4[:, 0:512], dq_sc[qpar], ALU.mult)
                    mark(s_dq, ins, f"dqv:d{t}:{q}", 1)
                return run

            for n in range(NGU):
                for j in range(8):
                    _gu_job(n, j)()
            _gu_dn_bar()()
            for t in range(NIT):
                for q in range(4):
                    dq_jobs.append(_dn_job(t, q))
            dq_jobs.reverse()

            def dq_step(k):
                for _ in range(k):
                    if dq_jobs:
                        dq_jobs.pop()()

            def dq_drain():
                while dq_jobs:
                    dq_jobs.pop()()
                M2["dqv:all"] = s_dq.n

            if upto < 2:
                dq_drain()

            if upto >= 2:
                for s in range(SPC):
                    for g in range(4):
                        nblk, keys, qts = QG_PLANS[g]
                        qgi = s * 4 + g
                        zmap, mmap = CHUNK_MAPS[g]
                        for h in range(NH):
                            hi = qgi * NH + h
                            last = None
                            for c in range(nblk):
                                if not mmap[c]:
                                    continue
                                e.wait_ge(s_c2.h, at(f"p2_ex:{hi}:{c}"))
                                for (l, w) in mmap[c]:
                                    mt = {"tri": c_tri, "trisink": c_trisink,
                                          "sink": c_sink}[w]
                                    sub = p2_pt[:, 512 * c + 128 * l:
                                                512 * c + 128 * l + 128]
                                    last = e.tensor_tensor(sub, sub, mt,
                                                           ALU.mult)
                            if last is not None:
                                mark(s_v2, last, f"p2_mk:{hi}", 1)
                            e.wait_ge(s_t2.h, at(f"p2_den:{hi}"))
                            mark(s_v2, e.reciprocal(p2_rdr, ps_f[0:1, 0:512]),
                                 f"p2_rc:{hi}", 1)
                            e.wait_ge(s_t2.h, at(f"p2_pv:{hi}"))
                            e.wait_ge(s_c2.h, at(f"p2_bcev:{hi}"))
                            if qgi >= 1 and h == 0:
                                e.wait_ge(s_t2.h, at(f"p2_womm:{qgi - 1}:3"))
                            mark(s_v2, e.tensor_tensor(
                                p2_afm[:, 512 * h:512 * h + 512],
                                ps_e[:], p2_bc, ALU.mult),
                                f"p2_pvev:{hi}", 1)
                            dq_step(1)
                        for tc in range(4):
                            n = qgi * 4 + tc
                            e.wait_ge(s_t2.h, at(f"p2_wot:{n}"))
                            dwait(e, f"p2_x:{qgi}:{tc}")
                            if n >= 1:
                                dwait(e, f"p2_stx1t:{n - 1}")
                            mark(s_v2, e.tensor_tensor(p2_r1, ps_big[:],
                                                       p2_x32[tc % 2],
                                                       ALU.add),
                                 f"p2_r1:{n}", 1)
                            dq_step(2)
                dq_drain()

            if upto >= 3:
                for tt in range(NTT):
                    for isl in range(NIT):
                        e.wait_ge(s_c3.h, at(f"p3_silu:{tt}:{isl}"))
                        e.wait_ge(s_t3.h, at(f"p3_umm:{tt}:{isl}"))
                        if isl == 0 and tt >= 1:
                            e.wait_ge(s_t3.h, at(f"p3_ymm:{tt - 1}:15"))
                        gpar = isl % 2
                        usrc = ps_d[:] if gpar == 0 else ps_big[:, 512:1024]
                        mark(s_v3, e.tensor_tensor(
                            p3_h[:, 512 * isl:512 * isl + 512],
                            p3_sg, usrc, ALU.mult), f"p3_h:{tt}:{isl}", 1)
                    for tc in range(4):
                        n = tt * 4 + tc
                        e.wait_ge(s_c3.h, at(f"p3_rms:{n}"))
                        dwait(e, f"p3_x1r:{n}")
                        mark(s_v3, e.scalar_tensor_tensor(
                            p3_yt, p3_yt, st_rstd, p3_x1r,
                            ALU.mult, ALU.add), f"p3_r2:{n}", 1)

    return nc, M2



def _perm():
    k = np.arange(FI)
    return 8 * (k % DWRD) + (k // DWRD)


def _host_prep(inputs):
    f32 = np.float32
    x = np.asarray(inputs["L0_attn_hidden_states"], dtype=f32).reshape(
        BATCH * SEQ, D)
    h = {}
    for w, key in (("q", "L0_attn_Wq"), ("k", "L0_attn_Wk"),
                   ("v", "L0_attn_Wv"), ("o", "L0_attn_Wo")):
        h[f"w{w}t"] = np.ascontiguousarray(np.asarray(inputs[key], dtype=f32).T)
    for w, key in (("q", "L0_attn_bq"), ("k", "L0_attn_bk"),
                   ("v", "L0_attn_bv"), ("o", "L0_attn_bo")):
        h[f"b{w}"] = np.ascontiguousarray(
            np.asarray(inputs[key], dtype=f32).reshape(16, 128).T)

    perm = _perm()
    gup = np.asarray(inputs["L0_ffn_gate_up_packed"])
    gp = np.empty_like(gup)
    gp[:FI] = gup[:FI][perm]
    gp[FI:] = gup[FI:][perm]
    h["gut"] = np.ascontiguousarray(gp.T)
    gs = np.asarray(inputs["L0_ffn_gate_up_scales"])
    gsp = np.empty_like(gs)
    gsp[:, :FI] = gs[:, :FI][:, perm]
    gsp[:, FI:] = gs[:, FI:][:, perm]
    g16 = gsp.astype(np.float16) if gsp.dtype != np.float16 else gsp
    sgu = np.empty((2, NOB, 128, 512), np.float16)
    rows = np.arange(128) // 16
    for cc in range(2):
        for ob in range(NOB):
            sgu[cc, ob] = g16[8 * cc + rows, 512 * ob:512 * ob + 512]
    h["sgu"] = sgu
    h["dnt"] = np.ascontiguousarray(np.asarray(inputs["L0_ffn_down_packed"]).T)
    dsc16 = np.asarray(inputs["L0_ffn_down_scales"]).astype(np.float16)
    pos = np.arange(NIT * 128)
    grp = (8 * (pos % DWRD) + pos // DWRD) // 128
    h["dnsb"] = np.ascontiguousarray(dsc16[grp].reshape(NIT, 128, D))

    h["idm"] = np.eye(128, dtype=f32)
    r = np.arange(128)
    c = np.arange(128)
    h["mT_tri"] = np.ascontiguousarray(
        (c[None, :] >= r[:, None]).astype(f32).T)
    h["mT_trisink"] = np.ascontiguousarray(
        ((c[None, :] < 4) | (c[None, :] >= r[:, None])).astype(f32).T)
    h["mT_sink"] = np.ascontiguousarray(
        np.broadcast_to((c < 4).astype(f32), (128, 128)).T.copy())
    h["ones1"] = np.ones((1, 128), dtype=f32)
    h["onesd"] = np.ones((128, 1), dtype=f32)
    h["negsh"] = np.full((128, 1), -SHIFT, dtype=f32)
    h["eps1"] = np.full((128, 1), LN_EPS, dtype=f32)
    h["eps2"] = np.full((128, 1), RMS_EPS, dtype=f32)

    return x, h


_CACHED = {}


def _get_nc(dbg=(), upto=3):
    key = (tuple(sorted(dbg)), upto)
    if key not in _CACHED:
        _, M = build_program({}, dbg=dbg, upto=upto)
        nc, _ = build_program(M, dbg=dbg, upto=upto)
        _CACHED[key] = nc
    return _CACHED[key]


def _make_in_maps(inputs):
    x, shared = _host_prep(inputs)
    in_maps = []
    for c in range(NCORES):
        m = dict(shared)
        m["x"] = np.ascontiguousarray(x[c * TOK:(c + 1) * TOK])
        in_maps.append(m)
    return in_maps


def kernel(**inputs):
    nc = _get_nc()
    in_maps = _make_in_maps(inputs)
    res = run_bass_kernel_spmd(nc, in_maps, core_ids=list(range(NCORES)))
    out = np.concatenate([r["y"] for r in res.results], axis=0)
    return out.reshape(BATCH, SEQ, D)


# version 21
# speedup vs baseline: 1.1983x; 1.0430x over previous
from contextlib import ExitStack

import numpy as np
import ml_dtypes

import concourse.bass as bass
import concourse.mybir as mybir
from concourse.bass_utils import run_bass_kernel_spmd

F32 = mybir.dt.float32
BF16 = mybir.dt.bfloat16
F16 = mybir.dt.float16
I32 = mybir.dt.int32
AF = mybir.ActivationFunctionType
ALU = mybir.AluOpType

D = 2048
NH = 16
HD = 128
SEQ = 2048
BATCH = 16
NCORES = 8
SPC = 2
TOK = SPC * SEQ
QSCALE = float(HD) ** -0.5
SHIFT = 8.0

FI = 5632
GU = 2 * FI
GW = 256
DWRD = 704
NIT = 44
NOB = 22
NGU = 2 * NOB
TTOK = 512
NTT = TOK // TTOK

LN_EPS = 1e-5
RMS_EPS = 1e-6

DN_TILES = []
for _t in range(NIT):
    _a, _j = (128 * _t) % DWRD, (128 * _t) // DWRD
    DN_TILES.append((_a, _j, (128 * _t + 127) // DWRD != _j))


def _qg_plan(g):
    if g <= 1:
        nblk = 16
        keys = [128 * c for c in range(nblk)]
    else:
        kfg = 4 * (g - 1)
        nblk = 16 - kfg + 1
        keys = [0] + [128 * (kfg + c - 1) for c in range(1, nblk)]
    qts = []
    for l in range(4):
        masks, zeros = [], []
        if g == 0:
            accum = (0, nblk)
            runs = [(0, nblk)]
        elif g == 1:
            if l == 0:
                masks = [(0, "trisink")]
                accum = (1, nblk)
                runs = [(0, nblk)]
            else:
                masks = [(0, "sink"), (l, "tri")]
                zeros = list(range(1, l))
                accum = (l + 1, nblk)
                runs = [(0, 1), (l, nblk)]
        else:
            masks = [(0, "sink"), (l + 1, "tri")]
            zeros = list(range(1, l + 1))
            accum = (l + 2, nblk)
            runs = [(0, 1), (l + 1, nblk)]
        mms = []
        for (c0, c1) in runs:
            c = c0
            while c < c1:
                ce = min(c1, (c // 4 + 1) * 4)
                mms.append((c, ce))
                c = ce
        qts.append(dict(masks=masks, zeros=zeros, accum=accum, mms=mms))
    return nblk, keys, qts


QG_PLANS = [_qg_plan(g) for g in range(4)]
NGRP = [(QG_PLANS[g][0] + 3) // 4 for g in range(4)]


def _chunk_maps(g):
    nblk, keys, qts = QG_PLANS[g]
    zmap = [[] for _ in range(nblk)]
    mmap = [[] for _ in range(nblk)]
    for l in range(4):
        for c in qts[l]["zeros"]:
            zmap[c].append(l)
        for (c, w) in qts[l]["masks"]:
            mmap[c].append((l, w))
    return zmap, mmap


CHUNK_MAPS = [_chunk_maps(g) for g in range(4)]


def _lastgrp(qi):
    g = (qi // 64) % 4
    return NGRP[g] - 1


def build_program(M, dbg=(), upto=3):
    M2 = {}

    def at(tag):
        return M.get(tag, 0)

    nc = bass.Bass("TRN2", target_bir_lowering=False, debug=False)

    def din(name, shape, dtype):
        return nc.dram_tensor(name, shape, dtype, kind="ExternalInput")

    def scratch(name, shape, dtype):
        kind = "ExternalOutput" if name in dbg else "Internal"
        return nc.dram_tensor(name, shape, dtype, kind=kind)

    x_in = din("x", [TOK, D], F32)
    w_t = {w: din(f"w{w}t", [D, D], F32) for w in "qkvo"}
    b_in = {w: din(f"b{w}", [128, 16], F32) for w in "qkvo"}
    gut = din("gut", [GW, GU], I32)
    sgu = din("sgu", [2, NOB, 128, 512], F16)
    dnt = din("dnt", [DWRD, D], I32)
    dnsb = din("dnsb", [NIT, 128, 2048], F16)
    idm = din("idm", [128, 128], F32)
    m_tri = din("mT_tri", [128, 128], F32)
    m_trisink = din("mT_trisink", [128, 128], F32)
    m_sink = din("mT_sink", [128, 128], F32)
    ones1 = din("ones1", [1, 128], F32)
    onesd = din("onesd", [128, 1], F32)
    negsh = din("negsh", [128, 1], F32)
    eps1 = din("eps1", [128, 1], F32)
    eps2 = din("eps2", [128, 1], F32)

    y_out = nc.dram_tensor("y", [TOK, D], F32, kind="ExternalOutput")

    s_wbf = {w: scratch(f"s_w{w}", [D, D], BF16) for w in "qkvo"}
    s_xfm = scratch("s_xfm", [SPC, D, SEQ], BF16)
    s_q = scratch("s_q", [SPC, D, SEQ], BF16)
    s_k = scratch("s_k", [SPC, D, SEQ], BF16)
    s_v = scratch("s_v", [SPC, NH, 128, SEQ], BF16)
    s_wgu = scratch("s_wgu", [2, NOB, 8, 128, 512], BF16)
    s_wdn = scratch("s_wdn", [NIT, 128, 2048], BF16)
    s_x1t = scratch("s_x1t", [TOK, D], F32)
    s_x1f = scratch("s_x1f", [D, TOK], BF16)

    with ExitStack() as ctx:
        AR_BYTES = 204 * 1024
        arena_h = ctx.enter_context(
            nc.sbuf_tensor("arena", [128, AR_BYTES // 2], BF16))
        ARENA_ADDR = nc.lookup_mloc(arena_h).addr
        ESZ = {F32: 4, I32: 4, BF16: 2, F16: 2}

        class Cur:
            def __init__(self, base):
                self.o = base

            def take(self, nbytes, align=64):
                self.o = (self.o + align - 1) // align * align
                r = self.o
                self.o += nbytes
                return r

        _tn = [0]

        def tile(cur, rows, cols, dtype):
            esz = ESZ[dtype]
            off = cur.take(cols * esz)
            _tn[0] += 1
            h = nc.alloc_sbuf_tensor_at(f"tl{_tn[0]}", [rows, cols], dtype,
                                        offset=ARENA_ADDR + off)
            return h[:, :]

        def tile_at(byte_off, rows, cols, dtype):
            _tn[0] += 1
            h = nc.alloc_sbuf_tensor_at(f"tl{_tn[0]}", [rows, cols], dtype,
                                        offset=ARENA_ADDR + byte_off)
            return h[:, :]

        R0 = Cur(0)
        c_id32 = tile(R0, 128, 128, F32)
        c_id16 = tile(R0, 128, 128, BF16)
        c_tri = tile(R0, 128, 128, BF16)
        c_trisink = tile(R0, 128, 128, BF16)
        c_sink = tile(R0, 128, 128, BF16)
        c_ones1 = tile(R0, 1, 128, F32)
        c_onesd = tile(R0, 128, 1, BF16)
        c_neg8 = tile(R0, 128, 1, F32)
        c_eps1 = tile(R0, 128, 1, F32)
        c_eps2 = tile(R0, 128, 1, F32)
        c_bias = {w: tile(R0, 128, 16, F32) for w in "qkvo"}
        c_bqs = tile(R0, 128, 16, F32)
        st_bn = tile(R0, 128, 24, F32)
        st_ag = tile(R0, 128, 2, F32)
        st_var = tile(R0, 128, 1, F32)
        st_rstd = tile(R0, 128, 1, F32)
        st_nmr = tile(R0, 128, 1, F32)
        st_sm = tile(R0, 128, 1, F32)
        st_ssq = tile(R0, 128, 1, F32)
        st_m2 = tile(R0, 128, 1, F32)
        st_rstdn = tile(R0, 128, 1, F32)
        p2_dn = tile(R0, 128, 4, F32)
        p2_kp = tile(R0, 128, 2, F32)
        p2_rd = tile(R0, 128, 1, F32)
        p2_rdr = tile(R0, 1, 512, F32)
        p2_bc = tile(R0, 128, 512, F32)
        assert R0.o <= 15 * 1024, R0.o

        RD_BASE = 15 * 1024
        RD = Cur(RD_BASE)
        dq_pk = [tile(RD, 128, 2048, I32) for _ in range(2)]
        dq_sc = [tile(RD, 128, 512, F32) for _ in range(2)]
        dq_out = [tile(RD, 128, 4096, BF16) for _ in range(2)]
        dq_mi = tile(RD, 128, 512, I32)
        dq_si = tile(RD, 128, 512, I32)
        dq_m = tile(RD, 128, 512, F32)
        dq_s = tile(RD, 128, 512, F32)
        dq_r4 = tile(RD, 128, 512, F32)
        dq_r6 = tile(RD, 128, 512, F32)
        assert RD.o <= 64 * 1024, RD.o
        RDP3 = Cur(RD_BASE)
        p3_wd = [tile(RDP3, 128, 22 * 256, BF16) for _ in range(2)]
        assert RDP3.o <= 64 * 1024, RDP3.o

        RP_BASE = 64 * 1024
        RP = Cur(RP_BASE)
        p1_wog = [tile(RP, 128, 16 * 512, BF16) for _ in range(2)]
        WOG_END = RP.o
        p1_x32 = [tile(RP, 128, 2048, F32) for _ in range(2)]
        p1_xb = [tile(RP, 128, 2048, BF16) for _ in range(2)]
        STX_OFF = (RP.o + 63) // 64 * 64
        p1_stx = tile(RP, 128, 4 * 2048, BF16)
        XQ_OFF = (RP.o + 63) // 64 * 64
        p1_xq = [tile(RP, 128, 16 * 512, BF16) for _ in range(2)]
        STG_OFF = (RP.o + 63) // 64 * 64
        p1_stg = tile(RP, 128, 4 * 2048, BF16)
        VFM_OFF = (RP.o + 63) // 64 * 64
        p1_vfm = tile(RP, 128, 4 * 2048, BF16)
        STV_OFF = (RP.o + 63) // 64 * 64
        p1_stv = tile(RP, 128, 2048, BF16)
        assert RP.o <= AR_BYTES, RP.o

        wb16 = [tile_at(VFM_OFF, 128, 2048, BF16),
                tile_at(VFM_OFF + 4096, 128, 2048, BF16)]

        RP2 = Cur(WOG_END)
        p2_q = [tile(RP2, 128, 512, BF16) for _ in range(2)]
        p2_k = [tile(RP2, 128, 2048, BF16) for _ in range(2)]
        p2_v = [tile(RP2, 128, 2048, BF16) for _ in range(2)]
        p2_pr = [tile(RP2, 128, 2048, BF16) for _ in range(2)]
        p2_pt = tile(RP2, 128, 16 * 512, BF16)
        p2_afm = tile(RP2, 128, 16 * 512, BF16)
        p2_wof = tile(RP2, 128, 16 * 512, BF16)
        p2_x32 = [tile(RP2, 128, 2048, F32) for _ in range(2)]
        p2_r1 = tile(RP2, 128, 2048, F32)
        p2_x1b = tile(RP2, 128, 2048, BF16)
        p2_stx1 = tile(RP2, 128, 2048, BF16)
        assert RP2.o <= AR_BYTES, RP2.o

        RP3 = Cur(RP_BASE)
        p3_x1j = tile(RP3, 128, 16 * 512, BF16)
        p3_wg = [tile(RP3, 128, 16 * 256, BF16) for _ in range(2)]
        p3_wu = [tile(RP3, 128, 16 * 256, BF16) for _ in range(2)]
        p3_sg = tile(RP3, 128, 512, BF16)
        p3_h = tile(RP3, 128, NIT * 512, BF16)
        p3_yf = tile(RP3, 128, 16 * 512, BF16)
        p3_yt = tile(RP3, 128, 2048, F32)
        p3_x1r = tile(RP3, 128, 2048, F32)
        p3_out = tile(RP3, 128, 2048, F32)
        assert RP3.o <= AR_BYTES, RP3.o

        psa = lambda n, s: ctx.enter_context(nc.psum_tensor(n, s, F32))
        ps_big = psa("ps_big", [128, 2048])
        ps_c = psa("ps_c", [128, 512])
        ps_d = psa("ps_d", [128, 512])
        ps_e = psa("ps_e", [128, 512])
        ps_f = psa("ps_f", [128, 512])
        QUAR = lambda i: ps_big[:, 512 * i:512 * i + 512]
        SETB = [ps_c, ps_d, ps_e, ps_f]

        def mksem(name):
            class Ct:
                __slots__ = ("h", "n", "name")
            c = Ct()
            c.h = ctx.enter_context(nc.semaphore(name))
            c.n = 0
            c.name = name
            return c

        def mark(c, instr, tag=None, k=1):
            instr.then_inc(c.h, k)
            c.n += k
            if tag is not None:
                M2[tag] = c.n
            return c.n

        s_pret = mksem("s_pret")
        s_prec = mksem("s_prec")
        s_c1 = mksem("s_c1")
        s_t1 = mksem("s_t1")
        s_t2 = mksem("s_t2")
        s_v2 = mksem("s_v2")
        s_c2 = mksem("s_c2")
        s_t3 = mksem("s_t3")
        s_v3 = mksem("s_v3")
        s_c3 = mksem("s_c3")
        s_dq = mksem("s_dq")

        d_cst = mksem("d_cst")
        d_gusb = [mksem(f"d_gusb{i}") for i in range(2)]
        d_indb = [mksem(f"d_indb{i}") for i in range(2)]
        d_pres = [mksem(f"d_pres{i}") for i in range(4)]
        d_presd = [mksem(f"d_presd{i}") for i in range(2)]
        d_p0bld = [mksem(f"d_p0bld{i}") for i in range(2)]
        d_p0bst = [mksem(f"d_p0bst{i}") for i in range(2)]
        d_p1ax = [mksem(f"d_p1ax{i}") for i in range(2)]
        d_p1ast = mksem("d_p1ast")
        d_p1w = [mksem(f"d_p1w{i}") for i in range(2)]
        d_p1xq = [mksem(f"d_p1xq{i}") for i in range(2)]
        d_p1st = mksem("d_p1st")
        d_p1stv = mksem("d_p1stv")
        d_p2k = [mksem(f"d_p2k{i}") for i in range(2)]
        d_p2v = [mksem(f"d_p2v{i}") for i in range(2)]
        d_p2q = [mksem(f"d_p2q{i}") for i in range(2)]
        d_p2wo = [mksem(f"d_p2wo{i}") for i in range(2)]
        d_p2x = [mksem(f"d_p2x{i}") for i in range(2)]
        d_p2s1t = mksem("d_p2s1t")
        d_p2s1f = mksem("d_p2s1f")
        d_p3x1j = mksem("d_p3x1j")
        d_p3wg = [mksem(f"d_p3wg{i}") for i in range(2)]
        d_p3wu = [mksem(f"d_p3wu{i}") for i in range(2)]
        d_p3wd = [mksem(f"d_p3wd{i}") for i in range(2)]
        d_p3x1r = mksem("d_p3x1r")
        d_out = mksem("d_out")
        d_poolc = mksem("d_poolc")
        d_pk = [mksem(f"d_pk{i}") for i in range(2)]
        d_psc = [mksem(f"d_psc{i}") for i in range(2)]
        d_gust = [mksem(f"d_gust{i}") for i in range(2)]
        d_dnst = [mksem(f"d_dnst{i}") for i in range(2)]

        def droute(tag):
            p = tag.split(":")
            f = p[0]
            if f == "cst":
                return d_cst
            if f == "gusb":
                return d_gusb[int(p[1]) % 2]
            if f == "indb":
                return d_indb[int(p[1]) % 2]
            if f == "pres":
                k = p[1]
                if k.startswith("d"):
                    return d_presd[int(k[1:]) % 2]
                return d_pres[int(k) % 4]
            if f == "p0b_ld":
                return d_p0bld[int(p[1]) % 2]
            if f == "p0b_st":
                return d_p0bst[int(p[1]) % 2]
            if f == "p1a_x":
                return d_p1ax[int(p[1]) % 2]
            if f == "p1a_st":
                return d_p1ast
            if f == "p1_w":
                return d_p1w[int(p[3]) % 2]
            if f == "p1_xq":
                return d_p1xq[int(p[4]) % 2]
            if f == "p1_st":
                return d_p1st
            if f == "p1_stv":
                return d_p1stv
            if f == "p2_k":
                return d_p2k[int(p[1]) % 2]
            if f == "p2_v":
                return d_p2v[int(p[1]) % 2]
            if f == "p2_q":
                return d_p2q[int(p[1]) % 2]
            if f == "p2_wo":
                return d_p2wo[int(p[2]) % 2]
            if f == "p2_x":
                return d_p2x[int(p[2]) % 2]
            if f == "p2_stx1t":
                return d_p2s1t
            if f == "p2_stx1f":
                return d_p2s1f
            if f == "p3_x1j":
                return d_p3x1j
            if f == "p3_wg":
                return d_p3wg[int(p[2]) % 2]
            if f == "p3_wu":
                return d_p3wu[int(p[2]) % 2]
            if f == "p3_wd":
                return d_p3wd[int(p[2]) % 2]
            if f == "p3_x1r":
                return d_p3x1r
            if f == "out":
                return d_out
            if f == "poolcst":
                return d_poolc
            if f in ("p0_pk", "p0_pkd"):
                return d_pk[int(p[1]) % 2]
            if f == "p0_sc":
                return d_psc[int(p[1]) % 2]
            if f == "p0_scd":
                return d_psc[int(p[2]) % 2]
            if f == "p0_gu":
                return d_gust[int(p[1]) % 2]
            if f == "p0_dnst":
                return d_dnst[int(p[2]) % 2]
            raise KeyError(tag)

        SPECIALS = {
            "pres:all": [(d_pres[i], f"pres:f{i}") for i in range(4)],
            "p0b:done": [(d_p0bst[0], "p0bst:f0"), (d_p0bst[1], "p0bst:f1")],
            "p1:done": [(d_p1st, "p1st:f"), (d_p1stv, "p1stv:f"),
                        (d_p1ast, "p1ast:f"),
                        (d_p0bst[0], "p0bst:f0"), (d_p0bst[1], "p0bst:f1")],
            "p2:done": [(d_p2s1t, "p2s1t:f"), (d_p2s1f, "p2s1f:f")],
            "p0gu:done": [(d_gust[0], "gust:f0"), (d_gust[1], "gust:f1")],
            "p0:done": [(d_gust[0], "gust:f0"), (d_gust[1], "gust:f1"),
                        (d_dnst[0], "dnst:f0"), (d_dnst[1], "dnst:f1")],
        }

        def dmark(instr, tag):
            return mark(droute(tag), instr, tag, 16)

        def dwait(e, tag):
            if tag in SPECIALS:
                for c, t in SPECIALS[tag]:
                    e.wait_ge(c.h, at(t))
            else:
                e.wait_ge(droute(tag).h, at(tag))

        blk = ctx.enter_context(nc.Block())

        @blk.sync
        def _(e):
            for dst_t, src_t in ((c_id32, idm), (c_ones1, ones1),
                                 (c_neg8, negsh),
                                 (c_eps1, eps1), (c_eps2, eps2)):
                dmark(e.dma_start(out=dst_t, in_=src_t[:]), "cst")
            for w in "qkvo":
                dmark(e.dma_start(out=c_bias[w], in_=b_in[w][:]), "cst")
            M2["cst"] = d_cst.n



            for s in range(SPC):
                for tt in range(16):
                    n = s * 16 + tt
                    if n >= 2:
                        e.wait_ge(s_c1.h, at(f"p1a_cast:{n - 2}"))
                    dmark(e.dma_start(out=p1_x32[n % 2],
                                      in_=x_in[2048 * s + 128 * tt:
                                               2048 * s + 128 * tt + 128, :]),
                          f"p1a_x:{n}")
                    if tt % 4 == 3:
                        tq = tt // 4
                        e.wait_ge(s_c1.h, at(f"p1a_evac:{n}"))
                        for dc in range(16):
                            src = p1_stx.rearrange("p (k c t) -> p k c t",
                                                   k=4, c=16)[:, :, dc, :]
                            dmark(e.dma_start(
                                out=s_xfm[s, 128 * dc:128 * dc + 128,
                                          512 * tq:512 * tq + 512], in_=src),
                                f"p1a_st:{s}:{tq}:{dc}")
                for wi, w in enumerate("qkv"):
                    dwait(e, f"p0b_st:{16 * wi + 15}")
                    dwait(e, f"p0b_st:{16 * wi + 14}")
                    for og in range(4):
                        par = og % 2
                        u0 = (s * 3 + wi) * 4 + og
                        if u0 >= 2:
                            pu = u0 - 2
                            ps_, r = divmod(pu, 12)
                            pw, po = divmod(r, 4)
                            e.wait_ge(s_t1.h, at(f"p1_mm:{ps_}:{pw}:{po}:3"))
                        for dc in range(16):
                            dmark(e.dma_start(
                                out=p1_wog[par][:, 512 * dc:512 * dc + 512],
                                in_=s_wbf[w][128 * dc:128 * dc + 128,
                                             512 * og:512 * og + 512]),
                                f"p1_w:{s}:{wi}:{og}:{dc}")
                        for ol in range(4):
                            oc = 4 * og + ol
                            e.wait_ge(s_c1.h, at(f"p1_evac:{s}:{wi}:{og}:3:{ol}"))
                            dstt = {"q": s_q, "k": s_k}.get(w)
                            if dstt is not None:
                                dmark(e.dma_start(
                                    out=dstt[s, 128 * oc:128 * oc + 128, :],
                                    in_=p1_stg[:, 2048 * ol:2048 * ol + 2048]),
                                    f"p1_st:{s}:{wi}:{oc}")
                        if w == "v":
                            for ol in range(4):
                                oc = 4 * og + ol
                                e.wait_ge(s_c1.h, at(f"p1_vevdone:{s}:{og}:{ol}"))
                                dmark(e.dma_start(out=s_v[s, oc], in_=p1_stv),
                                      f"p1_stv:{s}:{oc}")
            M2["p1st:f"] = d_p1st.n
            M2["p1stv:f"] = d_p1stv.n
            M2["p1ast:f"] = d_p1ast.n
            if upto < 2:
                dwait(e, "p1:done")
                dwait(e, "p0b:done")
                dwait(e, "p0:done")
                return

            dwait(e, "p1:done")
            for s in range(SPC):
                for g in range(4):
                    qgi = s * 4 + g
                    for h in range(NH):
                        par = h % 2
                        hi = qgi * NH + h
                        if hi >= 2:
                            e.wait_ge(s_t2.h, at(f"p2_pv:{hi - 2}"))
                        dmark(e.dma_start(out=p2_k[par],
                                          in_=s_k[s, 128 * h:128 * h + 128, :]),
                              f"p2_k:{hi}")
                        dmark(e.dma_start(out=p2_v[par], in_=s_v[s, h]),
                              f"p2_v:{hi}")
                        dmark(e.dma_start(out=p2_q[par],
                                          in_=s_q[s, 128 * h:128 * h + 128,
                                                  512 * g:512 * g + 512]),
                              f"p2_q:{hi}")
                    for og in range(4):
                        par = og % 2
                        if qgi == 0 and og < 2:
                            e.wait_ge(s_t1.h, at("p1_mm:1:2:3:3"))
                        if og >= 2:
                            e.wait_ge(s_t2.h, at(f"p2_womm:{qgi}:{og - 2}"))
                        elif qgi >= 1:
                            e.wait_ge(s_t2.h, at(f"p2_womm:{qgi - 1}:{og + 2}"))
                        for dc in range(16):
                            dmark(e.dma_start(
                                out=p1_wog[par][:, 512 * dc:512 * dc + 512],
                                in_=s_wbf["o"][128 * dc:128 * dc + 128,
                                               512 * og:512 * og + 512]),
                                f"p2_wo:{qgi}:{og}:{dc}")
                    def emit_x1store(pn):
                        pt0 = (2048 * (pn // 16) + 512 * ((pn // 4) % 4)
                               + 128 * (pn % 4))
                        e.wait_ge(s_c2.h, at(f"p2_x1tt:{pn}"))
                        dmark(e.dma_start(out=s_x1t[pt0:pt0 + 128, :],
                                          in_=p2_r1),
                              f"p2_stx1t:{pn}")
                        e.wait_ge(s_c2.h, at(f"p2_x1f:{pn}"))
                        dst = s_x1f.rearrange("(c p) t -> p c t", p=128)[
                            :, :, pt0:pt0 + 128]
                        dmark(e.dma_start(out=dst,
                                          in_=p2_stx1.rearrange(
                                              "p (c t) -> p c t", c=16)),
                              f"p2_stx1f:{pn}")

                    for tc in range(4):
                        n = qgi * 4 + tc
                        t0 = 2048 * s + 512 * g + 128 * tc
                        if n >= 1:
                            emit_x1store(n - 1)
                        if n >= 2:
                            e.wait_ge(s_v2.h, at(f"p2_r1:{n - 2}"))
                        dmark(e.dma_start(out=p2_x32[tc % 2],
                                          in_=x_in[t0:t0 + 128, :]),
                              f"p2_x:{qgi}:{tc}")
            emit_x1store(SPC * 16 - 1)
            M2["p2s1t:f"] = d_p2s1t.n
            M2["p2s1f:f"] = d_p2s1f.n
            if upto < 3:
                dwait(e, "p2:done")
                dwait(e, "p1:done")
                dwait(e, "p0:done")
                return

            dwait(e, "p2:done")
            for tt in range(NTT):
                if tt >= 1:
                    e.wait_ge(s_t3.h, at(f"p3_gudone:{tt - 1}"))
                for cc in range(2):
                    for j in range(8):
                        k = 8 * cc + j
                        src = s_x1f.rearrange("(w r) t -> w r t", r=8)[
                            128 * cc:128 * cc + 128, j,
                            TTOK * tt:TTOK * tt + TTOK]
                        dmark(e.dma_start(out=p3_x1j[:, 512 * k:512 * k + 512],
                                          in_=src), f"p3_x1j:{tt}:{k}")
                for usb in range(22):
                    gp, sbi = divmod(usb, 2)
                    par = usb % 2
                    if usb >= 2:
                        e.wait_ge(s_t3.h, at(f"p3_gumm:{tt}:{usb - 2}"))
                    elif tt >= 1:
                        e.wait_ge(s_t3.h, at(f"p3_gumm:{tt - 1}:{usb + 20}"))
                    dwait(e, "p0gu:done")
                    for cc in range(2):
                        srcg = s_wgu[cc, gp].rearrange("j p c -> p j c")[
                            :, :, 256 * sbi:256 * sbi + 256]
                        srcu = s_wgu[cc, 11 + gp].rearrange("j p c -> p j c")[
                            :, :, 256 * sbi:256 * sbi + 256]
                        dmark(e.dma_start(
                            out=p3_wg[par].rearrange("p (k c) -> p k c", c=256)[
                                :, 8 * cc:8 * cc + 8, :], in_=srcg),
                            f"p3_wg:{tt}:{usb}:{cc}")
                        dmark(e.dma_start(
                            out=p3_wu[par].rearrange("p (k c) -> p k c", c=256)[
                                :, 8 * cc:8 * cc + 8, :], in_=srcu),
                            f"p3_wu:{tt}:{usb}:{cc}")
                for ds in range(16):
                    ob, r = divmod(ds, 4)
                    ocp, half = divmod(r, 2)
                    par = ds % 2
                    if ds >= 2:
                        e.wait_ge(s_t3.h, at(f"p3_dmm:{tt}:{ds - 2}"))
                    elif tt >= 1:
                        e.wait_ge(s_t3.h, at(f"p3_dmm:{tt - 1}:{ds + 14}"))
                    dwait(e, "p0:done")
                    col0 = 512 * ob + 256 * ocp
                    for q in range(2):
                        src = s_wdn[22 * half + 11 * q:22 * half + 11 * q + 11,
                                    :, col0:col0 + 256].rearrange(
                                        "t p c -> p t c")
                        dmark(e.dma_start(
                            out=p3_wd[par].rearrange("p (i c) -> p i c", c=256)[
                                :, 11 * q:11 * q + 11, :], in_=src),
                            f"p3_wd:{tt}:{ds}:{q}")
                def emit_ostore(pn):
                    pt0 = TTOK * (pn // 4) + 128 * (pn % 4)
                    e.wait_ge(s_c3.h, at(f"p3_out:{pn}"))
                    dmark(e.dma_start(out=y_out[pt0:pt0 + 128, :], in_=p3_out),
                          f"out:{pn}")

                for tc in range(4):
                    n = tt * 4 + tc
                    t0 = TTOK * tt + 128 * tc
                    if n >= 1:
                        emit_ostore(n - 1)
                        e.wait_ge(s_v3.h, at(f"p3_r2:{n - 1}"))
                    dmark(e.dma_start(out=p3_x1r, in_=s_x1t[t0:t0 + 128, :]),
                          f"p3_x1r:{n}")
            emit_ostore(NTT * 4 - 1)
            e.wait_ge(d_out.h, d_out.n)

        @blk.gpsimd
        def _(e):
            e.dma_start(out=c_id16, in_=idm[:]).then_inc(d_poolc.h, 16)
            e.dma_start(out=c_tri, in_=m_tri[:]).then_inc(d_poolc.h, 16)
            e.dma_start(out=c_trisink, in_=m_trisink[:]).then_inc(d_poolc.h, 16)
            e.dma_start(out=c_onesd, in_=onesd[:]).then_inc(d_poolc.h, 16)
            d_poolc.n = 64
            dmark(e.dma_start(out=c_sink, in_=m_sink[:]), "poolcst")
            M2["poolcst"] = d_poolc.n

            def p0b_tile(n):
                wi_, dt = divmod(n, 16)
                w = "qkvo"[wi_]
                if n >= 2:
                    dwait(e, f"p0b_st:{n - 2}")
                dmark(e.dma_start(out=wb16[n % 2],
                                  in_=w_t[w][128 * dt:128 * dt + 128, :]),
                      f"p0b_ld:{n}")
                dwait(e, f"p0b_ld:{n}")
                dmark(e.dma_start(out=s_wbf[w][128 * dt:128 * dt + 128, :],
                                  in_=wb16[n % 2]), f"p0b_st:{n}")

            for n in range(48):
                p0b_tile(n)

            for n in range(NGU):
                cc, ob = divmod(n, NOB)
                par = n % 2
                for _w in range(4):
                    if 48 + 4 * n + _w < 64:
                        p0b_tile(48 + 4 * n + _w)
                if n >= 2:
                    e.wait_ge(s_dq.h, at(f"dqv:{n - 2}"))
                dmark(e.dma_start(out=dq_pk[par][:, 0:512],
                                  in_=gut[128 * cc:128 * cc + 128,
                                          512 * ob:512 * ob + 512]),
                      f"p0_pk:{n}")
                dmark(e.dma_start(out=dq_sc[par], in_=sgu[cc, ob]),
                      f"p0_sc:{n}")
                e.wait_ge(s_dq.h, at(f"dqv:{n}"))
                dmark(e.dma_start(
                    out=s_wgu[cc, ob].rearrange("j p c -> p j c"),
                    in_=dq_out[par].rearrange("p (j c) -> p j c", j=8)),
                    f"p0_gu:{n}")
            M2["p0bst:f0"] = d_p0bst[0].n
            M2["p0bst:f1"] = d_p0bst[1].n
            for t in range(NIT):
                a, j, mixed = DN_TILES[t]
                par = t % 2
                if t >= 2:
                    e.wait_ge(s_dq.h, at(f"dqv:d{t - 2}:3"))
                else:
                    e.wait_ge(s_dq.h, at(f"dqv:{NGU - 2 + t}"))
                if mixed:
                    dmark(e.dma_start(out=dq_pk[par][0:64, :],
                                      in_=dnt[a:a + 64, :]), f"p0_pkd:{t}")
                    dmark(e.dma_start(out=dq_pk[par][64:128, :],
                                      in_=dnt[0:64, :]), f"p0_pkd:{t}")
                else:
                    dmark(e.dma_start(out=dq_pk[par], in_=dnt[a:a + 128, :]),
                          f"p0_pkd:{t}")
                M2[f"p0_pkdone:{t}"] = d_pk[par].n
                for q in range(4):
                    qpar = q % 2
                    if 4 * t + q >= 2:
                        pt, pq = (t, q - 2) if q >= 2 else (t - 1, q + 2)
                        e.wait_ge(s_dq.h, at(f"dqv:d{pt}:{pq}"))
                    else:
                        e.wait_ge(s_dq.h, at(f"dqv:{NGU - 2 + qpar}"))
                    dmark(e.dma_start(
                        out=dq_sc[qpar],
                        in_=dnsb[t, :, 512 * q:512 * q + 512]),
                        f"p0_scd:{t}:{q}")
                    e.wait_ge(s_dq.h, at(f"dqv:d{t}:{q}"))
                    dmark(e.dma_start(
                        out=s_wdn[t, :, 512 * q:512 * q + 512],
                        in_=dq_out[par][:, 512 * qpar:512 * qpar + 512]),
                        f"p0_dnst:{t}:{q}")
            M2["gust:f0"] = d_gust[0].n
            M2["gust:f1"] = d_gust[1].n
            M2["dnst:f0"] = d_dnst[0].n
            M2["dnst:f1"] = d_dnst[1].n

        @blk.tensor
        def _(e):
            dwait(e, "cst")
            dwait(e, "poolcst")
            for s in range(SPC):
                for tt in range(16):
                    n = s * 16 + tt
                    e.wait_ge(s_c1.h, at(f"p1a_cast:{n}"))
                    if n >= 1:
                        e.wait_ge(s_c1.h, at(f"p1a_evac:{n - 1}"))
                    if s == 1 and tt == 0:
                        e.wait_ge(s_c1.h, at("p1_evacdone:0:2:3:3"))
                    for dc in range(16):
                        ins = e.matmul(ps_big[:, 128 * dc:128 * dc + 128],
                                       p1_xb[n % 2][:, 128 * dc:128 * dc + 128],
                                       c_id16, start=True, stop=True,
                                       skip_group_check=True)
                    mark(s_t1, ins, f"p1a_mm:{n}", 1)
                for wi, w in enumerate("qkv"):
                    for og in range(4):
                        par = og % 2
                        dwait(e, f"p1_w:{s}:{wi}:{og}:15")
                        for tq in range(4):
                            xpar = tq % 2
                            dwait(e, f"p1_xq:{s}:{wi}:{og}:{tq}:15")
                            u = ((s * 3 + wi) * 4 + og) * 4 + tq
                            if u >= 2:
                                pu = u - 2
                                pl = pu % 4
                                pog = (pu // 4) % 4
                                pw = (pu // 16) % 3
                                ps_ = pu // 48
                                e.wait_ge(s_c1.h,
                                          at(f"p1_evacdone:{ps_}:{pw}:{pog}:{pl}"))
                                if pw == 2 and pl == 3:
                                    e.wait_ge(s_c1.h,
                                              at(f"p1_vevdone:{ps_}:{pog}:3"))
                            elif s == 0 and wi == 0:
                                e.wait_ge(s_c1.h, at(f"p1a_evac:{15}"))
                            if w == "v" and tq == 1 and og >= 1:
                                e.wait_ge(s_c1.h, at(f"p1_vevdone:{s}:{og - 1}:3"))
                            for dc in range(16):
                                for ol in range(4):
                                    dst = (QUAR(ol) if tq % 2 == 0
                                           else SETB[ol][:])
                                    ins = e.matmul(
                                        dst,
                                        p1_wog[par][:, 512 * dc + 128 * ol:
                                                    512 * dc + 128 * ol + 128],
                                        p1_xq[xpar][:, 512 * dc:512 * dc + 512],
                                        start=(dc == 0), stop=(dc == 15))
                            mark(s_t1, ins, f"p1_mm:{s}:{wi}:{og}:{tq}", 1)
                        if w == "v":
                            for ol in range(4):
                                e.wait_ge(s_c1.h, at(f"p1_evac:{s}:{wi}:{og}:3:{ol}"))
                                if ol == 0:
                                    e.wait_ge(s_c1.h,
                                              at(f"p1_evac:{s}:{wi}:{og}:3:1"))
                                for kc in range(16):
                                    grp = kc // 4
                                    dstp = (ps_c, ps_d)[grp % 2]
                                    vsl = 4 * ol + grp
                                    if kc % 4 == 0 and vsl >= 2:
                                        e.wait_ge(s_c1.h,
                                                  at(f"p1_vev:{s}:{og}:"
                                                     f"{(vsl - 2) // 4}:"
                                                     f"{(vsl - 2) % 4}"))
                                    ins = e.matmul(
                                        dstp[:, 128 * (kc % 4):128 * (kc % 4) + 128],
                                        p1_vfm[:, 2048 * ol + 128 * kc:
                                               2048 * ol + 128 * kc + 128],
                                        c_id16, start=True, stop=True,
                                        skip_group_check=True)
                                    if kc % 4 == 3:
                                        mark(s_t1, ins, f"p1_vtr:{s}:{og}:{ol}:{grp}", 1)
                M2[f"p1_vtrdone:{s}"] = s_t1.n
            if upto < 2:
                return

            sc_bank = [None] * 4
            for s in range(SPC):
                for g in range(4):
                    nblk, keys, qts = QG_PLANS[g]
                    zmap, mmap = CHUNK_MAPS[g]
                    qgi = s * 4 + g
                    def emit_bc(phi):
                        e.wait_ge(s_v2.h, at(f"p2_rc:{phi}"))
                        ins = e.matmul(ps_f[:, 0:512], c_ones1, p2_rdr,
                                       start=True, stop=True)
                        mark(s_t2, ins, f"p2_bc:{phi}", 1)

                    for h in range(NH):
                        par = h % 2
                        hi = qgi * NH + h
                        dwait(e, f"p2_q:{hi}")
                        dwait(e, f"p2_k:{hi}")
                        if h >= 1:
                            emit_bc(hi - 1)
                        for c in range(nblk):
                            b = c % 4
                            if hi == 0 and c < 4:
                                e.wait_ge(s_c1.h, at("p1_evacdone:1:2:3:3"))
                            elif c % 2 == 0 and c >= 4:
                                e.wait_ge(s_c2.h, at(f"p2_ex:{hi}:{c - 3}"))
                            if h == 0 and c == 0 and qgi >= 1:
                                e.wait_ge(s_v2.h, at(f"p2_r1:{qgi * 4 - 1}"))
                            ins = e.matmul(
                                ps_big[:, 512 * b:512 * b + 512],
                                p2_k[par][:, keys[c]:keys[c] + 128],
                                p2_q[par][:, 0:512],
                                start=True, stop=True, skip_group_check=True)
                            mark(s_t2, ins, f"p2_sc:{hi}:{c}", 1)
                        if any(mmap):
                            e.wait_ge(s_v2.h, at(f"p2_mk:{hi}"))
                        if hi >= 1:
                            e.wait_ge(s_c2.h, at(f"p2_bcev:{hi - 1}"))
                        e.wait_ge(s_c2.h, at(f"p2_ex:{hi}:{max(nblk - 5, 0)}"))
                        for c in range(nblk):
                            if c == nblk - 4:
                                e.wait_ge(s_c2.h,
                                          at(f"p2_ex:{hi}:{nblk - 1}"))
                            ins = e.matmul(ps_f[0:1, 0:512], c_onesd,
                                           p2_pt[:, 512 * c:512 * c + 512],
                                           start=(c == 0),
                                           stop=(c == nblk - 1),
                                           skip_group_check=True)
                        mark(s_t2, ins, f"p2_den:{hi}", 1)
                        dwait(e, f"p2_v:{hi}")
                        if hi >= 1:
                            e.wait_ge(s_v2.h, at(f"p2_pvev:{hi - 1}"))
                        for c in range(nblk):
                            vc = c if g <= 1 else (0 if c == 0 else
                                                   4 * (g - 1) + c - 1)
                            ins = e.matmul(ps_e[:],
                                           p2_v[par][:, 128 * vc:128 * vc + 128],
                                           p2_pt[:, 512 * c:512 * c + 512],
                                           start=(c == 0), stop=(c == nblk - 1))
                        mark(s_t2, ins, f"p2_pv:{hi}", 1)
                    emit_bc(qgi * NH + NH - 1)
                    e.wait_ge(s_c2.h,
                              at(f"p2_ex:{qgi * NH + NH - 1}:{nblk - 1}"))
                    e.wait_ge(s_v2.h, at(f"p2_pvev:{qgi * NH + NH - 1}"))
                    if qgi >= 1:
                        e.wait_ge(s_v2.h, at(f"p2_r1:{qgi * 4 - 1}"))
                    for og in range(4):
                        par = og % 2
                        dwait(e, f"p2_wo:{qgi}:{og}:15")
                        if og >= 1:
                            e.wait_ge(s_c2.h, at(f"p2_woev:{qgi}:{og - 1}"))
                        for dc in range(16):
                            for ol in range(4):
                                ins = e.matmul(
                                    QUAR(ol),
                                    p1_wog[par][:, 512 * dc + 128 * ol:
                                                512 * dc + 128 * ol + 128],
                                    p2_afm[:, 512 * dc:512 * dc + 512],
                                    start=(dc == 0), stop=(dc == 15))
                        mark(s_t2, ins, f"p2_womm:{qgi}:{og}", 1)
                    for tc in range(4):
                        n = qgi * 4 + tc
                        e.wait_ge(s_c2.h, at(f"p2_woev:{qgi}:3"))
                        if tc >= 1:
                            e.wait_ge(s_v2.h, at(f"p2_r1:{n - 1}"))
                        for oc in range(16):
                            ins = e.matmul(ps_big[:, 128 * oc:128 * oc + 128],
                                           p2_wof[:, 512 * oc + 128 * tc:
                                                  512 * oc + 128 * tc + 128],
                                           c_id16, start=True, stop=True,
                                           skip_group_check=True)
                        mark(s_t2, ins, f"p2_wot:{n}", 1)
                        e.wait_ge(s_c2.h, at(f"p2_x1b:{n}"))
                        if n == 0:
                            e.wait_ge(s_c1.h, at("p1_vevdone:1:3:3"))
                        for dc in range(16):
                            grp = dc // 4
                            dstp = (ps_c, ps_d)[grp % 2]
                            if dc % 4 == 0 and grp >= 2:
                                e.wait_ge(s_c2.h, at(f"p2_x1ev:{n}:{grp - 2}"))
                            ins = e.matmul(
                                dstp[:, 128 * (dc % 4):128 * (dc % 4) + 128],
                                p2_x1b[:, 128 * dc:128 * dc + 128],
                                c_id16, start=True, stop=True,
                                skip_group_check=True)
                            if dc % 4 == 3:
                                mark(s_t2, ins, f"p2_x1tr:{n}:{grp}", 1)
            if upto < 3:
                return

            for tt in range(NTT):
                dwait(e, f"p3_x1j:{tt}:15")
                for usb in range(22):
                    par = usb % 2
                    dwait(e, f"p3_wg:{tt}:{usb}:1")
                    dwait(e, f"p3_wu:{tt}:{usb}:1")
                    for ol in range(2):
                        isl = 2 * usb + ol
                        gpar = isl % 2
                        gdst = ps_c[:] if gpar == 0 else ps_big[:, 0:512]
                        udst = ps_d[:] if gpar == 0 else ps_big[:, 512:1024]
                        if isl >= 2:
                            e.wait_ge(s_c3.h, at(f"p3_silu:{tt}:{isl - 2}"))
                            e.wait_ge(s_v3.h, at(f"p3_h:{tt}:{isl - 2}"))
                        elif tt >= 1:
                            e.wait_ge(s_c3.h, at(f"p3_ytevdone:{tt - 1}"))
                            e.wait_ge(s_c3.h, at(f"p3_silu:{tt - 1}:{isl + 42}"))
                            e.wait_ge(s_v3.h, at(f"p3_h:{tt - 1}:{isl + 42}"))
                        else:
                            e.wait_ge(s_c2.h, at("p2_lastx1ev"))
                        for kk in range(16):
                            cc, j = divmod(kk, 8)
                            ins = e.matmul(
                                gdst,
                                p3_wg[par][:, 2048 * cc + 256 * j + 128 * ol:
                                           2048 * cc + 256 * j + 128 * ol + 128],
                                p3_x1j[:, 512 * kk:512 * kk + 512],
                                start=(kk == 0), stop=(kk == 15))
                        mark(s_t3, ins, f"p3_gmm:{tt}:{isl}", 1)
                        for kk in range(16):
                            cc, j = divmod(kk, 8)
                            ins = e.matmul(
                                udst,
                                p3_wu[par][:, 2048 * cc + 256 * j + 128 * ol:
                                           2048 * cc + 256 * j + 128 * ol + 128],
                                p3_x1j[:, 512 * kk:512 * kk + 512],
                                start=(kk == 0), stop=(kk == 15))
                        mark(s_t3, ins, f"p3_umm:{tt}:{isl}", 1)
                    M2[f"p3_gumm:{tt}:{usb}"] = s_t3.n
                M2[f"p3_gudone:{tt}"] = s_t3.n
                for ds in range(16):
                    ob, r = divmod(ds, 4)
                    ocp, half = divmod(r, 2)
                    par = ds % 2
                    dwait(e, f"p3_wd:{tt}:{ds}:1")
                    if half == 0:
                        e.wait_ge(s_v3.h, at(f"p3_h:{tt}:43"))
                        pb = 4 * ob + 2 * ocp
                        if pb >= 2:
                            e.wait_ge(s_c3.h, at(f"p3_yev:{tt}:{pb - 1}"))
                        elif tt >= 1:
                            e.wait_ge(s_c3.h, at(f"p3_yev:{tt - 1}:{pb + 15}"))
                    for oc in range(2):
                        ydst = (ps_e, ps_f)[oc]
                        for it in range(22):
                            ins = e.matmul(
                                ydst[:],
                                p3_wd[par][:, 256 * it + 128 * oc:
                                           256 * it + 128 * oc + 128],
                                p3_h[:, 512 * (22 * half + it):
                                     512 * (22 * half + it) + 512],
                                start=(half == 0 and it == 0),
                                stop=(half == 1 and it == 21))
                        if half == 1:
                            mark(s_t3, ins,
                                 f"p3_ymm:{tt}:{4 * ob + 2 * ocp + oc}", 1)
                        elif oc == 1:
                            mark(s_t3, ins, f"p3_dmm:{tt}:{ds}", 1)
                    if half == 1:
                        M2[f"p3_dmm:{tt}:{ds}"] = s_t3.n
                for tc in range(4):
                    n = tt * 4 + tc
                    e.wait_ge(s_c3.h, at(f"p3_yev:{tt}:15"))
                    if tc >= 1:
                        e.wait_ge(s_c3.h, at(f"p3_ytev:{n - 1}:3"))
                    for oc in range(16):
                        grp = oc // 4
                        dstp = (ps_c, ps_d)[grp % 2]
                        if oc % 4 == 0 and grp >= 2:
                            e.wait_ge(s_c3.h, at(f"p3_ytev:{n}:{grp - 2}"))
                        ins = e.matmul(
                            dstp[:, 128 * (oc % 4):128 * (oc % 4) + 128],
                            p3_yf[:, 512 * oc + 128 * tc:512 * oc + 128 * tc + 128],
                            c_id16, start=True, stop=True,
                            skip_group_check=True)
                        if oc % 4 == 3:
                            mark(s_t3, ins, f"p3_yt:{n}:{grp}", 1)

        @blk.scalar
        def _(e):


            dwait(e, "cst")
            e.mul(c_bqs, c_bias["q"], QSCALE)
            e.drain()

            def emit_xq(s, wi, og, tq):
                xpar = tq % 2
                if wi == 0 and og == 0:
                    dwait(e, f"p1a_st:{s}:{tq}:15")
                for dc in range(16):
                    dmark(e.dma_start(
                        out=p1_xq[xpar][:, 512 * dc:512 * dc + 512],
                        in_=s_xfm[s, 128 * dc:128 * dc + 128,
                                  512 * tq:512 * tq + 512]),
                        f"p1_xq:{s}:{wi}:{og}:{tq}:{dc}")

            for s in range(SPC):
                for tt in range(16):
                    n = s * 16 + tt
                    dwait(e, f"p1a_x:{n}")
                    if n >= 2:
                        e.wait_ge(s_t1.h, at(f"p1a_mm:{n - 2}"))
                    mark(s_c1, e.activation(p1_xb[n % 2], p1_x32[n % 2],
                                            AF.Copy), f"p1a_cast:{n}", 1)
                    e.wait_ge(s_t1.h, at(f"p1a_mm:{n}"))
                    gq = n // 4
                    if gq >= 1:
                        dwait(e, f"p1a_st:{(gq - 1) // 4}:{(gq - 1) % 4}:15")
                    mark(s_c1, e.activation(
                        p1_stx[:, 2048 * (tt % 4):2048 * (tt % 4) + 2048],
                        ps_big[:], AF.Copy), f"p1a_evac:{n}", 1)
                emit_xq(s, 0, 0, 0)
                emit_xq(s, 0, 0, 1)
                for wi, w in enumerate("qkv"):
                    for og in range(4):
                        for tq in range(4):
                            e.wait_ge(s_t1.h, at(f"p1_mm:{s}:{wi}:{og}:{tq}"))
                            for ol in range(4):
                                src = (QUAR(ol) if tq % 2 == 0 else SETB[ol][:])
                                oc = 4 * og + ol
                                if tq == 0:
                                    if w == "v":
                                        if og >= 1:
                                            e.wait_ge(s_t1.h,
                                                      at(f"p1_vtr:{s}:{og - 1}:{ol}:3"))
                                        elif s == 1:
                                            e.wait_ge(s_t1.h,
                                                      at(f"p1_vtr:0:3:{ol}:3"))
                                    else:
                                        po = og - 1
                                        pwi, psx = wi, s
                                        if po < 0:
                                            pwi -= 1
                                            po = 3
                                        if pwi < 0:
                                            psx -= 1
                                            pwi = 1
                                        if psx >= 0:
                                            dwait(e, f"p1_st:{psx}:{pwi}:{4 * po + 3}")
                                dst = (p1_vfm if w == "v" else p1_stg)
                                if (w == "v" and s == 0 and og == 0
                                        and tq == 0 and ol == 0):
                                    dwait(e, "p0b:done")
                                if w == "q":
                                    ins = e.activation(
                                        dst[:, 2048 * ol + 512 * tq:
                                            2048 * ol + 512 * tq + 512],
                                        src, AF.Identity,
                                        bias=c_bqs[:, oc:oc + 1], scale=QSCALE)
                                else:
                                    ins = e.activation(
                                        dst[:, 2048 * ol + 512 * tq:
                                            2048 * ol + 512 * tq + 512],
                                        src, AF.Identity,
                                        bias=c_bias[w][:, oc:oc + 1], scale=1.0)
                                mark(s_c1, ins, f"p1_evac:{s}:{wi}:{og}:{tq}:{ol}", 1)
                            M2[f"p1_evacdone:{s}:{wi}:{og}:{tq}"] = s_c1.n
                            u2 = (wi * 4 + og) * 4 + tq + 2
                            if u2 < 48:
                                emit_xq(s, u2 // 16, (u2 // 4) % 4, u2 % 4)
                        if w == "v":
                            for ol in range(4):
                                oc = 4 * og + ol
                                if oc >= 1:
                                    dwait(e, f"p1_stv:{s}:{oc - 1}")
                                elif s == 1:
                                    dwait(e, "p1_stv:0:15")
                                for grp in range(4):
                                    e.wait_ge(s_t1.h, at(f"p1_vtr:{s}:{og}:{ol}:{grp}"))
                                    dstp = (ps_c, ps_d)[grp % 2]
                                    mark(s_c1, e.activation(
                                        p1_stv[:, 512 * grp:512 * grp + 512],
                                        dstp[:], AF.Copy),
                                        f"p1_vev:{s}:{og}:{ol}:{grp}", 1)
                                M2[f"p1_vevdone:{s}:{og}:{ol}"] = s_c1.n
            if upto < 2:
                return

            for s in range(SPC):
                for g in range(4):
                    nblk, keys, qts = QG_PLANS[g]
                    qgi = s * 4 + g
                    for h in range(NH):
                        hi = qgi * NH + h
                        for c in range(nblk):
                            e.wait_ge(s_t2.h, at(f"p2_sc:{hi}:{c}"))
                            b = c % 4
                            le = 4
                            if g == 1 and c >= 1:
                                le = min(c + 1, 4)
                            elif g >= 2 and c >= 1:
                                le = min(c, 4)
                            ins = e.activation(
                                p2_pt[:, 512 * c:512 * c + 128 * le],
                                ps_big[:, 512 * b:512 * b + 128 * le],
                                AF.Exp, bias=c_neg8, scale=1.0)
                            if le < 4:
                                ins = e.activation(
                                    p2_pt[:, 512 * c + 128 * le:512 * c + 512],
                                    ps_big[:, 512 * b + 128 * le:512 * b + 512],
                                    AF.Copy, scale=0.0)
                            mark(s_c2, ins, f"p2_ex:{hi}:{c}", 1)
                        e.wait_ge(s_t2.h, at(f"p2_bc:{hi}"))
                        if hi >= 1:
                            e.wait_ge(s_v2.h, at(f"p2_pvev:{hi - 1}"))
                        mark(s_c2, e.activation(p2_bc, ps_f[:, 0:512], AF.Copy),
                             f"p2_bcev:{hi}", 1)
                    if qgi >= 1:
                        e.wait_ge(s_t2.h, at(f"p2_wot:{qgi * 4 - 1}"))
                    for og in range(4):
                        e.wait_ge(s_t2.h, at(f"p2_womm:{qgi}:{og}"))
                        for ol in range(4):
                            oc = 4 * og + ol
                            ins = e.activation(
                                p2_wof[:, 512 * oc:512 * oc + 512],
                                QUAR(ol), AF.Identity,
                                bias=c_bias["o"][:, oc:oc + 1], scale=1.0)
                        mark(s_c2, ins, f"p2_woev:{qgi}:{og}", 1)
                    for tc in range(4):
                        n = qgi * 4 + tc
                        e.wait_ge(s_v2.h, at(f"p2_r1:{n}"))
                        e.activation(p2_x1b, p2_r1, AF.Square,
                                     accum_out=st_ssq)
                        e.activation(p2_r1, p2_r1, AF.Identity,
                                     accum_out=st_sm)
                        e.drain()
                        e.activation(st_m2, st_sm, AF.Square, scale=1.0 / D)
                        e.drain()
                        e.activation(st_m2, st_m2, AF.Identity, scale=-1.0,
                                     bias=c_eps1[:, 0:1])
                        e.drain()
                        e.activation(st_rstd, st_ssq, AF.Ln, bias=st_m2,
                                     scale=1.0 / D)
                        e.drain()
                        e.activation(st_rstd, st_rstd, AF.Exp, scale=-0.5)
                        e.drain()
                        e.activation(st_rstdn, st_rstd, AF.Identity,
                                     scale=-1.0 / D)
                        e.drain()
                        e.activation(st_nmr, st_sm, AF.Identity,
                                     scale=st_rstdn)
                        e.drain()
                        mark(s_c2, e.activation(p2_x1b, p2_r1, AF.Identity,
                                                bias=st_nmr, scale=st_rstd),
                             f"p2_x1b:{n}", 1)
                        if n >= 1:
                            dwait(e, f"p2_stx1t:{n - 1}")
                        e.drain()
                        mark(s_c2, e.activation(p2_r1, p2_r1, AF.Identity,
                                                bias=st_nmr, scale=st_rstd),
                             f"p2_x1tt:{n}", 1)
                        if n >= 1:
                            dwait(e, f"p2_stx1f:{n - 1}")
                        for grp in range(4):
                            e.wait_ge(s_t2.h, at(f"p2_x1tr:{n}:{grp}"))
                            dstp = (ps_c, ps_d)[grp % 2]
                            mark(s_c2, e.activation(
                                p2_stx1[:, 512 * grp:512 * grp + 512],
                                dstp[:], AF.Copy), f"p2_x1ev:{n}:{grp}", 1)
                        M2[f"p2_x1f:{n}"] = s_c2.n
                    M2[f"p2_x1evdone:{qgi}"] = s_c2.n
            M2["p2_lastx1ev"] = s_c2.n
            if upto < 3:
                return

            for tt in range(NTT):
                for isl in range(NIT):
                    e.wait_ge(s_t3.h, at(f"p3_gmm:{tt}:{isl}"))
                    gpar = isl % 2
                    gsrc = ps_c[:] if gpar == 0 else ps_big[:, 0:512]
                    if isl >= 1:
                        e.wait_ge(s_v3.h, at(f"p3_h:{tt}:{isl - 1}"))
                    mark(s_c3, e.activation(p3_sg, gsrc, AF.Silu),
                         f"p3_silu:{tt}:{isl}", 1)
                for o128 in range(16):
                    e.wait_ge(s_t3.h, at(f"p3_ymm:{tt}:{o128}"))
                    if tt >= 1 and o128 == 0:
                        e.wait_ge(s_t3.h, at(f"p3_yt:{tt * 4 - 1}:3"))
                    ysrc = (ps_e, ps_f)[o128 % 2]
                    mark(s_c3, e.activation(
                        p3_yf[:, 512 * o128:512 * o128 + 512], ysrc[:], AF.Copy),
                        f"p3_yev:{tt}:{o128}", 1)
                for tc in range(4):
                    n = tt * 4 + tc
                    e.drain()
                    for grp in range(4):
                        e.wait_ge(s_t3.h, at(f"p3_yt:{n}:{grp}"))
                        dstp = (ps_c, ps_d)[grp % 2]
                        mark(s_c3, e.activation(
                            p3_yt[:, 512 * grp:512 * grp + 512], dstp[:],
                            AF.Copy), f"p3_ytev:{n}:{grp}", 1)
                    if n >= 1:
                        dwait(e, f"out:{n - 1}")
                    e.drain()
                    e.activation(p3_out, p3_yt, AF.Square, accum_out=st_ssq)
                    e.drain()
                    e.activation(st_rstd, st_ssq, AF.Ln, bias=c_eps2,
                                 scale=1.0 / D)
                    e.drain()
                    mark(s_c3, e.activation(st_rstd, st_rstd, AF.Exp,
                                            scale=-0.5), f"p3_rms:{n}", 1)
                    e.wait_ge(s_v3.h, at(f"p3_r2:{n}"))
                    e.activation(p3_out, p3_yt, AF.Square, accum_out=st_ssq)
                    e.activation(p3_yt, p3_yt, AF.Identity, accum_out=st_sm)
                    e.drain()
                    e.activation(st_m2, st_sm, AF.Square, scale=1.0 / D)
                    e.drain()
                    e.activation(st_m2, st_m2, AF.Identity, scale=-1.0,
                                 bias=c_eps1[:, 0:1])
                    e.drain()
                    e.activation(st_rstd, st_ssq, AF.Ln, bias=st_m2,
                                 scale=1.0 / D)
                    e.drain()
                    e.activation(st_rstd, st_rstd, AF.Exp, scale=-0.5)
                    e.drain()
                    e.activation(st_rstdn, st_rstd, AF.Identity,
                                 scale=-1.0 / D)
                    e.drain()
                    e.activation(st_nmr, st_sm, AF.Identity, scale=st_rstdn)
                    e.drain()
                    mark(s_c3, e.activation(p3_out, p3_yt, AF.Identity,
                                            bias=st_nmr, scale=st_rstd),
                         f"p3_out:{n}", 1)
                M2[f"p3_ytevdone:{tt}"] = s_c3.n

        @blk.vector
        def _(e):
            def lut_pass(src_words, shift, rows, cols, half=False):
                r = rows
                e.tensor_scalar(dq_mi[r, :], src_words[r, cols], shift, 7,
                                ALU.logical_shift_right, ALU.bitwise_and)
                e.tensor_scalar(dq_si[r, :], src_words[r, cols], shift + 3, 1,
                                ALU.logical_shift_right, ALU.bitwise_and)
                e.drain()
                if half:
                    e.tensor_scalar(dq_r4[r, :], dq_mi[r, :], 1.0, -2.0,
                                    ALU.mult, ALU.add)
                    e.tensor_scalar(dq_r6[r, :], dq_mi[r, :], 2.0, -8.0,
                                    ALU.mult, ALU.add)
                    e.tensor_scalar(dq_m[r, :], dq_mi[r, :], 0.5, 0.0,
                                    ALU.mult, ALU.add)
                else:
                    e.tensor_scalar(dq_r4[r, :], dq_mi[r, :], 2.0, -4.0,
                                    ALU.mult, ALU.add)
                    e.tensor_scalar(dq_r6[r, :], dq_mi[r, :], 4.0, -16.0,
                                    ALU.mult, ALU.add)
                    e.tensor_copy(dq_m[r, :], dq_mi[r, :])
                e.tensor_scalar(dq_s[r, :], dq_si[r, :], -2.0, 1.0,
                                ALU.mult, ALU.add)
                e.drain()
                e.tensor_tensor(dq_r4[r, :], dq_r4[r, :], dq_r6[r, :], ALU.max)
                e.drain()
                e.tensor_tensor(dq_r4[r, :], dq_m[r, :], dq_r4[r, :], ALU.max)
                e.drain()
                e.tensor_tensor(dq_r4[r, :], dq_r4[r, :], dq_s[r, :], ALU.mult)
                e.drain()

            dq_jobs = []

            def _gu_job(n, j):
                def run():
                    par = n % 2
                    if j == 0:
                        dwait(e, f"p0_pk:{n}")
                        dwait(e, f"p0_sc:{n}")
                        if n >= 2:
                            dwait(e, f"p0_gu:{n - 2}")
                    lut_pass(dq_pk[par], 4 * j, slice(0, 128), slice(0, 512),
                             True)
                    ins = e.tensor_tensor(dq_out[par][:, 512 * j:512 * j + 512],
                                          dq_r4[:, 0:512], dq_sc[par], ALU.mult)
                    if j == 7:
                        mark(s_dq, ins, f"dqv:{n}", 1)
                return run

            def _gu_dn_bar():
                def run():
                    dwait(e, "p0gu:done")
                return run

            def _dn_job(t, q):
                def run():
                    a, j, mixed = DN_TILES[t]
                    par = t % 2
                    qpar = q % 2
                    if q == 0:
                        e.wait_ge(d_pk[par].h, at(f"p0_pkdone:{t}"))
                    dwait(e, f"p0_scd:{t}:{q}")
                    if 4 * t + q >= 2:
                        pt, pq = (t, q - 2) if q >= 2 else (t - 1, q + 2)
                        dwait(e, f"p0_dnst:{pt}:{pq}")
                    cols = slice(512 * q, 512 * q + 512)
                    if mixed:
                        lut_pass(dq_pk[par], 4 * j, slice(0, 64), cols, True)
                        lut_pass(dq_pk[par], 4 * (j + 1), slice(64, 128), cols,
                                 True)
                    else:
                        lut_pass(dq_pk[par], 4 * j, slice(0, 128), cols, True)
                    ins = e.tensor_tensor(
                        dq_out[par][:, 512 * qpar:512 * qpar + 512],
                        dq_r4[:, 0:512], dq_sc[qpar], ALU.mult)
                    mark(s_dq, ins, f"dqv:d{t}:{q}", 1)
                return run

            for n in range(NGU):
                for j in range(8):
                    _gu_job(n, j)()
            _gu_dn_bar()()
            for t in range(NIT):
                for q in range(4):
                    dq_jobs.append(_dn_job(t, q))
            dq_jobs.reverse()

            def dq_step(k):
                for _ in range(k):
                    if dq_jobs:
                        dq_jobs.pop()()

            def dq_drain():
                while dq_jobs:
                    dq_jobs.pop()()
                M2["dqv:all"] = s_dq.n

            if upto < 2:
                dq_drain()

            if upto >= 2:
                for s in range(SPC):
                    for g in range(4):
                        nblk, keys, qts = QG_PLANS[g]
                        qgi = s * 4 + g
                        zmap, mmap = CHUNK_MAPS[g]
                        for h in range(NH):
                            hi = qgi * NH + h
                            last = None
                            for c in range(nblk):
                                if not mmap[c]:
                                    continue
                                e.wait_ge(s_c2.h, at(f"p2_ex:{hi}:{c}"))
                                for (l, w) in mmap[c]:
                                    mt = {"tri": c_tri, "trisink": c_trisink,
                                          "sink": c_sink}[w]
                                    sub = p2_pt[:, 512 * c + 128 * l:
                                                512 * c + 128 * l + 128]
                                    last = e.tensor_tensor(sub, sub, mt,
                                                           ALU.mult)
                            if last is not None:
                                mark(s_v2, last, f"p2_mk:{hi}", 1)
                            e.wait_ge(s_t2.h, at(f"p2_den:{hi}"))
                            mark(s_v2, e.reciprocal(p2_rdr, ps_f[0:1, 0:512]),
                                 f"p2_rc:{hi}", 1)
                            e.wait_ge(s_t2.h, at(f"p2_pv:{hi}"))
                            e.wait_ge(s_c2.h, at(f"p2_bcev:{hi}"))
                            if qgi >= 1 and h == 0:
                                e.wait_ge(s_t2.h, at(f"p2_womm:{qgi - 1}:3"))
                            mark(s_v2, e.tensor_tensor(
                                p2_afm[:, 512 * h:512 * h + 512],
                                ps_e[:], p2_bc, ALU.mult),
                                f"p2_pvev:{hi}", 1)
                            dq_step(1)
                        for tc in range(4):
                            n = qgi * 4 + tc
                            e.wait_ge(s_t2.h, at(f"p2_wot:{n}"))
                            dwait(e, f"p2_x:{qgi}:{tc}")
                            if n >= 1:
                                dwait(e, f"p2_stx1t:{n - 1}")
                            mark(s_v2, e.tensor_tensor(p2_r1, ps_big[:],
                                                       p2_x32[tc % 2],
                                                       ALU.add),
                                 f"p2_r1:{n}", 1)
                            dq_step(2)
                dq_drain()

            if upto >= 3:
                for tt in range(NTT):
                    for isl in range(NIT):
                        e.wait_ge(s_c3.h, at(f"p3_silu:{tt}:{isl}"))
                        e.wait_ge(s_t3.h, at(f"p3_umm:{tt}:{isl}"))
                        if isl == 0 and tt >= 1:
                            e.wait_ge(s_t3.h, at(f"p3_ymm:{tt - 1}:15"))
                        gpar = isl % 2
                        usrc = ps_d[:] if gpar == 0 else ps_big[:, 512:1024]
                        mark(s_v3, e.tensor_tensor(
                            p3_h[:, 512 * isl:512 * isl + 512],
                            p3_sg, usrc, ALU.mult), f"p3_h:{tt}:{isl}", 1)
                    for tc in range(4):
                        n = tt * 4 + tc
                        e.wait_ge(s_c3.h, at(f"p3_rms:{n}"))
                        dwait(e, f"p3_x1r:{n}")
                        mark(s_v3, e.scalar_tensor_tensor(
                            p3_yt, p3_yt, st_rstd, p3_x1r,
                            ALU.mult, ALU.add), f"p3_r2:{n}", 1)

    return nc, M2



def _perm():
    k = np.arange(FI)
    return 8 * (k % DWRD) + (k // DWRD)


def _host_prep(inputs):
    f32 = np.float32
    x = np.asarray(inputs["L0_attn_hidden_states"], dtype=f32).reshape(
        BATCH * SEQ, D)
    h = {}
    for w, key in (("q", "L0_attn_Wq"), ("k", "L0_attn_Wk"),
                   ("v", "L0_attn_Wv"), ("o", "L0_attn_Wo")):
        h[f"w{w}t"] = np.ascontiguousarray(np.asarray(inputs[key], dtype=f32).T)
    for w, key in (("q", "L0_attn_bq"), ("k", "L0_attn_bk"),
                   ("v", "L0_attn_bv"), ("o", "L0_attn_bo")):
        h[f"b{w}"] = np.ascontiguousarray(
            np.asarray(inputs[key], dtype=f32).reshape(16, 128).T)

    perm = _perm()
    gup = np.asarray(inputs["L0_ffn_gate_up_packed"])
    gp = np.empty_like(gup)
    gp[:FI] = gup[:FI][perm]
    gp[FI:] = gup[FI:][perm]
    h["gut"] = np.ascontiguousarray(gp.T)
    gs = np.asarray(inputs["L0_ffn_gate_up_scales"])
    gsp = np.empty_like(gs)
    gsp[:, :FI] = gs[:, :FI][:, perm]
    gsp[:, FI:] = gs[:, FI:][:, perm]
    g16 = gsp.astype(np.float16) if gsp.dtype != np.float16 else gsp
    sgu = np.empty((2, NOB, 128, 512), np.float16)
    rows = np.arange(128) // 16
    for cc in range(2):
        for ob in range(NOB):
            sgu[cc, ob] = g16[8 * cc + rows, 512 * ob:512 * ob + 512]
    h["sgu"] = sgu
    h["dnt"] = np.ascontiguousarray(np.asarray(inputs["L0_ffn_down_packed"]).T)
    dsc16 = np.asarray(inputs["L0_ffn_down_scales"]).astype(np.float16)
    pos = np.arange(NIT * 128)
    grp = (8 * (pos % DWRD) + pos // DWRD) // 128
    h["dnsb"] = np.ascontiguousarray(dsc16[grp].reshape(NIT, 128, D))

    h["idm"] = np.eye(128, dtype=f32)
    r = np.arange(128)
    c = np.arange(128)
    h["mT_tri"] = np.ascontiguousarray(
        (c[None, :] >= r[:, None]).astype(f32).T)
    h["mT_trisink"] = np.ascontiguousarray(
        ((c[None, :] < 4) | (c[None, :] >= r[:, None])).astype(f32).T)
    h["mT_sink"] = np.ascontiguousarray(
        np.broadcast_to((c < 4).astype(f32), (128, 128)).T.copy())
    h["ones1"] = np.ones((1, 128), dtype=f32)
    h["onesd"] = np.ones((128, 1), dtype=f32)
    h["negsh"] = np.full((128, 1), -SHIFT, dtype=f32)
    h["eps1"] = np.full((128, 1), LN_EPS, dtype=f32)
    h["eps2"] = np.full((128, 1), RMS_EPS, dtype=f32)

    return x, h


_CACHED = {}


def _get_nc(dbg=(), upto=3):
    key = (tuple(sorted(dbg)), upto)
    if key not in _CACHED:
        _, M = build_program({}, dbg=dbg, upto=upto)
        nc, _ = build_program(M, dbg=dbg, upto=upto)
        _CACHED[key] = nc
    return _CACHED[key]


def _make_in_maps(inputs):
    x, shared = _host_prep(inputs)
    in_maps = []
    for c in range(NCORES):
        m = dict(shared)
        m["x"] = np.ascontiguousarray(x[c * TOK:(c + 1) * TOK])
        in_maps.append(m)
    return in_maps


def kernel(**inputs):
    nc = _get_nc()
    in_maps = _make_in_maps(inputs)
    res = run_bass_kernel_spmd(nc, in_maps, core_ids=list(range(NCORES)))
    out = np.concatenate([r["y"] for r in res.results], axis=0)
    return out.reshape(BATCH, SEQ, D)
